# version 10
# baseline (speedup 1.0000x reference)
import math
from contextlib import ExitStack

import numpy as np
import ml_dtypes

import concourse.bass as bass
import concourse.mybir as mybir
from concourse.bass_utils import run_bass_kernel_spmd

F32 = mybir.dt.float32
BF16 = mybir.dt.bfloat16
I32 = mybir.dt.int32
AF = mybir.ActivationFunctionType
ALU = mybir.AluOpType
AX = mybir.AxisListType

NCORES = 8
P = 128
T = 4096
D = 1024
NT = T // P

bf16 = ml_dtypes.bfloat16



PE_SELF_SKIP = True
ENGINES = ("pe", "act", "dve", "pool", "sp")
_ENG_ATTR = {"pe": "tensor", "act": "scalar", "dve": "vector",
             "pool": "gpsimd", "sp": "sync"}


class Prog:
    def __init__(self, nc):
        self.nc = nc
        self.es = ExitStack()
        self.sem = {}
        for e in ENGINES:
            self.sem[e] = self.es.enter_context(nc.semaphore(f"sem_{e}"))
        self.dsem = {}
        for q in ("sp", "pool", "act", "dve"):
            self.dsem[q + ".ld"] = self.es.enter_context(
                nc.semaphore(f"dsem_{q}_ld"))
            self.dsem[q + ".st"] = self.es.enter_context(
                nc.semaphore(f"dsem_{q}_st"))
        self.cnt = {e: 0 for e in ENGINES}
        self.dcnt = {q: 0 for q in self.dsem}
        self.last_w = {}
        self.readers = {}
        self.phase = 0
        self.ops = []
        self.phase_dma = {q: 0 for q in self.dsem}
        self._sbufs = {}
        self._psums = {}
        self._frees = []

    def sbuf(self, name, shape, dt=F32):
        t = self.es.enter_context(self.nc.sbuf_tensor(name, shape, dt))
        self._sbufs[name] = t
        return t

    def psum(self, name, shape, dt=F32):
        bank_elems = 512 if dt == F32 else 1024
        assert len(shape) == 2 and shape[0] == 128 and shape[1] <= bank_elems, shape
        t = self.es.enter_context(self.nc.psum_tensor(name, [128, bank_elems],
                                                      dt))
        self._psums[name] = t
        return t

    def dram(self, name, shape, dt=F32, kind=None):
        if kind is None:
            h = self.nc.dram_tensor(name, shape, dt)
        else:
            h = self.nc.dram_tensor(name, shape, dt, kind=kind)
        return h.ap()

    @staticmethod
    def _key(ap):
        t = ap.tensor if hasattr(ap, "tensor") else ap
        return t.name

    def _waits_for(self, eng, reads, writes, is_dma=False):
        waits = {}

        def add_wait(tag):
            kind, name, seq, ph = tag
            if ph < self.phase:
                return
            if PE_SELF_SKIP and kind == "eng" and name == "pe" \
                    and eng == "pe":
                return
            if kind == "eng":
                waits[("e", name)] = max(waits.get(("e", name), 0), seq)
            else:
                waits[("d", name)] = max(waits.get(("d", name), 0),
                                         self.dcnt[name])

        for ap in reads:
            k = self._key(ap)
            w = self.last_w.get(k)
            if w is not None:
                add_wait(w)
        for ap in writes:
            k = self._key(ap)
            w = self.last_w.get(k)
            if w is not None:
                add_wait(w)
            for (kind, name), (seq, ph) in self.readers.get(k, {}).items():
                add_wait((kind, name, seq, ph))
        return waits

    def _record(self, eng_tag, reads, writes):
        for ap in reads:
            k = self._key(ap)
            d = self.readers.setdefault(k, {})
            kk = (eng_tag[0], eng_tag[1])
            old = d.get(kk)
            if old is None or old[0] < eng_tag[2]:
                d[kk] = (eng_tag[2], self.phase)
        for ap in writes:
            k = self._key(ap)
            self.last_w[k] = eng_tag + (self.phase,)
            self.readers[k] = {}

    def op(self, eng, fn, reads=(), writes=(), inc_extra=0):
        waits = self._waits_for(eng, reads, writes)
        self.cnt[eng] += 1
        seq = self.cnt[eng]
        self.ops.append((eng, waits, fn, seq))
        self._record(("eng", eng, seq), reads, writes)
        return seq

    def mm_group(self, mms, reads=(), writes=()):
        waits = self._waits_for("pe", reads, writes)
        self.cnt["pe"] += 1
        seq = self.cnt["pe"]
        def fn(e):
            last = None
            for (ps, lhsT, rhs, st, sp, tp) in mms:
                last = e.matmul(ps, lhsT, rhs, start=st, stop=sp,
                                tile_position=tp, skip_group_check=True)
            return last
        self.ops.append(("pe", waits, fn, seq))
        self._record(("eng", "pe", seq), reads, writes)
        return seq

    def dma(self, dst, src, q="sp", no_track=False):
        eng = q
        is_store = "Dram" in type(dst.tensor).__name__ or \
                   getattr(dst.tensor, "base_partition", None) is None
        qs = q + (".st" if is_store else ".ld")
        waits = self._waits_for(eng, [src], [] if no_track else [dst])
        self.dcnt[qs] += 1
        self.phase_dma[qs] += 1
        seq = self.dcnt[qs]
        def fn(e):
            return e.dma_start(dst, src)
        self.ops.append((eng, waits, fn, ("dma", qs, seq)))
        self._record(("dma", qs, seq), [src], [] if no_track else [dst])
        return seq

    def mark_load(self, dst, q):
        qs = q + ".ld"
        self.last_w[self._key(dst)] = ("dma", qs, self.dcnt[qs], self.phase)
        self.readers[self._key(dst)] = {}

    def barrier(self):
        per_eng = {e: [] for e in ENGINES}
        for (eng, waits, fn, seq) in self.ops:
            per_eng[eng].append((waits, fn, seq))
        for qs, n in self.phase_dma.items():
            if n:
                per_eng[qs.split(".")[0]].append(
                    ({("d", qs): self.dcnt[qs]}, None, None))
        with self.nc.Block() as blk:
            for eng in ENGINES:
                lst = per_eng[eng]
                if not lst:
                    continue
                dec = getattr(blk, _ENG_ATTR[eng])
                def make(lst=lst, eng=eng):
                    def body(e):
                        for (waits, fn, seq) in lst:
                            for (kind, name), v in waits.items():
                                if kind == "e":
                                    e.wait_ge(self.sem[name], v)
                                else:
                                    e.wait_ge(self.dsem[name], 16 * v)
                            if fn is None:
                                continue
                            inst = fn(e)
                            if isinstance(seq, tuple):
                                inst.then_inc(self.dsem[seq[1]], 16)
                            else:
                                inst.then_inc(self.sem[eng], 1)
                    return body
                dec(make())
        self.ops = []
        self.phase_dma = {q: 0 for q in self.dsem}
        self.phase += 1

    def close(self):
        assert not self.ops, "un-flushed ops; call barrier()"
        self.es.close()



def op_tt(p, out, a, b, alu, eng="dve"):
    p.op(eng, lambda e: e.tensor_tensor(out, a, b, alu),
         reads=[a, b], writes=[out])


def op_ts(p, out, a, s1, alu, s2=None, alu2=None, eng="dve"):
    rd = [a] + ([s1] if hasattr(s1, "tensor") else []) \
             + ([s2] if hasattr(s2, "tensor") else [])
    if alu2 is None:
        p.op(eng, lambda e: e.tensor_scalar(out, a, s1, None, alu),
             reads=rd, writes=[out])
    else:
        p.op(eng, lambda e: e.tensor_scalar(out, a, s1, s2, alu, alu2),
             reads=rd, writes=[out])


def op_stt(p, out, in0, scal, in1, alu0, alu1, eng="dve"):
    rd = [in0, in1] + ([scal] if hasattr(scal, "tensor") else [])
    p.op(eng, lambda e: e.scalar_tensor_tensor(out, in0, scal, in1, alu0, alu1),
         reads=rd, writes=[out])


def op_copy(p, out, a, eng="dve"):
    if eng == "act":
        p.op(eng, lambda e: e.activation(out, a, AF.Copy),
             reads=[a], writes=[out])
    else:
        p.op(eng, lambda e: e.tensor_copy(out, a), reads=[a], writes=[out])


def op_act(p, out, a, func, bias=None, scale=None, eng="act", accum_out=None):
    kw = {}
    rd = [a]
    if bias is not None:
        kw["bias"] = bias
        if hasattr(bias, "tensor"):
            rd.append(bias)
    if scale is not None:
        kw["scale"] = scale
        if hasattr(scale, "tensor"):
            rd.append(scale)
    wr = [out]
    if accum_out is not None:
        kw["accum_out"] = accum_out
        wr = [out, accum_out]
    p.op(eng, lambda e: e.activation(out, a, func, **kw), reads=rd, writes=wr)


def op_red(p, out, a, alu, absval=False, negate=None, eng="dve"):
    kw = {}
    if absval:
        kw["apply_absolute_value"] = True
    if negate is not None:
        kw["negate"] = negate
    p.op(eng, lambda e: e.tensor_reduce(out, a, AX.X, alu, **kw),
         reads=[a], writes=[out])


def op_scan(p, out, a_coef, b, init, eng="dve"):
    rd = [a_coef, b] + ([init] if hasattr(init, "tensor") else [])
    p.op(eng, lambda e: e.tensor_tensor_scan(out, a_coef, b, init,
                                             ALU.mult, ALU.add),
         reads=rd, writes=[out])


def op_memset(p, ap, v, eng="pool"):
    p.op(eng, lambda e: e.memset(ap, v), reads=[], writes=[ap])


def op_transpose(p, out_psum, in_sbuf, ident, eng="pe"):
    p.op(eng, lambda e: e.transpose(out_psum, in_sbuf, ident),
         reads=[in_sbuf, ident], writes=[out_psum])


def ceil_div(a, b):
    return (a + b - 1) // b



def stage_inputs(inputs):
    f32 = np.float32
    g = inputs
    common = {}

    def _tr(a):
        return np.ascontiguousarray(np.asarray(a, f32).T)
    T_ = _tr

    def asf(a):
        return np.ascontiguousarray(np.asarray(a, f32))

    common["c_ident"] = np.eye(128, dtype=bf16)
    common["c_pihalf"] = np.full((128, 1), np.pi / 2, f32)
    cm = np.zeros((128, 896), bf16)
    for s in range(128):
        cm[s, s + 384:] = 1.0
    common["c_cmask"] = cm
    o2 = np.zeros((128, 65), bf16)
    o2[0:64, 0] = 1.0
    o2[64:128, 64] = 1.0
    common["c_ones2"] = o2
    common["c_onesr"] = np.ones((1, 128), bf16)
    e8 = np.zeros((8, 512), bf16)
    for e in range(8):
        e8[e, e * 64:(e + 1) * 64] = 1.0
    common["c_e8"] = e8
    common["c_identf"] = np.eye(128, dtype=f32)
    common["c_onesrf"] = np.ones((1, 128), f32)
    tri = np.fromfunction(lambda s, t: (s <= t), (128, 128))
    common["c_trilneg"] = np.where(tri, 0.0, -1e30).astype(bf16)
    common["c_triu1f"] = np.fromfunction(
        lambda ta, t: (ta <= t), (128, 128)).astype(f32)
    Hm = np.array([[1.0]])
    while Hm.shape[0] < 128:
        Hm = np.block([[Hm, Hm], [Hm, -Hm]])
    levs = np.zeros(128, np.int64)
    for u in range(1, 128):
        levs[u] = int(np.floor(np.log2(u))) + 1
    E = np.zeros((8, 128))
    for u in range(128):
        E[levs[u], u] = 1.0
    common["c_ehs"] = ((E @ Hm) / 128.0).astype(bf16)
    common["c_had"] = Hm.astype(bf16)
    e12 = np.zeros((12, 4 * 64), bf16)
    for k in range(4):
        e12[8 + k, k * 64:(k + 1) * 64] = 1.0
    common["c_e12"] = e12
    common["w_l3_lpT"] = np.ascontiguousarray(
        np.asarray(g["L3_attn_Lp"], f32).T)
    common["w_l3_alogR"] = asf(g["L3_attn_A_log"]).reshape(1, 16)
    common["w_l3_dtbR"] = asf(g["L3_attn_dt_bias"]).reshape(1, 16)

    common["w_l0_qT"] = _tr(g["L0_attn_Wq"])
    common["w_l0_kT"] = _tr(g["L0_attn_Wk"])
    common["w_l0_vT"] = _tr(g["L0_attn_Wv"])
    common["w_l0_oT"] = _tr(g["L0_attn_Wo"])
    common["w_l0_w1aT"] = _tr(g["L0_ffn_w1_A"])
    common["w_l0_w1b"] = asf(g["L0_ffn_w1_B"])
    common["w_l0_w2aT"] = _tr(g["L0_ffn_w2_A"])
    common["w_l0_w2b"] = asf(g["L0_ffn_w2_B"])
    common["w_l1_inw"] = asf(g["L1_attn_in_w"])
    common["w_l1_outw"] = asf(g["L1_attn_out_w"])
    common["w_l1_inb"] = np.ascontiguousarray(
        np.asarray(g["L1_attn_in_b"], f32).reshape(48, 128).T)
    common["w_l1_outb"] = asf(g["L1_attn_out_b"]).reshape(1, D)
    common["w_l1_gamma"] = asf(g["L1_attn_norm_gamma"]).reshape(1, D)
    common["w_l1_beta"] = asf(g["L1_attn_norm_beta"]).reshape(1, D)
    common["w_l1_convw"] = np.ascontiguousarray(
        np.asarray(g["L1_attn_conv_w"], f32).reshape(2048, 3)
        .reshape(16, 128, 3).transpose(1, 0, 2).reshape(128, 48))
    common["w_l1_convb"] = np.ascontiguousarray(
        np.asarray(g["L1_attn_conv_b"], f32).reshape(16, 128).T)
    def chan_major(a):
        return np.ascontiguousarray(
            np.asarray(a, f32).reshape(16, 128, 64).transpose(1, 0, 2)
            .reshape(128, 16 * 64))
    common["w_l1_alog"] = chan_major(g["L1_attn_A_log"])
    common["w_l1_bp"] = chan_major(g["L1_attn_B_p"])
    common["w_l1_cp"] = chan_major(g["L1_attn_C_p"])
    common["w_l1_dp"] = np.ascontiguousarray(
        np.asarray(g["L1_attn_D_p"], f32).reshape(16, 128).T)
    common["w_l1_theta"] = asf(g["L1_ffn_theta"])
    common["w_l1_phi"] = asf(g["L1_ffn_phi"])
    common["w_l1_thn"] = asf(g["L1_ffn_theta_noise"])
    common["w_l1_phn"] = asf(g["L1_ffn_phi_noise"])
    common["w_l2_qT"] = _tr(g["L2_attn_Wq"])
    common["w_l2_kT"] = _tr(g["L2_attn_Wk"])
    common["w_l2_vT"] = _tr(g["L2_attn_Wv"])
    common["w_l2_oT"] = _tr(g["L2_attn_Wo"])
    common["w_l2_bo"] = asf(g["L2_attn_bo"]).reshape(1, D)
    qsr = np.asarray(g["L2_attn_q_scale"], f32).reshape(16, 64)
    ksr = np.asarray(g["L2_attn_k_scale"], f32).reshape(16, 64)
    qs2 = np.zeros((128, 8), f32)
    ks2 = np.zeros((128, 8), f32)
    for i in range(8):
        qs2[0:64, i] = qsr[2 * i]
        qs2[64:128, i] = qsr[2 * i + 1]
        ks2[0:64, i] = ksr[2 * i]
        ks2[64:128, i] = ksr[2 * i + 1]
    common["w_l2_qs2"] = qs2
    common["w_l2_ks2"] = ks2
    common["w_l2_wrT"] = _tr(g["L2_ffn_W_r"])
    common["w_l2_wkT"] = _tr(g["L2_ffn_W_k"])
    common["w_l2_wvT"] = _tr(g["L2_ffn_W_v"])
    common["w_l2_mur"] = asf(g["L2_ffn_mu_r"]).reshape(8, 128).T
    common["w_l2_muk"] = asf(g["L2_ffn_mu_k"]).reshape(8, 128).T
    common["w_l3_qT"] = _tr(g["L3_attn_Wq"])
    common["w_l3_kT"] = _tr(g["L3_attn_Wk"])
    common["w_l3_vT"] = _tr(g["L3_attn_Wv"])
    common["w_l3_dlT"] = _tr(g["L3_attn_Wdl"])
    common["w_l3_dtT"] = _tr(g["L3_attn_Wdt"])
    common["w_l3_outT"] = _tr(g["L3_attn_Wout"])
    common["w_l3_alog"] = asf(g["L3_attn_A_log"]).reshape(16, 1)
    common["w_l3_dtb"] = asf(g["L3_attn_dt_bias"]).reshape(16, 1)
    common["w_l3_lp"] = asf(g["L3_attn_Lp"]).reshape(192, 1)
    common["w_l3_lng"] = asf(g["L3_attn_ln_g"]).reshape(1, D)
    common["w_l3_lnb"] = asf(g["L3_attn_ln_b"]).reshape(1, D)
    common["w_l3_gate"] = asf(g["L3_ffn_w_gate"])
    common["w_l3_up"] = asf(g["L3_ffn_w_up"])
    common["w_l3_down"] = asf(g["L3_ffn_w_down"])

    x_full = np.asarray(g["L0_attn_x"], f32)
    ew = np.asarray(g["L0_ffn_expert_weights"], f32)
    ei = np.asarray(g["L0_ffn_expert_indices"])

    in_maps = []
    for c in range(NCORES):
        m = dict(common)
        m["x"] = np.ascontiguousarray(x_full[c])
        coeff = np.zeros((T, 8), f32)
        t0 = c * T
        if t0 < ew.shape[0]:
            sl = slice(t0, min(t0 + T, ew.shape[0]))
            eic = ei[sl]
            ewc = ew[sl]
            for k in range(eic.shape[1]):
                np.add.at(coeff, (np.arange(eic.shape[0]), eic[:, k]),
                          ewc[:, k])
        m["l0_coeffT"] = np.ascontiguousarray(coeff.T)
        in_maps.append(m)
    return in_maps



class Rot:
    def __init__(self, items):
        self.items = items
        self.i = 0

    def nxt(self):
        r = self.items[self.i % len(self.items)]
        self.i += 1
        return r

    def nxt_not(self, excl):
        for _ in range(len(self.items) + 1):
            r = self.nxt()
            if all(r is not e for e in excl):
                return r
        raise RuntimeError("pool exhausted")


class Bufs:
    def __init__(self, p, np_inputs_shapes):
        self.p = p
        self.din_shapes = np_inputs_shapes
        self.dins = {}
        self.used_inputs = set()
        self.XT = p.sbuf("XT", [P, 8 * T], BF16)
        self.A2 = p.sbuf("A2", [P, 8 * T], BF16)
        self.A3 = p.sbuf("A3", [P, 2 * T], BF16)
        self.ft = Rot([p.sbuf(f"ft{i}", [P, 1024]) for i in range(4)])
        self.fs = Rot([p.sbuf(f"fs{i}", [P, 512]) for i in range(4)])
        self.xb = Rot([p.sbuf(f"xb{i}", [P, 1024], BF16) for i in range(3)])
        self.col = Rot([p.sbuf(f"col{i}", [P, 1]) for i in range(10)])
        self.gbc = p.sbuf("gbc", [P, 1024], BF16)
        self.bbc = p.sbuf("bbc", [P, 1024], BF16)
        self.g1 = p.sbuf("g1", [P, 512])
        self.sh1 = p.sbuf("sh1", [12, 512], BF16)
        self.pb = Rot([p.psum(f"pb{i}", [P, 512]) for i in range(6)])
        self.pt = Rot([p.psum(f"pt{i}", [P, 1024], BF16) for i in range(2)])
        self.ident = p.sbuf("ident", [P, P], BF16)
        self.pihalf = p.sbuf("pihalf", [P, 1])
        self.epscol = p.sbuf("epscol", [P, 1])

    def din(self, name):
        if name not in self.dins:
            shape, dt = self.din_shapes[name]
            self.dins[name] = self.p.dram(name, list(shape), dt,
                                          kind="ExternalInput")
            self.used_inputs.add(name)
        return self.dins[name]

    def load_consts(self):
        p = self.p
        p.dma(self.ident[:], self.din("c_ident"))
        p.dma(self.pihalf[:], self.din("c_pihalf"))
        op_memset(p, self.epscol[:], 1e-5)


def op_multi(p, eng, fns, reads=(), writes=()):
    waits = p._waits_for(eng, reads, writes)
    p.cnt[eng] += 1
    seq = p.cnt[eng]

    def fn(e):
        last = None
        for f in fns:
            last = f(e)
        return last
    p.ops.append((eng, waits, fn, seq))
    p._record(("eng", eng, seq), reads, writes)
    return seq


def xt_chunk(XT, c, t0, tn, T_=T):
    return XT[:, c * T_ + t0: c * T_ + t0 + tn]


def emit_ln_xt(p, B, x_dram, XT=None, group=2, extra_tile_fn=None):
    XT = XT if XT is not None else B.XT
    XTv = XT[:].rearrange("p (c t) -> p c t", t=T)
    for i0 in range(0, NT, group):
        tiles = range(i0, min(i0 + group, NT))
        fts = {}
        for i in tiles:
            fts[i] = B.ft.nxt()
            p.dma(fts[i][:], x_dram[i * P:(i + 1) * P, :])
        for i in tiles:
            xt = fts[i]
            xc = B.ft.nxt()
            m = B.col.nxt(); ss = B.col.nxt(); rs = B.col.nxt()
            op_red(p, m[:], xt[:], ALU.add)
            op_ts(p, m[:], m[:], 1.0 / D, ALU.mult)
            op_ts(p, xc[:], xt[:], m[:], ALU.subtract)
            op_act(p, xt[:], xc[:], AF.Square, accum_out=ss[:])
            op_ts(p, ss[:], ss[:], 1.0 / D, ALU.mult)
            op_act(p, ss[:], ss[:], AF.Ln, bias=B.epscol[:])
            op_act(p, rs[:], ss[:], AF.Exp, scale=-0.5)
            xnb = B.xb.nxt()
            op_ts(p, xnb[:], xc[:], rs[:], ALU.mult)
            if extra_tile_fn is not None:
                extra_tile_fn(i, xc, rs, m)
            ptb = B.pt.nxt()
            op_multi(p, "pe",
                     [(lambda e, c=c, xnb=xnb, ptb=ptb:
                       e.transpose(ptb[:, c * P:(c + 1) * P],
                                   xnb[:, c * P:(c + 1) * P], B.ident[:]))
                      for c in range(8)],
                     reads=[xnb[:], B.ident[:]], writes=[ptb[:]])
            op_copy(p, XTv[:, :, i * P:(i + 1) * P],
                    ptb[:].rearrange("p (c t) -> p c t", t=P))
        p.barrier()


def emit_load_wT(p, B, w_name, dst, dst_ofs, K, N, nstride=None, q="sp",
                 col0=0):
    nstride = nstride or N
    wd = B.din(w_name)
    kc = K // P
    jobs = [(c, n0, min(1024, N - n0))
            for c in range(kc) for n0 in range(0, N, 1024)]
    for g0 in range(0, len(jobs), 4):
        grp = jobs[g0:g0 + 4]
        stgs = []
        for (c, n0, nw) in grp:
            stg = B.ft.nxt()
            stgs.append(stg)
            p.dma(stg[:, :nw], wd[c * P:(c + 1) * P, col0 + n0:col0 + n0 + nw],
                  q=q)
        for (c, n0, nw), stg in zip(grp, stgs):
            op_copy(p, dst[:, dst_ofs + c * nstride + n0:
                           dst_ofs + c * nstride + n0 + nw],
                    stg[:, :nw], eng="dve" if (c + n0 // 1024) % 2 == 0 else "act")
        p.barrier()



def emit_l1ffn(p, B, xin, xout):
    for c in range(8):
        th = B.ft.nxt(); tn = B.ft.nxt()
        p.dma(th[:], B.din("w_l1_theta")[c * P:(c + 1) * P, :])
        p.dma(tn[:], B.din("w_l1_thn")[c * P:(c + 1) * P, :], q="pool")
        ph = B.ft.nxt(); pn = B.ft.nxt()
        p.dma(ph[:], B.din("w_l1_phi")[c * P:(c + 1) * P, :])
        p.dma(pn[:], B.din("w_l1_phn")[c * P:(c + 1) * P, :], q="pool")
        op_tt(p, th[:], th[:], tn[:], ALU.add)
        op_act(p, B.A2[:, c * 1024:(c + 1) * 1024], th[:], AF.Sin,
               bias=B.pihalf[:])
        op_tt(p, ph[:], ph[:], pn[:], ALU.add)
        op_act(p, B.A2[:, 8192 + c * 1024: 8192 + (c + 1) * 1024], ph[:],
               AF.Sin)
        p.barrier()
    for i0 in range(0, NT, 4):
        tiles = range(i0, min(i0 + 4, NT))
        xres = {}
        for i in tiles:
            xres[i] = B.ft.nxt()
            p.dma(xres[i][:], xin[i * P:(i + 1) * P, :])
        for i in tiles:
            for j in (0, 512):
                psA = B.pb.nxt(); psB = B.pb.nxt()
                p.mm_group([(psA[:, :512], xt_chunk(B.XT, c, i * P, P),
                             B.A2[:, c * 1024 + j: c * 1024 + j + 512],
                             c == 0, c == 7, None) for c in range(8)],
                           reads=[B.XT[:], B.A2[:]], writes=[psA[:]])
                p.mm_group([(psB[:, :512], xt_chunk(B.XT, c, i * P, P),
                             B.A2[:, 8192 + c * 1024 + j: 8192 + c * 1024 + j + 512],
                             c == 0, c == 7, None) for c in range(8)],
                           reads=[B.XT[:], B.A2[:]], writes=[psB[:]])
                fA = B.fs.nxt(); fB = B.fs.nxt()
                op_act(p, fA[:], psA[:, :512], AF.Square)
                op_act(p, fB[:], psB[:, :512], AF.Square)
                op_tt(p, fA[:], fA[:], fB[:], ALU.add)
                op_act(p, fA[:], fA[:], AF.Sqrt)
                op_tt(p, xres[i][:, j:j + 512], xres[i][:, j:j + 512],
                      fA[:], ALU.add)
            p.dma(xout[i * P:(i + 1) * P, :], xres[i][:], q="pool")
        p.barrier()



BLOCK_SPEC_DEFAULT = [(0, "fa"), (1, "fa"), (2, "fa"), (3, "fa")]


def emit_block(p, B, k, branches, xin, xout):
    emit_ln_xt(p, B, xin)
    first = True
    if k == 1:
        if "f" in branches:
            emit_l1ffn(p, B, xin, xout)
            first = False
        if "a" in branches:
            emit_l1attn(p, B, xin, xout, first)
            first = False
    elif k == 0:
        if "a" in branches:
            emit_l0attn(p, B, xin, xout, first)
            first = False
        if "f" in branches:
            emit_l0ffn(p, B, xin, xout, first)
            first = False
    elif k == 2:
        if "a" in branches:
            emit_l2attn(p, B, xin, xout, first)
            first = False
        if "f" in branches:
            emit_l2ffn(p, B, xin, xout, first)
            first = False
    elif k == 3:
        if "a" in branches:
            emit_l3attn(p, B, xin, xout, first)
            first = False
        if "f" in branches:
            emit_l3ffn(p, B, xin, xout, first)
            first = False
    assert not first, f"no branches for block {k}"


def build_program(spec, shapes):
    nc = bass.Bass("TRN2", target_bir_lowering=False, debug=False)
    p = Prog(nc)
    B = Bufs(p, shapes)
    x_in = p.dram("x", [T, D], F32, kind="ExternalInput")
    out = p.dram("out", [T, D], F32, kind="ExternalOutput")
    scratch = [p.dram(f"xs{i}", [T, D], F32) for i in range(2)]
    B.load_consts()
    p.barrier()
    cur = x_in
    for idx, (bk, branches) in enumerate(spec):
        dst = out if idx == len(spec) - 1 else scratch[idx % 2]
        emit_block(p, B, bk, branches, cur, dst)
        cur = dst
    p.close()
    nc.finalize()
    return nc, B.used_inputs


def _shapes_of(m):
    out = {}
    for k, v in m.items():
        dt = {np.dtype(np.float32): F32, np.dtype(bf16): BF16,
              np.dtype(np.int32): I32}[v.dtype]
        out[k] = (v.shape, dt)
    return out


KERNEL_SPEC = BLOCK_SPEC_DEFAULT


def kernel(**inputs):
    in_maps = stage_inputs(inputs)
    shapes = _shapes_of(in_maps[0])
    nc, used = build_program(KERNEL_SPEC, shapes)
    used = set(used) | {"x"}
    ims = [{k: m[k] for k in m if k in used} for m in in_maps]
    res = run_bass_kernel_spmd(nc, ims, core_ids=list(range(NCORES)))
    return np.stack([np.asarray(r["out"], np.float32)
                     for r in res.results], axis=0)



def emit_l2ffn(p, B, xin, xout, first):
    QT = T // 4
    emit_load_wT(p, B, "w_l2_wrT", B.A2, 0, D, D)
    emit_load_wT(p, B, "w_l2_wkT", B.A2, 8192, D, D)
    emit_load_wT(p, B, "w_l2_wvT", B.A2, 16384, D, D)
    RQ = 24576
    mur = p.sbuf("l2_mur", [P, 8])
    muk = p.sbuf("l2_muk", [P, 8])
    bcol = p.sbuf("l2_bcol", [P, 8])
    p.dma(mur[:], B.din("w_l2_mur"))
    p.dma(muk[:], B.din("w_l2_muk"))
    p.barrier()

    def build_mix(q, mu, dst_ofs):
        t0 = q * QT
        for c in range(8):
            dst = B.A3[:, dst_ofs + c * QT: dst_ofs + (c + 1) * QT]
            tmp = B.xb.nxt()
            if t0 == 0:
                op_ts(p, dst[:, 0:1], B.XT[:, c * T: c * T + 1],
                      mu[:, c:c + 1], ALU.mult)
                cur = B.XT[:, c * T + 1: c * T + QT]
                prv = B.XT[:, c * T: c * T + QT - 1]
                op_tt(p, tmp[:, :QT - 1], cur, prv, ALU.subtract)
                op_stt(p, dst[:, 1:], tmp[:, :QT - 1], mu[:, c:c + 1], prv,
                       ALU.mult, ALU.add)
            else:
                pcol = bcol[:, c:c + 1]
                op_tt(p, tmp[:, 0:1], B.XT[:, c * T + t0: c * T + t0 + 1],
                      pcol, ALU.subtract)
                op_stt(p, dst[:, 0:1], tmp[:, 0:1], mu[:, c:c + 1], pcol,
                       ALU.mult, ALU.add)
                cur = B.XT[:, c * T + t0 + 1: c * T + t0 + QT]
                prv = B.XT[:, c * T + t0: c * T + t0 + QT - 1]
                op_tt(p, tmp[:, 1:QT], cur, prv, ALU.subtract)
                op_stt(p, dst[:, 1:], tmp[:, 1:QT], mu[:, c:c + 1], prv,
                       ALU.mult, ALU.add)

    for q in range(4):
        t0 = q * QT
        build_mix(q, mur, 0)
        for it in range(QT // P):
            for j in (0, 512):
                ps = B.pb.nxt()
                p.mm_group([(ps[:, :512],
                             B.A3[:, c * QT + it * P: c * QT + (it + 1) * P],
                             B.A2[:, c * 1024 + j: c * 1024 + j + 512],
                             c == 0, c == 7, None) for c in range(8)],
                           reads=[B.A3[:], B.A2[:]], writes=[ps[:]])
                op_act(p, B.A2[:, RQ + it * 1024 + j: RQ + it * 1024 + j + 512],
                       ps[:, :512], AF.Sigmoid)
        for c in range(8):
            op_copy(p, bcol[:, c:c + 1],
                    B.XT[:, c * T + t0 + QT - 1: c * T + t0 + QT], eng="act")
        p.barrier()
        build_mix(q, muk, 0)
        for dc in range(8):
            for jt in (0, 512):
                ps = B.pb.nxt()
                p.mm_group([(ps[:, :512],
                             B.A2[:, 8192 + c * 1024 + dc * P:
                                  8192 + c * 1024 + (dc + 1) * P],
                             B.A3[:, c * QT + jt: c * QT + jt + 512],
                             c == 0, c == 7, None) for c in range(8)],
                           reads=[B.A3[:], B.A2[:]], writes=[ps[:]])
                fr = B.fs.nxt()
                op_act(p, fr[:], ps[:, :512], AF.Relu)
                op_act(p, B.XT[:, dc * T + t0 + jt: dc * T + t0 + jt + 512],
                       fr[:], AF.Square)
        p.barrier()
        for it0 in range(0, QT // P, 4):
          xres_t = {}
          for it in range(it0, it0 + 4):
            gi = t0 // P + it
            xres_t[it] = B.ft.nxt()
            p.dma(xres_t[it][:], (xin if first else xout)[gi * P:(gi + 1) * P, :])
          for it in range(it0, it0 + 4):
            gi = t0 // P + it
            xres = xres_t[it]
            for j in (0, 512):
                ps = B.pb.nxt()
                p.mm_group([(ps[:, :512],
                             B.XT[:, c * T + t0 + it * P:
                                  c * T + t0 + (it + 1) * P],
                             B.A2[:, 16384 + c * 1024 + j:
                                  16384 + c * 1024 + j + 512],
                             c == 0, c == 7, None) for c in range(8)],
                           reads=[B.XT[:], B.A2[:]], writes=[ps[:]])
                fv = B.fs.nxt()
                op_tt(p, fv[:], ps[:, :512],
                      B.A2[:, RQ + it * 1024 + j: RQ + it * 1024 + j + 512],
                      ALU.mult)
                op_tt(p, xres[:, j:j + 512], xres[:, j:j + 512], fv[:],
                      ALU.add)
            p.dma(xout[gi * P:(gi + 1) * P, :], xres[:], q="pool")
          p.barrier()



def _flash_tks(p, B, psS, psO_A, psO_B, tqc, ntk, ncA, ncB, v2, cmask=None):
    def emit_S(tk, pA, pB):
        p.mm_group(
            [(pA[:, :512], B.A3[0:64, tk * 128: tk * 128 + 128],
              B.A3[0:64, T + tqc * 512: T + (tqc + 1) * 512],
              True, True, (0, 0)),
             (pB[:, :512], B.A3[64:128, tk * 128: tk * 128 + 128],
              B.A3[64:128, T + tqc * 512: T + (tqc + 1) * 512],
              True, True, (64, 0))],
            reads=[B.A3[:]], writes=[pA[:], pB[:]])
    banks = []
    for tk in range(min(2, ntk)):
        pA, pB = psS.nxt(), psS.nxt()
        banks.append((pA, pB))
        emit_S(tk, pA, pB)
    for tk in range(ntk):
        pA, pB = banks[tk % 2]
        fsA = B.fs.items[(2 * tk) % 4]
        fsB = B.fs.items[(2 * tk + 1) % 4]
        op_copy(p, fsA[:, :512], pA[:, :512], eng="dve")
        op_copy(p, fsB[:, :512], pB[:, :512], eng="dve")
        ptA = B.xb.nxt()
        op_act(p, ptA[:, :512], fsA[:, :512], AF.Exp, bias=ncA[:],
               scale=0.125)
        op_act(p, ptA[:, 512:1024], fsB[:, :512], AF.Exp, bias=ncB[:],
               scale=0.125)
        if cmask is not None:
            diag_off = tk * 128 - tqc * 512
            if 0 <= diag_off < 512:
                msl = cmask[:, 384 - diag_off: 896 - diag_off]
                op_tt(p, ptA[:, :512], ptA[:, :512], msl, ALU.mult)
                op_tt(p, ptA[:, 512:1024], ptA[:, 512:1024], msl, ALU.mult)
        if tk + 2 < ntk:
            emit_S(tk + 2, pA, pB)
        p.mm_group([(psO_A[0:65, :512], v2[:, tk * 65: tk * 65 + 65],
                     ptA[:, :512], tk == 0, tk == ntk - 1, None)],
                   reads=[v2[:], ptA[:]], writes=[psO_A[:]])
        p.mm_group([(psO_B[0:65, :512],
                     v2[:, 2080 + tk * 65: 2080 + tk * 65 + 65],
                     ptA[:, 512:1024], tk == 0, tk == ntk - 1, None)],
                   reads=[v2[:], ptA[:]], writes=[psO_B[:]])


def emit_l0attn(p, B, xin, xout, first, wq="w_l0_qT", wk="w_l0_kT",
                wv="w_l0_vT", wo="w_l0_oT"):
    v2 = p.sbuf("l0_v2", [P, 2 * 32 * 65], BF16)
    cmask = p.sbuf("l0_cmask", [P, 896], BF16)
    ones2 = p.sbuf("l0_ones2", [P, 65], BF16)
    onesr = p.sbuf("l0_onesr", [1, P], BF16)
    mxq = p.sbuf("l0_mxq", [P, 1])
    mxk = p.sbuf("l0_mxk", [P, 1])
    ncol = p.sbuf("l0_ncol", [P, 1])
    nb16 = p.sbuf("l0_nb16", [1, 2], BF16)
    lrow = p.sbuf("l0_lrow", [1, 512])
    lb16 = p.sbuf("l0_lb16", [1, 512], BF16)
    lbc = p.sbuf("l0_lbc", [P, 512], BF16)
    p.dma(cmask[:], B.din("c_cmask"))
    p.dma(ones2[:], B.din("c_ones2"))
    p.dma(onesr[:], B.din("c_onesr"))
    p.barrier()

    psO_A, psO_B = B.pb.items[0], B.pb.items[1]
    psS = Rot(B.pb.items[2:6])

    for pr in range(8):
        hA, hB = 2 * pr, 2 * pr + 1
        wq_t = B.xb.items[0]; wk_t = B.xb.items[1]; wv_t = B.xb.items[2]
        for wname, wt in ((wq, wq_t), (wk, wk_t), (wv, wv_t)):
            for g in range(2):
                stgs = []
                for c in range(4 * g, 4 * g + 4):
                    stg = B.fs.nxt()
                    stgs.append((c, stg))
                    p.dma(stg[:, :128],
                          B.din(wname)[c * P:(c + 1) * P,
                                       pr * 128:(pr + 1) * 128],
                          q="sp" if c % 2 else "pool")
                for c, stg in stgs:
                    op_copy(p, wt[:, c * 128:(c + 1) * 128], stg[:, :128],
                            eng="dve" if c % 2 else "act")
                p.barrier()
        for (wt, dst_ofs) in ((wq_t, T), (wk_t, 0)):
            for tq in range(8):
                ps = psS.nxt()
                p.mm_group([(ps[:, :512], wt[:, c * 128:(c + 1) * 128],
                             xt_chunk(B.XT, c, tq * 512, 512),
                             c == 0, c == 7, None) for c in range(8)],
                           reads=[wt[:], B.XT[:]], writes=[ps[:]])
                op_copy(p, B.A3[:, dst_ofs + tq * 512: dst_ofs + (tq + 1) * 512],
                        ps[:, :512], eng="act")
        op_memset(p, v2[:], 1.0)
        for tq in range(8):
            ps = psS.nxt()
            p.mm_group([(ps[:, :512], wv_t[:, c * 128:(c + 1) * 128],
                         xt_chunk(B.XT, c, tq * 512, 512),
                         c == 0, c == 7, None) for c in range(8)],
                       reads=[wv_t[:], B.XT[:]], writes=[ps[:]])
            vtb = B.xb.items[0] if tq % 2 == 0 else B.xb.items[1]
            op_copy(p, vtb[:, :512], ps[:, :512])
            for q in range(4):
                it = tq * 4 + q
                ptb = B.pt.nxt()
                op_transpose(p, ptb[:, :P], vtb[:, q * 128:(q + 1) * 128],
                             B.ident[:])
                op_copy(p, v2[:, it * 65: it * 65 + 64],
                        ptb[:, 0:64], eng="act")
                op_copy(p, v2[:, 2080 + it * 65: 2080 + it * 65 + 64],
                        ptb[:, 64:128], eng="act")
        p.barrier()
        for (src_ofs, mx) in ((T, mxq), (0, mxk)):
            op_memset(p, mx[:], 0.0, eng="pool")
            for g in range(4):
                sq = B.xb.items[2]
                op_act(p, sq[:], B.A3[:, src_ofs + g * 1024:
                                      src_ofs + (g + 1) * 1024], AF.Square)
                for j in (0, 512):
                    psn = psS.nxt()
                    p.mm_group([(psn[0:65, :512], ones2[:], sq[:, j:j + 512],
                                 True, True, None)],
                               reads=[ones2[:], sq[:]], writes=[psn[:]])
                    r = B.col.nxt()
                    op_red(p, r[0:1, :], psn[0:1, :512], ALU.max)
                    op_red(p, r[64:65, :], psn[64:65, :512], ALU.max)
                    op_tt(p, mx[0:1, :], mx[0:1, :], r[0:1, :], ALU.max)
                    op_tt(p, mx[64:65, :], mx[64:65, :], r[64:65, :], ALU.max)
        op_tt(p, ncol[0:1, :], mxq[0:1, :], mxk[0:1, :], ALU.mult)
        op_tt(p, ncol[64:65, :], mxq[64:65, :], mxk[64:65, :], ALU.mult)
        op_act(p, ncol[0:1, :], ncol[0:1, :], AF.Sqrt)
        op_act(p, ncol[64:65, :], ncol[64:65, :], AF.Sqrt)
        op_ts(p, ncol[0:1, :], ncol[0:1, :], -0.125, ALU.mult)
        op_ts(p, ncol[64:65, :], ncol[64:65, :], -0.125, ALU.mult)
        op_copy(p, nb16[:, 0:1], ncol[0:1, :])
        op_copy(p, nb16[:, 1:2], ncol[64:65, :])
        ncA = B.col.nxt(); ncB = B.col.nxt()
        psn = psS.nxt()
        p.mm_group([(psn[:, 0:1], onesr[:], nb16[:, 0:1], True, True, None),
                    (psn[:, 1:2], onesr[:], nb16[:, 1:2], False, True, None)],
                   reads=[onesr[:], nb16[:]], writes=[psn[:]])
        op_copy(p, ncA[:], psn[:, 0:1])
        op_copy(p, ncB[:], psn[:, 1:2])
        p.barrier()
        for tqc in range(8):
            ntk = 4 * tqc + 4
            _flash_tks(p, B, psS, psO_A, psO_B, tqc, ntk, ncA, ncB, v2,
                       cmask=cmask)
            for (psO, h) in ((psO_A, hA), (psO_B, hB)):
                c = h // 2
                ro = (h % 2) * 64
                op_act(p, lrow[:, :512], psO[64:65, :512], AF.Ln)
                op_act(p, lrow[:, :512], lrow[:, :512], AF.Exp, scale=-1.0)
                op_copy(p, lb16[:, :512], lrow[:, :512])
                psb = psS.nxt()
                p.mm_group([(psb[0:64, :512], onesr[:, 0:64], lb16[:, :512],
                             True, True, None)],
                           reads=[onesr[:], lb16[:]], writes=[psb[:]])
                op_copy(p, lbc[0:64, :512], psb[0:64, :512], eng="act")
                p.op("dve",
                     lambda e, c=c, ro=ro, tqc=tqc, psO=psO:
                     e.tensor_tensor(
                         B.A2[ro:ro + 64,
                              c * T + tqc * 512: c * T + (tqc + 1) * 512],
                         psO[0:64, :512], lbc[0:64, :512], ALU.mult),
                     reads=[psO[:], lbc[:]], writes=[B.A2[:]])
            p.barrier()
    emit_load_wT(p, B, wo, B.A3, 0, D, D)
    for i0 in range(0, NT, 4):
        xres_t = {}
        for i in range(i0, i0 + 4):
            xres_t[i] = B.ft.nxt()
            p.dma(xres_t[i][:], (xin if first else xout)[i * P:(i + 1) * P, :])
        for i in range(i0, i0 + 4):
            xres = xres_t[i]
            for j in (0, 512):
                ps = B.pb.nxt()
                p.mm_group([(ps[:, :512], xt_chunk(B.A2, c, i * P, P),
                             B.A3[:, c * 1024 + j: c * 1024 + j + 512],
                             c == 0, c == 7, None) for c in range(8)],
                           reads=[B.A2[:], B.A3[:]], writes=[ps[:]])
                op_tt(p, xres[:, j:j + 512], xres[:, j:j + 512], ps[:, :512],
                      ALU.add)
            p.dma(xout[i * P:(i + 1) * P, :], xres[:], q="pool")
        p.barrier()



def emit_l2attn(p, B, xin, xout, first):
    v2 = p._sbufs.get("l0_v2") or p.sbuf("l0_v2", [P, 2 * 32 * 65], BF16)
    ones2 = p._sbufs["l0_ones2"] if "l0_ones2" in p._sbufs else \
        p.sbuf("l0_ones2", [P, 65], BF16)
    onesr = p._sbufs["l0_onesr"] if "l0_onesr" in p._sbufs else \
        p.sbuf("l0_onesr", [1, P], BF16)
    mxq = p._sbufs.get("l0_mxq") or p.sbuf("l0_mxq", [P, 1])
    mxk = p._sbufs.get("l0_mxk") or p.sbuf("l0_mxk", [P, 1])
    ncol = p._sbufs.get("l0_ncol") or p.sbuf("l0_ncol", [P, 1])
    nb16 = p._sbufs.get("l0_nb16") or p.sbuf("l0_nb16", [1, 2], BF16)
    lrow = p._sbufs.get("l0_lrow") or p.sbuf("l0_lrow", [1, 512])
    lb16 = p._sbufs.get("l0_lb16") or p.sbuf("l0_lb16", [1, 512], BF16)
    lbc = p._sbufs.get("l0_lbc") or p.sbuf("l0_lbc", [P, 512], BF16)
    qs2 = p.sbuf("l2_qs2", [P, 8])
    ks2 = p.sbuf("l2_ks2", [P, 8])
    frow = lrow
    fb16 = lb16
    if "l0_ones2" not in p._sbufs or True:
        p.dma(ones2[:], B.din("c_ones2"))
        p.dma(onesr[:], B.din("c_onesr"))
    p.dma(qs2[:], B.din("w_l2_qs2"))
    p.dma(ks2[:], B.din("w_l2_ks2"))
    p.barrier()

    psO_A, psO_B = B.pb.items[0], B.pb.items[1]
    psS = Rot(B.pb.items[2:6])

    for pr in range(8):
        hA, hB = 2 * pr, 2 * pr + 1
        wq_t = B.xb.items[0]; wk_t = B.xb.items[1]; wv_t = B.xb.items[2]
        for wname, wt in (("w_l2_qT", wq_t), ("w_l2_kT", wk_t),
                          ("w_l2_vT", wv_t)):
            for g in range(2):
                stgs = []
                for c in range(4 * g, 4 * g + 4):
                    stg = B.fs.nxt()
                    stgs.append((c, stg))
                    p.dma(stg[:, :128],
                          B.din(wname)[c * P:(c + 1) * P,
                                       pr * 128:(pr + 1) * 128],
                          q="sp" if c % 2 else "pool")
                for c, stg in stgs:
                    op_copy(p, wt[:, c * 128:(c + 1) * 128], stg[:, :128],
                            eng="dve" if c % 2 else "act")
                p.barrier()
        for (wt, dst_ofs, sc) in ((wq_t, T, qs2), (wk_t, 0, ks2)):
            for tq in range(8):
                ps = psS.nxt()
                p.mm_group([(ps[:, :512], wt[:, c * 128:(c + 1) * 128],
                             xt_chunk(B.XT, c, tq * 512, 512),
                             c == 0, c == 7, None) for c in range(8)],
                           reads=[wt[:], B.XT[:]], writes=[ps[:]])
                op_copy(p, B.A3[:, dst_ofs + tq * 512: dst_ofs + (tq + 1) * 512],
                        ps[:, :512], eng="act")
        op_memset(p, v2[:], 1.0)
        for tq in range(8):
            ps = psS.nxt()
            p.mm_group([(ps[:, :512], wv_t[:, c * 128:(c + 1) * 128],
                         xt_chunk(B.XT, c, tq * 512, 512),
                         c == 0, c == 7, None) for c in range(8)],
                       reads=[wv_t[:], B.XT[:]], writes=[ps[:]])
            vtb = B.xb.items[0] if tq % 2 == 0 else B.xb.items[1]
            op_copy(p, vtb[:, :512], ps[:, :512])
            for q in range(4):
                it = tq * 4 + q
                ptb = B.pt.nxt()
                op_transpose(p, ptb[:, :P], vtb[:, q * 128:(q + 1) * 128],
                             B.ident[:])
                op_copy(p, v2[:, it * 65: it * 65 + 64],
                        ptb[:, 0:64], eng="act")
                op_copy(p, v2[:, 2080 + it * 65: 2080 + it * 65 + 64],
                        ptb[:, 64:128], eng="act")
        p.barrier()
        for (dst_ofs, sc) in ((T, qs2), (0, ks2)):
            for tq in range(8):
                sq = B.xb.nxt()
                op_act(p, sq[:, :512],
                       B.A3[:, dst_ofs + tq * 512: dst_ofs + (tq + 1) * 512],
                       AF.Square)
                psn = psS.nxt()
                p.mm_group([(psn[0:65, :512], ones2[:], sq[:, :512],
                             True, True, None)],
                           reads=[ones2[:], sq[:]], writes=[psn[:]])
                for hh, row in ((0, 0), (64, 64)):
                    op_act(p, frow[:, :512], psn[row:row + 1, :512], AF.Ln)
                    op_ts(p, frow[:, :512], frow[:, :512], 0.5, ALU.mult,
                          -27.631021, ALU.max)
                    op_act(p, frow[:, :512], frow[:, :512], AF.Exp,
                           scale=-1.0)
                    op_copy(p, fb16[:, :512], frow[:, :512])
                    psb = psS.nxt()
                    p.mm_group([(psb[:, :512], onesr[:], fb16[:, :512],
                                 True, True, None)],
                               reads=[onesr[:], fb16[:]], writes=[psb[:]])
                    fbc = B.fs.nxt()
                    op_copy(p, fbc[:, :512], psb[:, :512], eng="act")
                    sl = B.A3[hh:hh + 64,
                              dst_ofs + tq * 512: dst_ofs + (tq + 1) * 512]
                    op_stt(p, sl, sl, sc[hh:hh + 64, pr:pr + 1],
                           fbc[hh:hh + 64, :512], ALU.mult, ALU.mult)
        p.barrier()
        for (src_ofs, mx) in ((T, mxq), (0, mxk)):
            op_memset(p, mx[:], 0.0, eng="pool")
            for g in range(4):
                sq = B.xb.items[2]
                op_act(p, sq[:], B.A3[:, src_ofs + g * 1024:
                                      src_ofs + (g + 1) * 1024], AF.Square)
                for j in (0, 512):
                    psn = psS.nxt()
                    p.mm_group([(psn[0:65, :512], ones2[:], sq[:, j:j + 512],
                                 True, True, None)],
                               reads=[ones2[:], sq[:]], writes=[psn[:]])
                    r = B.col.nxt()
                    op_red(p, r[0:1, :], psn[0:1, :512], ALU.max)
                    op_red(p, r[64:65, :], psn[64:65, :512], ALU.max)
                    op_tt(p, mx[0:1, :], mx[0:1, :], r[0:1, :], ALU.max)
                    op_tt(p, mx[64:65, :], mx[64:65, :], r[64:65, :], ALU.max)
        op_tt(p, ncol[0:1, :], mxq[0:1, :], mxk[0:1, :], ALU.mult)
        op_tt(p, ncol[64:65, :], mxq[64:65, :], mxk[64:65, :], ALU.mult)
        op_act(p, ncol[0:1, :], ncol[0:1, :], AF.Sqrt)
        op_act(p, ncol[64:65, :], ncol[64:65, :], AF.Sqrt)
        op_ts(p, ncol[0:1, :], ncol[0:1, :], -0.125, ALU.mult)
        op_ts(p, ncol[64:65, :], ncol[64:65, :], -0.125, ALU.mult)
        op_copy(p, nb16[:, 0:1], ncol[0:1, :])
        op_copy(p, nb16[:, 1:2], ncol[64:65, :])
        ncA = B.col.nxt(); ncB = B.col.nxt()
        psn = psS.nxt()
        p.mm_group([(psn[:, 0:1], onesr[:], nb16[:, 0:1], True, True, None),
                    (psn[:, 1:2], onesr[:], nb16[:, 1:2], False, True, None)],
                   reads=[onesr[:], nb16[:]], writes=[psn[:]])
        op_copy(p, ncA[:], psn[:, 0:1])
        op_copy(p, ncB[:], psn[:, 1:2])
        p.barrier()
        for tqc in range(8):
            ntk = 32
            _flash_tks(p, B, psS, psO_A, psO_B, tqc, ntk, ncA, ncB, v2)
            for (psO, h) in ((psO_A, hA), (psO_B, hB)):
                c = h // 2
                ro = (h % 2) * 64
                op_act(p, lrow[:, :512], psO[64:65, :512], AF.Ln)
                op_act(p, lrow[:, :512], lrow[:, :512], AF.Exp, scale=-1.0)
                op_copy(p, lb16[:, :512], lrow[:, :512])
                psb = psS.nxt()
                p.mm_group([(psb[0:64, :512], onesr[:, 0:64], lb16[:, :512],
                             True, True, None)],
                           reads=[onesr[:], lb16[:]], writes=[psb[:]])
                op_copy(p, lbc[0:64, :512], psb[0:64, :512], eng="act")
                p.op("dve",
                     lambda e, c=c, ro=ro, tqc=tqc, psO=psO:
                     e.tensor_tensor(
                         B.A2[ro:ro + 64,
                              c * T + tqc * 512: c * T + (tqc + 1) * 512],
                         psO[0:64, :512], lbc[0:64, :512], ALU.mult),
                     reads=[psO[:], lbc[:]], writes=[B.A2[:]])
            p.barrier()
    emit_load_wT(p, B, "w_l2_oT", B.A3, 0, D, D)
    bor = p._sbufs.get("l1_grow") or p.sbuf("l1_grow", [1, D], BF16)
    bof = B.ft.nxt()
    p.dma(bof[0:1, :], B.din("w_l2_bo"))
    op_copy(p, bor[:], bof[0:1, :])
    p.barrier()
    for j in (0, 512):
        psb = B.pb.nxt()
        p.mm_group([(psb[:, :512], onesr[:], bor[:, j:j + 512],
                     True, True, None)],
                   reads=[onesr[:], bor[:]], writes=[psb[:]])
        op_copy(p, B.bbc[:, j:j + 512], psb[:, :512], eng="act")
    p.barrier()
    for i0 in range(0, NT, 4):
        xres_t = {}
        for i in range(i0, i0 + 4):
            xres_t[i] = B.ft.nxt()
            p.dma(xres_t[i][:], (xin if first else xout)[i * P:(i + 1) * P, :])
        for i in range(i0, i0 + 4):
            xres = xres_t[i]
            for j in (0, 512):
                ps = B.pb.nxt()
                p.mm_group([(ps[:, :512], xt_chunk(B.A2, c, i * P, P),
                             B.A3[:, c * 1024 + j: c * 1024 + j + 512],
                             c == 0, c == 7, None) for c in range(8)],
                           reads=[B.A2[:], B.A3[:]], writes=[ps[:]])
                op_tt(p, xres[:, j:j + 512], xres[:, j:j + 512], ps[:, :512],
                      ALU.add)
                op_tt(p, xres[:, j:j + 512], xres[:, j:j + 512],
                      B.bbc[:, j:j + 512], ALU.add)
            p.dma(xout[i * P:(i + 1) * P, :], xres[:], q="pool")
        p.barrier()



L1_TAPS = 20
L1_HALO = L1_TAPS + 2
QTL = 1024


def emit_l1attn(p, B, xin, xout, first):
    QT = 1024
    cw = p.sbuf("l1_cw", [P, 48])
    cb = p.sbuf("l1_cb", [P, 16])
    inb = p.sbuf("l1_inb", [P, 48])
    dp = p.sbuf("l1_dp", [P, 16])
    onesr = p._sbufs.get("l0_onesr") or p.sbuf("l0_onesr", [1, P], BF16)
    grow = p.sbuf("l1_grow", [1, D], BF16)
    u2 = p.sbuf("l1_u2", [P, QTL + L1_HALO + 2], BF16)
    p.dma(cw[:], B.din("w_l1_convw"))
    p.dma(cb[:], B.din("w_l1_convb"))
    p.dma(inb[:], B.din("w_l1_inb"))
    p.dma(dp[:], B.din("w_l1_dp"))
    p.dma(onesr[:], B.din("c_onesr"))
    gr = B.ft.nxt()
    p.dma(gr[0:1, :], B.din("w_l1_gamma"))
    br = B.ft.nxt()
    p.dma(br[0:1, :], B.din("w_l1_beta"))
    op_copy(p, grow[:], gr[0:1, :])
    p.barrier()
    for j in (0, 512):
        psb = B.pb.nxt()
        p.mm_group([(psb[:, :512], onesr[:], grow[:, j:j + 512],
                     True, True, None)], reads=[onesr[:], grow[:]],
                   writes=[psb[:]])
        op_copy(p, B.gbc[:, j:j + 512], psb[:, :512], eng="act")
    op_copy(p, grow[:], br[0:1, :])
    p.barrier()
    for j in (0, 512):
        psb = B.pb.nxt()
        p.mm_group([(psb[:, :512], onesr[:], grow[:, j:j + 512],
                     True, True, None)], reads=[onesr[:], grow[:]],
                   writes=[psb[:]])
        op_copy(p, B.bbc[:, j:j + 512], psb[:, :512], eng="act")
    p.barrier()
    fA = B.ft.nxt(); fB = B.ft.nxt(); fC = B.ft.nxt(); fE = B.ft.nxt()
    p.dma(fA[:], B.din("w_l1_alog"))
    p.dma(fB[:], B.din("w_l1_bp"))
    p.dma(fC[:], B.din("w_l1_cp"))
    op_act(p, fA[:], fA[:], AF.Exp)
    op_ts(p, fA[:], fA[:], -1.0, ALU.mult)
    op_act(p, fA[:], fA[:], AF.Exp)
    op_tt(p, fB[:], fB[:], fC[:], ALU.mult)
    op_copy(p, fE[:], fB[:])
    g1v = B.g1[:, 0:16 * L1_TAPS].rearrange("p (c d) -> p c d", d=L1_TAPS)
    for d in range(L1_TAPS):
        p.op("dve",
             lambda e, d=d: e.tensor_reduce(
                 g1v[:, :, d:d + 1],
                 fE[:].rearrange("p (c s) -> p c s", s=64), AX.X, ALU.add),
             reads=[fE[:]], writes=[B.g1[:]])
        if d < L1_TAPS - 1:
            op_tt(p, fE[:], fE[:], fA[:], ALU.mult)
    p.op("dve", lambda e: e.tensor_tensor(g1v[:, :, 0], g1v[:, :, 0],
                                          dp[:], ALU.add),
         reads=[B.g1[:], dp[:]], writes=[B.g1[:]])
    g1b = B.A3[:, 7168:7168 + 16 * L1_TAPS]
    op_copy(p, g1b, B.g1[:, 0:16 * L1_TAPS])
    p.barrier()
    A2v = B.A2[:].rearrange("p (c t) -> p c t", t=T)
    for i0 in range(0, NT, 1):
        tiles = range(i0, min(i0 + 1, NT))
        fts = {}
        for i in tiles:
            fts[i] = B.ft.nxt()
            p.dma(fts[i][:], xin[i * P:(i + 1) * P, :])
        for i in tiles:
            xt = fts[i]
            xc = B.ft.nxt()
            m = B.col.nxt(); ss = B.col.nxt(); rs = B.col.nxt()
            op_red(p, m[:], xt[:], ALU.add)
            op_ts(p, m[:], m[:], 1.0 / D, ALU.mult)
            op_ts(p, xc[:], xt[:], m[:], ALU.subtract)
            op_act(p, xt[:], xc[:], AF.Square, accum_out=ss[:])
            op_ts(p, ss[:], ss[:], 1.0 / D, ALU.mult)
            op_act(p, ss[:], ss[:], AF.Ln, bias=B.epscol[:])
            op_act(p, rs[:], ss[:], AF.Exp, scale=-0.5)
            op_ts(p, xc[:], xc[:], rs[:], ALU.mult)
            m2 = B.col.nxt(); ss2 = B.col.nxt(); rs2 = B.col.nxt()
            op_red(p, m2[:], xc[:], ALU.add)
            op_ts(p, m2[:], m2[:], 1.0 / D, ALU.mult)
            op_ts(p, xt[:], xc[:], m2[:], ALU.subtract)
            op_act(p, xc[:], xt[:], AF.Square, accum_out=ss2[:])
            op_ts(p, ss2[:], ss2[:], 1.0 / D, ALU.mult)
            op_act(p, ss2[:], ss2[:], AF.Sqrt, bias=B.epscol[:])
            p.op("dve", lambda e, a=rs2, b=ss2: e.reciprocal(a[:], b[:]),
                 reads=[ss2[:]], writes=[rs2[:]])
            op_ts(p, xt[:], xt[:], rs2[:], ALU.mult)
            op_tt(p, xt[:], xt[:], B.gbc[:], ALU.mult)
            xnb = B.xb.nxt()
            op_tt(p, xnb[:], xt[:], B.bbc[:], ALU.add)
            ptb = B.pt.nxt()
            op_multi(p, "pe",
                     [(lambda e, c=c, xnb=xnb, ptb=ptb:
                       e.transpose(ptb[:, c * P:(c + 1) * P],
                                   xnb[:, c * P:(c + 1) * P], B.ident[:]))
                      for c in range(8)],
                     reads=[xnb[:], B.ident[:]], writes=[ptb[:]])
            op_copy(p, A2v[:, :, i * P:(i + 1) * P],
                    ptb[:].rearrange("p (c t) -> p c t", t=P))
        p.barrier()
    emit_load_wT(p, B, "w_l1_outw", B.XT, 0, 2048, 1024)
    ZOF = 16 * 1024
    inw = B.din("w_l1_inw")
    for q in range(4):
        t0 = q * QT
        h0 = max(0, t0 - L1_HALO)
        hw = t0 + QT - h0
        for c in range(16):
            wslc = {}
            for bi, cc in ((0, c), (1, 16 + c), (2, 32 + c)):
                stg = B.ft.nxt()
                wslc[bi] = stg
                qq = "sp" if bi % 2 else "pool"
                for k in range(8):
                    p.dma(stg[:, k * 128:(k + 1) * 128],
                          inw[k * P:(k + 1) * P, cc * 128:(cc + 1) * 128],
                          q=qq, no_track=True)
                p.mark_load(stg[:], qq)
            wu = B.A3[:, 4096:5120]
            wv = B.A3[:, 5120:6144]
            wg = B.A3[:, 6144:7168]
            op_copy(p, wu, wslc[0][:])
            op_copy(p, wv, wslc[1][:], eng="act")
            op_copy(p, wg, wslc[2][:])
            rext = 1 if t0 + QT < T else 0
            uwidth = hw + rext
            nsub = ceil_div(uwidth, 512)
            ups = []
            for s in range(nsub):
                n0 = h0 + s * 512
                nw = min(512, h0 + uwidth - n0)
                ps = B.pb.nxt()
                p.mm_group([(ps[:, :nw], wu[:, k * 128:(k + 1) * 128],
                             xt_chunk(B.A2, k, n0, nw), k == 0, k == 7, None)
                            for k in range(8)],
                           reads=[B.A3[:], B.A2[:]], writes=[ps[:]])
                ups.append((ps, n0, nw))
            for (ps, n0, nw) in ups:
                op_ts(p, B.A3[:, 2048 + (n0 - h0): 2048 + (n0 - h0) + nw],
                      ps[:, :nw], inb[:, c:c + 1], ALU.add)
            ucv = B.A3[:, 2048:2048 + uwidth]
            tmp = u2
            op_ts(p, tmp[:, :hw], ucv[:, 0:hw], cw[:, 16 + c:17 + c],
                  ALU.mult)
            if h0 == 0:
                op_stt(p, tmp[:, 1:hw], ucv[:, 0:hw - 1], cw[:, c:c + 1],
                       tmp[:, 1:hw], ALU.mult, ALU.add)
            else:
                op_stt(p, tmp[:, 1:hw], ucv[:, 0:hw - 1], cw[:, c:c + 1],
                       tmp[:, 1:hw], ALU.mult, ALU.add)
            if rext:
                op_stt(p, tmp[:, 0:hw], ucv[:, 1:hw + 1],
                       cw[:, 32 + c:33 + c], tmp[:, 0:hw], ALU.mult, ALU.add)
            else:
                op_stt(p, tmp[:, 0:hw - 1], ucv[:, 1:hw],
                       cw[:, 32 + c:33 + c], tmp[:, 0:hw - 1],
                       ALU.mult, ALU.add)
            up = B.A3[:, 0:uwidth]
            op_act(p, up[:, :hw], tmp[:, :hw], AF.Silu, bias=cb[:, c:c + 1])
            ofs = t0 - h0
            yg = [B.xb.nxt(), B.xb.nxt()]
            ztmp = B.xb.nxt_not(yg)
            NG = [(0, 10, "dve"), (10, L1_TAPS, "dve")]
            for gi, (d0, d1, eng) in enumerate(NG):
                g = yg[gi]
                tmp = ztmp
                for d in range(d0, d1):
                    lo = ofs - d
                    gcb = g1b[:, c * L1_TAPS + d: c * L1_TAPS + d + 1]
                    bc = gcb.broadcast_to((P, QT))
                    if lo >= 0:
                        if d == d0:
                            op_tt(p, g[:, :QT], up[:, lo: lo + QT], bc,
                                  ALU.mult, eng=eng)
                        else:
                            op_tt(p, tmp[:, :QT], up[:, lo: lo + QT], bc,
                                  ALU.mult, eng=eng)
                            op_tt(p, g[:, :QT], g[:, :QT], tmp[:, :QT],
                                  ALU.add, eng=eng)
                    else:
                        sk = -lo
                        bcs = gcb.broadcast_to((P, QT - sk))
                        if d == d0:
                            op_memset(p, g[:, 0:sk], 0.0)
                            if sk < QT:
                                op_tt(p, g[:, sk:QT], up[:, 0: QT - sk], bcs,
                                      ALU.mult, eng=eng)
                        elif sk < QT:
                            op_tt(p, tmp[:, 0:QT - sk], up[:, 0: QT - sk],
                                  bcs, ALU.mult, eng=eng)
                            op_tt(p, g[:, sk:QT], g[:, sk:QT],
                                  tmp[:, 0:QT - sk], ALU.add, eng=eng)
            for (wsl, act, dst_ofs, bc) in ((wv, AF.Silu, 0, 16 + c),
                                            (wg, AF.Sigmoid, 1024, 32 + c)):
                for s in range(2):
                    n0 = t0 + s * 512
                    ps = B.pb.nxt()
                    p.mm_group([(ps[:, :512], wsl[:, k * 128:(k + 1) * 128],
                                 xt_chunk(B.A2, k, n0, 512),
                                 k == 0, k == 7, None) for k in range(8)],
                               reads=[B.A3[:], B.A2[:]], writes=[ps[:]])
                    op_act(p, B.A3[:, 2048 + dst_ofs + s * 512:
                                   2048 + dst_ofs + s * 512 + 512],
                           ps[:, :512], act, bias=inb[:, bc:bc + 1])
            zf = ztmp
            for s in range(2):
                zs = zf[:, s * 512:(s + 1) * 512]
                op_tt(p, zs, yg[0][:, s * 512:(s + 1) * 512],
                      yg[1][:, s * 512:(s + 1) * 512], ALU.add)
                op_tt(p, zs, zs,
                      B.A3[:, 2048 + 1024 + s * 512: 2048 + 1024 + s * 512 + 512],
                      ALU.mult)
                op_tt(p, B.XT[:, ZOF + c * QT + s * 512:
                              ZOF + c * QT + s * 512 + 512], zs,
                      B.A3[:, 2048 + s * 512: 2048 + s * 512 + 512], ALU.add)
            p.barrier()
        for it0 in range(0, QT // P, 4):
            xres_t = {}
            for it in range(it0, it0 + 4):
                gi = t0 // P + it
                xres_t[it] = B.ft.nxt()
                p.dma(xres_t[it][:],
                      (xin if first else xout)[gi * P:(gi + 1) * P, :])
            for it in range(it0, it0 + 4):
                gi = t0 // P + it
                xres = xres_t[it]
                for j in (0, 512):
                    ps = B.pb.nxt()
                    p.mm_group([(ps[:, :512],
                                 B.XT[:, ZOF + cc * QT + it * P:
                                      ZOF + cc * QT + (it + 1) * P],
                                 B.XT[:, cc * 1024 + j: cc * 1024 + j + 512],
                                 cc == 0, cc == 15, None)
                                for cc in range(16)],
                               reads=[B.XT[:]], writes=[ps[:]])
                    op_tt(p, xres[:, j:j + 512], xres[:, j:j + 512],
                          ps[:, :512], ALU.add)
                p.dma(xout[gi * P:(gi + 1) * P, :], xres[:], q="pool")
            p.barrier()
    ob = B.ft.nxt()
    p.dma(ob[0:1, :], B.din("w_l1_outb"))
    op_copy(p, grow[:], ob[0:1, :])
    p.barrier()
    for j in (0, 512):
        psb = B.pb.nxt()
        p.mm_group([(psb[:, :512], onesr[:], grow[:, j:j + 512],
                     True, True, None)], reads=[onesr[:], grow[:]],
                   writes=[psb[:]])
        op_copy(p, B.bbc[:, j:j + 512], psb[:, :512], eng="act")
    p.barrier()
    for i0 in range(0, NT, 2):
        xres_t = {}
        xin_t = {}
        for i in range(i0, i0 + 2):
            xres_t[i] = B.ft.nxt()
            p.dma(xres_t[i][:], xout[i * P:(i + 1) * P, :])
            xin_t[i] = B.ft.nxt()
            p.dma(xin_t[i][:], xin[i * P:(i + 1) * P, :], q="pool")
        for i in range(i0, i0 + 2):
            xres = xres_t[i]
            xt = xin_t[i]
            m = B.col.nxt(); ss = B.col.nxt(); rs = B.col.nxt()
            op_red(p, m[:], xt[:], ALU.add)
            op_ts(p, m[:], m[:], 1.0 / D, ALU.mult)
            op_ts(p, xt[:], xt[:], m[:], ALU.subtract)
            sq = B.fs.nxt()
            op_act(p, sq[:, :512], xt[:, :512], AF.Square, accum_out=ss[:])
            ss2 = B.col.nxt()
            op_act(p, sq[:, :512], xt[:, 512:], AF.Square, accum_out=ss2[:])
            op_tt(p, ss[:], ss[:], ss2[:], ALU.add)
            op_ts(p, ss[:], ss[:], 1.0 / D, ALU.mult)
            op_act(p, ss[:], ss[:], AF.Ln, bias=B.epscol[:])
            op_act(p, rs[:], ss[:], AF.Exp, scale=-0.5)
            op_ts(p, xt[:], xt[:], rs[:], ALU.mult)
            op_tt(p, xres[:], xres[:], xt[:], ALU.add)
            op_tt(p, xres[:], xres[:], B.bbc[:], ALU.add)
            p.dma(xout[i * P:(i + 1) * P, :], xres[:], q="pool")
        p.barrier()



def emit_quant(p, B, out_bf, src, W1, W2, W3, scl, rsc, F):
    G = F // 16
    s3 = src.rearrange("p (g k) -> p g k", k=16)
    p.op("dve", lambda e: e.tensor_reduce(
        scl.rearrange("p (g o) -> p g o", o=1), s3, AX.X, ALU.max,
        apply_absolute_value=True), reads=[src], writes=[scl])
    op_ts(p, scl, scl, 1.0 / 6.0, ALU.mult, 1e-12, ALU.max)
    p.op("dve", lambda e: e.reciprocal(rsc, scl), reads=[scl], writes=[rsc])
    op_act(p, W1, src, AF.Abs)
    rb = rsc.rearrange("p (g o) -> p g o", o=1)[:, :G].broadcast_to((P, G, 16))
    w13 = W1.rearrange("p (g k) -> p g k", k=16)
    p.op("dve", lambda e: e.tensor_tensor(w13, w13, rb, ALU.mult),
         reads=[W1, rsc], writes=[W1])
    op_ts(p, W2, W1, 0.25, ALU.is_gt)
    for m in (0.75, 1.25, 1.75):
        op_stt(p, W2, W1, m, W2, ALU.is_gt, ALU.add)
    op_ts(p, W3, W1, 2.5, ALU.is_gt)
    for m in (3.5, 5.0, 5.0):
        op_stt(p, W3, W1, m, W3, ALU.is_gt, ALU.add)
    op_stt(p, W2, W2, 0.5, W3, ALU.mult, ALU.add)
    op_act(p, W1, src, AF.Sign)
    op_tt(p, W2, W2, W1, ALU.mult)
    sb = scl.rearrange("p (g o) -> p g o", o=1)[:, :G].broadcast_to((P, G, 16))
    o3 = out_bf.rearrange("p (g k) -> p g k", k=16)
    w23 = W2.rearrange("p (g k) -> p g k", k=16)
    p.op("dve", lambda e: e.tensor_tensor(o3, w23, sb, ALU.mult),
         reads=[W2, scl], writes=[out_bf])


L3F_PARTS = [(0, 768), (768, 768), (1536, 768), (2304, 512)]


def emit_l3ffn(p, B, xin, xout, first):
    qsc = B.g1[:, 0:64]
    qrs = B.g1[:, 64:128]
    v2f = p._sbufs.get("l0_v2") or p.sbuf("l0_v2", [P, 2 * 32 * 65], BF16)
    qa = v2f[:, 0:1024]
    qb = v2f[:, 1024:2048]
    hqt = v2f[:, 2048:2816]
    XTv = B.XT[:].rearrange("p (c t) -> p c t", t=T)
    for i0 in range(0, NT, 2):
        tiles = range(i0, min(i0 + 2, NT))
        fts = {}
        for i in tiles:
            fts[i] = B.ft.nxt()
            p.dma(fts[i][:], xin[i * P:(i + 1) * P, :])
        for i in tiles:
            xt = fts[i]
            xc = B.ft.nxt_not(list(fts.values()))
            m = B.col.nxt(); ss = B.col.nxt(); rs = B.col.nxt()
            op_red(p, m[:], xt[:], ALU.add)
            op_ts(p, m[:], m[:], 1.0 / D, ALU.mult)
            op_ts(p, xc[:], xt[:], m[:], ALU.subtract)
            op_act(p, xt[:], xc[:], AF.Square, accum_out=ss[:])
            op_ts(p, ss[:], ss[:], 1.0 / D, ALU.mult)
            op_act(p, ss[:], ss[:], AF.Ln, bias=B.epscol[:])
            op_act(p, rs[:], ss[:], AF.Exp, scale=-0.5)
            op_ts(p, xc[:], xc[:], rs[:], ALU.mult)
            xnb = B.xb.nxt()
            w2 = B.ft.nxt_not([xt, xc] + list(fts.values()))
            emit_quant(p, B, xnb[:], xc[:], xt[:], w2[:], qa, qsc, qrs, 1024)
            ptb = B.pt.nxt()
            op_multi(p, "pe",
                     [(lambda e, c=c, xnb=xnb, ptb=ptb:
                       e.transpose(ptb[:, c * P:(c + 1) * P],
                                   xnb[:, c * P:(c + 1) * P], B.ident[:]))
                      for c in range(8)],
                     reads=[xnb[:], B.ident[:]], writes=[ptb[:]])
            op_copy(p, XTv[:, :, i * P:(i + 1) * P],
                    ptb[:].rearrange("p (c t) -> p c t", t=P))
        p.barrier()
    for (i0p, ipw) in L3F_PARTS:
        nrc = ipw // 128
        for (wname, aofs) in (("w_l3_gate", 0), ("w_l3_up", 8 * ipw)):
            for r in range(nrc):
                stg = B.ft.nxt()
                p.dma(stg[:], B.din(wname)[i0p + r * P: i0p + (r + 1) * P, :])
                p.barrier()
                w2 = B.ft.nxt(); w3 = B.ft.nxt()
                emit_quant(p, B, qa, stg[:], w2[:], w3[:], qb, qsc, qrs, 1024)
                ptb = B.pt.nxt()
                op_multi(p, "pe",
                         [(lambda e, c=c, ptb=ptb:
                           e.transpose(ptb[:, c * P:(c + 1) * P],
                                       qa[:, c * P:(c + 1) * P], B.ident[:]))
                          for c in range(8)],
                         reads=[qa, B.ident[:]], writes=[ptb[:]])
                for c in range(8):
                    op_copy(p, B.A2[:, aofs + c * ipw + r * P:
                                    aofs + c * ipw + (r + 1) * P],
                            ptb[:, c * P:(c + 1) * P],
                            eng="act" if c % 2 else "dve")
                p.barrier()
        WDOF = 16 * ipw
        for r in range(8):
            stg = B.ft.nxt()
            p.dma(stg[:, :ipw],
                  B.din("w_l3_down")[r * P:(r + 1) * P, i0p:i0p + ipw])
            p.barrier()
            w2 = B.ft.nxt(); w3 = B.ft.nxt()
            emit_quant(p, B, qa[:, :ipw], stg[:, :ipw], w2[:, :ipw],
                       w3[:, :ipw], qb[:, :ipw], qsc[:, :ipw // 16], qrs[:, :ipw // 16], ipw)
            ptb = B.pt.nxt()
            op_multi(p, "pe",
                     [(lambda e, c=c, ptb=ptb:
                       e.transpose(ptb[:, c * P:(c + 1) * P],
                                   qa[:, c * P:(c + 1) * P], B.ident[:]))
                      for c in range(nrc)],
                     reads=[qa, B.ident[:]], writes=[ptb[:]])
            for c in range(nrc):
                op_copy(p, B.A2[:, WDOF + c * 1024 + r * P:
                                WDOF + c * 1024 + (r + 1) * P],
                        ptb[:, c * P:(c + 1) * P],
                        eng="act" if c % 2 else "dve")
            p.barrier()
        for i0 in range(0, NT, 1):
            xres_t = {}
            for i in range(i0, i0 + 1):
                xres_t[i] = B.ft.nxt()
                p.dma(xres_t[i][:],
                      (xin if first and i0p == 0 else xout)
                      [i * P:(i + 1) * P, :])
            for i in range(i0, i0 + 1):
                evs = {}
                for (aofs, nm) in ((0, "g"), (8 * ipw, "u")):
                    pss = []
                    for n0 in range(0, ipw, 512):
                        nw = min(512, ipw - n0)
                        ps = B.pb.nxt()
                        p.mm_group(
                            [(ps[:, :nw], xt_chunk(B.XT, c, i * P, P),
                              B.A2[:, aofs + c * ipw + n0:
                                   aofs + c * ipw + n0 + nw],
                              c == 0, c == 7, None) for c in range(8)],
                            reads=[B.XT[:], B.A2[:]], writes=[ps[:]])
                        pss.append((ps, n0, nw))
                    evs[nm] = pss
                fg = B.ft.nxt_not(list(xres_t.values()))
                fu = B.ft.nxt_not(list(xres_t.values()) + [fg])
                for (ps, n0, nw) in evs["g"]:
                    op_act(p, fg[:, n0:n0 + nw], ps[:, :nw], AF.Silu)
                for (ps, n0, nw) in evs["u"]:
                    op_copy(p, fu[:, n0:n0 + nw], ps[:, :nw], eng="act")
                op_tt(p, fg[:, :ipw], fg[:, :ipw], fu[:, :ipw], ALU.mult)
                w3 = B.ft.nxt_not(list(xres_t.values()) + [fg, fu])
                emit_quant(p, B, qa[:, :ipw], fg[:, :ipw], fu[:, :ipw],
                           w3[:, :ipw], qb[:, :ipw], qsc[:, :ipw // 16],
                           qrs[:, :ipw // 16], ipw)
                ptb = B.pt.nxt()
                op_multi(p, "pe",
                         [(lambda e, c=c, ptb=ptb:
                           e.transpose(ptb[:, c * P:(c + 1) * P],
                                       qa[:, c * P:(c + 1) * P], B.ident[:]))
                          for c in range(nrc)],
                         reads=[qa, B.ident[:]], writes=[ptb[:]])
                op_copy(p, hqt[:, :nrc * P],
                        ptb[:, :nrc * P])
                xres = xres_t[i]
                for j in (0, 512):
                    ps = B.pb.nxt()
                    p.mm_group(
                        [(ps[:, :512], hqt[:, c * P:(c + 1) * P],
                          B.A2[:, WDOF + c * 1024 + j:
                               WDOF + c * 1024 + j + 512],
                          c == 0, c == nrc - 1, None) for c in range(nrc)],
                        reads=[hqt, B.A2[:]], writes=[ps[:]])
                    op_tt(p, xres[:, j:j + 512], xres[:, j:j + 512],
                          ps[:, :512], ALU.add)
                p.dma(xout[i * P:(i + 1) * P, :], xres[:], q="pool")
            p.barrier()



def emit_l0ffn(p, B, xin, xout, first):
    v2 = p._sbufs.get("l0_v2") or p.sbuf("l0_v2", [P, 2 * 32 * 65], BF16)
    e8c = p.sbuf("moe_e8", [8, 512], BF16)
    cf8b = B.sh1[0:8, :]
    p.dma(e8c[:], B.din("c_e8"))
    p.barrier()
    emit_load_wT(p, B, "w_l0_w1aT", B.A3, 0, D, 512)
    for erc in range(4):
        for tq in range(8):
            ps = B.pb.nxt()
            p.mm_group([(ps[:, :512],
                         B.A3[:, c * 512 + erc * P: c * 512 + (erc + 1) * P],
                         xt_chunk(B.XT, c, tq * 512, 512),
                         c == 0, c == 7, None) for c in range(8)],
                       reads=[B.A3[:], B.XT[:]], writes=[ps[:]])
            op_copy(p, B.A2[:, erc * T + tq * 512: erc * T + (tq + 1) * 512],
                    ps[:, :512], eng="act")
        p.barrier()
    w1b = B.din("w_l0_w1b")
    w2aT = B.din("w_l0_w2aT")
    w2b = B.din("w_l0_w2b")
    for pr in range(4):
        for piece in range(4):
            stg = B.ft.nxt()
            p.dma(stg[:], w1b[pr * P:(pr + 1) * P,
                              piece * 1024:(piece + 1) * 1024],
                  q="sp" if piece % 2 else "pool")
            op_copy(p, B.A2[:, 16384 + pr * 4096 + piece * 1024:
                            16384 + pr * 4096 + (piece + 1) * 1024],
                    stg[:], eng="dve" if piece % 2 else "act")
            if piece % 2 == 1:
                p.barrier()
    p.barrier()
    w2aT_r = w2aT.rearrange("(f p) e -> p f e", p=P)
    for pr in range(4):
        for fg in range(4):
            stg = B.ft.nxt()
            p.dma(stg[:].rearrange("p (f e) -> p f e", e=128),
                  w2aT_r[:, fg * 8:(fg + 1) * 8, pr * 128:(pr + 1) * 128],
                  q="sp" if fg % 2 else "pool")
            op_copy(p, B.XT[:, pr * 4096 + fg * 1024:
                            pr * 4096 + (fg + 1) * 1024],
                    stg[:], eng="dve" if fg % 2 else "act")
            if fg % 2 == 1:
                p.barrier()
    for ee in range(8):
        stg = B.ft.nxt()
        p.dma(stg[0:64, :], w2b[ee * 64:(ee + 1) * 64, :],
              q="sp" if ee % 2 else "pool")
        op_copy(p, B.XT[0:64, 16384 + ee * 1024: 16384 + (ee + 1) * 1024],
                stg[0:64, :], eng="act" if ee % 2 else "dve")
        if ee % 2 == 1:
            p.barrier()
    p.barrier()
    coeffT = B.din("l0_coeffT")
    for tq in range(8):
        xres_t = {}
        for ts in range(4):
            xres_t[ts] = B.ft.items[ts]
            p.dma(xres_t[ts][:],
                  (xin if first else xout)[(tq * 4 + ts) * P:
                                           (tq * 4 + ts + 1) * P, :])
        cf8f = B.fs.nxt()
        p.dma(cf8f[0:8, :], coeffT[:, tq * 512:(tq + 1) * 512], q="pool")
        op_copy(p, cf8b, cf8f[0:8, :])
        for pr in range(4):
            psH3A = B.pb.items[2]; psH3B = B.pb.items[3]
            psH2 = [(B.pb.items[0], B.pb.items[1]),
                    (B.pb.items[4], B.pb.items[5])]

            def emit_H2(f):
                pA, pB = psH2[f % 2]
                p.mm_group(
                    [(pA[:, :512],
                      B.A2[0:64, 16384 + pr * 4096 + f * 128:
                           16384 + pr * 4096 + f * 128 + 128],
                      B.A2[0:64, pr * T + tq * 512: pr * T + (tq + 1) * 512],
                      True, True, (0, 0)),
                     (pB[:, :512],
                      B.A2[64:128, 16384 + pr * 4096 + f * 128:
                           16384 + pr * 4096 + f * 128 + 128],
                      B.A2[64:128, pr * T + tq * 512: pr * T + (tq + 1) * 512],
                      True, True, (64, 0))],
                    reads=[B.A2[:]], writes=[pA[:], pB[:]])
            emit_H2(0)
            emit_H2(1)
            for f in range(32):
                pA, pB = psH2[f % 2]
                fsA = B.fs.items[(2 * f) % 4]
                fsB = B.fs.items[(2 * f + 1) % 4]
                op_copy(p, fsA[:, :512], pA[:, :512], eng="dve")
                op_copy(p, fsB[:, :512], pB[:, :512], eng="dve")
                gx = B.xb.items[f % 3]
                ga = gx[:, :512]
                gb = gx[:, 512:1024]
                op_act(p, ga, fsA[:, :512], AF.Gelu)
                op_act(p, gb, fsB[:, :512], AF.Gelu)
                if f + 2 < 32:
                    emit_H2(f + 2)
                p.mm_group([(psH3A[0:64, :512],
                             B.XT[:, pr * 4096 + f * 128: pr * 4096 + f * 128 + 64],
                             ga, f == 0, f == 31, None)],
                           reads=[B.XT[:], gx[:]], writes=[psH3A[:]])
                p.mm_group([(psH3B[0:64, :512],
                             B.XT[:, pr * 4096 + f * 128 + 64:
                                  pr * 4096 + f * 128 + 128],
                             gb, f == 0, f == 31, None)],
                           reads=[B.XT[:], gx[:]], writes=[psH3B[:]])
            psBC = B.pb.items[4]
            rotY = Rot([B.pb.items[5], B.pb.items[0], B.pb.items[1]])
            for (ee, psH3) in ((2 * pr, psH3A), (2 * pr + 1, psH3B)):
                vof = 1024 * (ee % 2)
                p.mm_group([(psBC[0:64, :512],
                             e8c[:, ee * 64:(ee + 1) * 64],
                             cf8b, True, True, None)],
                           reads=[e8c[:], B.sh1[:]], writes=[psBC[:]])
                op_copy(p, v2[0:64, vof:vof + 512], psBC[0:64, :512],
                        eng="act")
                p.op("dve",
                     lambda e, psH3=psH3, vof=vof:
                     e.tensor_tensor(v2[0:64, vof + 512:vof + 1024],
                                     psH3[0:64, :512], v2[0:64, vof:vof + 512],
                                     ALU.mult),
                     reads=[psH3[:], v2[:]], writes=[v2[:]])
                for ts in range(4):
                    for dc in (0, 512):
                        psY = rotY.nxt()
                        p.mm_group(
                            [(psY[:, :512],
                              v2[0:64, vof + 512 + ts * 128:
                                 vof + 512 + (ts + 1) * 128],
                              B.XT[0:64, 16384 + ee * 1024 + dc:
                                   16384 + ee * 1024 + dc + 512],
                              True, True, None)],
                            reads=[v2[:], B.XT[:]], writes=[psY[:]])
                        op_tt(p, xres_t[ts][:, dc:dc + 512],
                              xres_t[ts][:, dc:dc + 512], psY[:, :512],
                              ALU.add)
        for ts in range(4):
            p.dma(xout[(tq * 4 + ts) * P:(tq * 4 + ts + 1) * P, :],
                  xres_t[ts][:], q="pool")
        p.barrier()



def emit_l3attn(p, B, xin, xout, first):
    lam = B.sh1
    wdl = p.sbuf("l3_wdl", [P, 96], BF16)
    wdt = p.sbuf("l3_wdt", [P, 128], BF16)
    bdc = p.sbuf("l3_bdc", [P, 33])
    bd = bdc[:, 0:28]
    bcc = bdc[:, 28:32]
    l3sc = p.sbuf("l3_sc", [1, 24])
    carry = l3sc[0:1, 0:16]
    csav = l3sc[0:1, 16:18]
    nrf = l3sc[0:1, 18:22]
    f1r = p._sbufs.get("l0_lb16") or p.sbuf("l0_lb16", [1, 512], BF16)
    f2r = f1r
    g10s = p.sbuf("l3_g10s", [64, 512], BF16)
    g11s = p.sbuf("l3_g11s", [64, 512], BF16)
    onecol = bdc[:, 32:33]
    identf = p.sbuf("c_identf_t", [P, P])
    onesrf = p.sbuf("c_onesrf_t", [1, P])
    trilneg = p.sbuf("c_trilneg_t", [P, P], BF16)
    triu1f = p.sbuf("c_triu1f_t", [P, P])
    ehs = p.sbuf("c_ehs_t", [8, P], BF16)
    had = p.sbuf("c_had_t", [P, P], BF16)
    e12 = p.sbuf("c_e12_t", [12, 256], BF16)
    v2 = p._sbufs.get("l0_v2") or p.sbuf("l0_v2", [P, 2 * 32 * 65], BF16)
    onesr = p._sbufs.get("l0_onesr") or p.sbuf("l0_onesr", [1, P], BF16)
    lrow = p._sbufs.get("l0_lrow") or p.sbuf("l0_lrow", [1, 512])
    for (t, nm) in ((identf, "c_identf"), (onesrf, "c_onesrf"),
                    (trilneg, "c_trilneg"), (triu1f, "c_triu1f"),
                    (ehs, "c_ehs"), (had, "c_had"), (e12, "c_e12")):
        p.dma(t[:], B.din(nm))
    p.dma(onesr[:], B.din("c_onesr"), q="pool")
    op_memset(p, onecol, 1.0)
    p.barrier()
    emit_load_wT(p, B, "w_l3_dtT", wdt, 0, D, 16, nstride=16)
    arow = B.fs.nxt()
    p.dma(arow[0:1, 0:16], B.din("w_l3_alogR"))
    dbr = B.fs.nxt()
    p.dma(dbr[0:1, 0:16], B.din("w_l3_dtbR"))
    op_act(p, arow[0:1, 0:16], arow[0:1, 0:16], AF.Exp)
    op_ts(p, arow[0:1, 0:16], arow[0:1, 0:16], -1.0, ALU.mult)
    abc = B.fs.nxt(); dbc = B.fs.nxt()
    ps = B.pb.items[5]
    p.mm_group([(ps[:, 0:16], onesrf[:], arow[0:1, 0:16], True, True, None)],
               reads=[onesrf[:], arow[:]], writes=[ps[:]])
    op_copy(p, abc[:, 0:16], ps[:, 0:16], eng="act")
    p.mm_group([(ps[:, 0:16], onesrf[:], dbr[0:1, 0:16], True, True, None)],
               reads=[onesrf[:], dbr[:]], writes=[ps[:]])
    op_copy(p, dbc[:, 0:16], ps[:, 0:16], eng="act")
    op_memset(p, carry, 0.0)
    p.barrier()
    for i in range(NT):
        ps = B.pb.items[5]
        p.mm_group([(ps[:, 0:16], xt_chunk(B.XT, c, i * P, P),
                     wdt[:, c * 16:(c + 1) * 16], c == 0, c == 7, None)
                    for c in range(8)], reads=[wdt[:], B.XT[:]],
                   writes=[ps[:]])
        t0f = B.fs.nxt_not([abc, dbc])
        t1f = B.fs.nxt_not([abc, dbc, t0f])
        op_tt(p, t0f[:, 0:16], ps[:, 0:16], dbc[:, 0:16], ALU.add)
        op_act(p, t1f[:, 0:16], t0f[:, 0:16], AF.Abs)
        op_ts(p, t1f[:, 0:16], t1f[:, 0:16], -1.0, ALU.mult)
        op_act(p, t1f[:, 0:16], t1f[:, 0:16], AF.Exp)
        op_act(p, t1f[:, 0:16], t1f[:, 0:16], AF.Ln, bias=onecol)
        op_act(p, t0f[:, 0:16], t0f[:, 0:16], AF.Relu)
        op_tt(p, t0f[:, 0:16], t0f[:, 0:16], t1f[:, 0:16], ALU.add)
        op_tt(p, t0f[:, 0:16], t0f[:, 0:16], abc[:, 0:16], ALU.mult)
        pc = B.pb.items[4]
        p.mm_group([(pc[:, 0:16], triu1f[:], t0f[:, 0:16], True, False, None),
                    (pc[:, 0:16], onesrf[:], carry, False, True,
                     None)],
                   reads=[triu1f[:], t0f[:], onesrf[:], l3sc[:]],
                   writes=[pc[:]])
        op_copy(p, B.g1[:, i * 16:(i + 1) * 16], pc[:, 0:16])
        p.mm_group([(ps[0:1, 0:16], onecol[:], t0f[:, 0:16], True, True,
                     None)], reads=[onecol[:], t0f[:]], writes=[ps[:]])
        p.op("dve", lambda e, ps=ps: e.tensor_tensor(
            carry, carry, ps[0:1, 0:16], ALU.add),
            reads=[l3sc[:], ps[:]], writes=[l3sc[:]])
        if i % 4 == 3:
            p.barrier()
    p.barrier()
    G8, G9, G10, G11, Gd, WRK = (B.pb.items[0], B.pb.items[1], B.pb.items[2],
                                 B.pb.items[3], B.pb.items[4], B.pb.items[5])
    lpall = p.sbuf("l3_lpall", [12, 16])
    p.dma(lpall[:], B.din("w_l3_lpT"))
    p.barrier()
    lpc = B.fs.nxt()
    for h in range(16):
        wq_t = B.xb.items[0]; wk_t = B.xb.items[1]; wv_t = B.xb.items[2]
        for (wname, wt) in (("w_l3_qT", wq_t), ("w_l3_kT", wk_t),
                            ("w_l3_vT", wv_t)):
            for g4 in range(2):
                stgs = []
                for c in range(4 * g4, 4 * g4 + 4):
                    stg = B.ft.nxt()
                    stgs.append((c, stg))
                    p.dma(stg[:, :64],
                          B.din(wname)[c * P:(c + 1) * P,
                                       h * 64:(h + 1) * 64],
                          q="sp" if c % 2 else "pool")
                for c, stg in stgs:
                    op_copy(p, wt[:, c * 64:(c + 1) * 64], stg[:, :64],
                            eng="dve" if c % 2 else "act")
                p.barrier()
        for (wt, dst_ofs) in ((wq_t, T), (wk_t, 0)):
            for tq in range(8):
                wb = WRK if tq % 2 == 0 else Gd
                p.mm_group([(wb[0:64, :512], wt[:, c * 64:(c + 1) * 64],
                             xt_chunk(B.XT, c, tq * 512, 512),
                             c == 0, c == 7, None) for c in range(8)],
                           reads=[wt[:], B.XT[:]], writes=[wb[:]])
                op_copy(p, B.A3[0:64, dst_ofs + tq * 512:
                                dst_ofs + (tq + 1) * 512],
                        wb[0:64, :512], eng="act")
        for tq in range(8):
            wb = WRK if tq % 2 == 0 else Gd
            p.mm_group([(wb[0:64, :512], wv_t[:, c * 64:(c + 1) * 64],
                         xt_chunk(B.XT, c, tq * 512, 512),
                         c == 0, c == 7, None) for c in range(8)],
                       reads=[wv_t[:], B.XT[:]], writes=[wb[:]])
            vtb = B.xb.items[0] if tq % 2 == 0 else B.xb.items[1]
            op_copy(p, vtb[0:64, :512], wb[0:64, :512])
            for q4 in range(4):
                it = tq * 4 + q4
                ptb = B.pt.nxt()
                op_transpose(p, ptb[:, 0:64],
                             vtb[0:64, q4 * 128:(q4 + 1) * 128],
                             B.ident[0:64, 0:64])
                op_copy(p, v2[:, it * 64:(it + 1) * 64],
                        ptb[:, 0:64], eng="act")
        p.barrier()
        for g4 in range(2):
            stgs = []
            for c in range(4 * g4, 4 * g4 + 4):
                stg = B.ft.nxt()
                stgs.append((c, stg))
                p.dma(stg[:, :12],
                      B.din("w_l3_dlT")[c * P:(c + 1) * P,
                                        h * 12:(h + 1) * 12],
                      q="sp" if c % 2 else "pool")
            for c, stg in stgs:
                op_copy(p, wdl[:, c * 12:(c + 1) * 12], stg[:, :12],
                        eng="dve" if c % 2 else "act")
            p.barrier()
        op_memset(p, csav, 0.0)
        p.barrier()
        g1v3 = B.g1[:, 0:512].rearrange("p (i hh) -> p i hh", hh=16)
        for a in range(8):
            gfirst = {8: True, 9: True, 10: True, 11: True}
            glast = {}
            offlist = [(tk, min(10 + int(math.log2(a ^ (tk // 4))), 11))
                       for tk in range(4 * a)] if a > 0 else []
            for L in (10, 11):
                lst = [tk for tk, LL in offlist if LL == L]
                glast[L] = lst[-1] if lst else None
            inlist = {8: [(1, 0), (3, 2)], 9: [(2, 0), (3, 0), (2, 1), (3, 1)]}
            rotS = [Gd, G8, G9]

            def emit_S(tk, sb):
                p.mm_group([(sb[:, :512],
                             B.A3[0:64, tk * 128:(tk + 1) * 128],
                             B.A3[0:64, T + a * 512: T + (a + 1) * 512],
                             True, True, None)],
                           reads=[B.A3[:]], writes=[sb[:]])
            p.mm_group([(WRK[0:12, :512], wdl[:, c * 12:(c + 1) * 12],
                         xt_chunk(B.XT, c, a * 512, 512),
                         c == 0, c == 7, None) for c in range(8)],
                       reads=[wdl[:], B.XT[:]], writes=[WRK[:]])
            for i3 in range(min(3, len(offlist))):
                emit_S(offlist[i3][0], rotS[i3 % 3])
            tA = B.fs.nxt_not([lpc]); tB = B.fs.nxt_not([lpc, tA])
            op_ts(p, tA[0:12, :], WRK[0:12, :512], lpall[:, h:h + 1], ALU.mult)
            op_act(p, tB[0:12, :], tA[0:12, :], AF.Abs)
            op_ts(p, tB[0:12, :], tB[0:12, :], -1.0, ALU.mult)
            op_act(p, tB[0:12, :], tB[0:12, :], AF.Exp)
            op_act(p, tB[0:12, :], tB[0:12, :], AF.Ln, bias=onecol[0:12, 0:1])
            op_act(p, tA[0:12, :], tA[0:12, :], AF.Relu)
            op_tt(p, tA[0:12, :], tA[0:12, :], tB[0:12, :], ALU.add)
            op_copy(p, lam[:, 0:512], tA[0:12, :])
            for j4 in range(4):
                p.op("pe", lambda e, j4=j4: e.transpose(
                    WRK[0:1, j4 * 128:(j4 + 1) * 128],
                    B.g1[:, (4 * a + j4) * 16 + h:(4 * a + j4) * 16 + h + 1],
                    identf[:]), reads=[B.g1[:], identf[:]], writes=[WRK[:]])
            op_copy(p, lrow[:, 0:512], WRK[0:1, 0:512], eng="act")
            op_ts(p, nrf[0:1, 0:1], csav[0:1, 0:1], -1.0, ALU.mult)
            for i in (1, 2, 3):
                op_ts(p, nrf[0:1, i:i + 1], lrow[:, i * 128 - 1:i * 128],
                      -1.0, ALU.mult)
            op_copy(p, csav[0:1, 0:1], lrow[:, 511:512])
            for i in range(4):
                p.mm_group([(WRK[:, i:i + 1], onesrf[:], nrf[0:1, i:i + 1],
                             i == 0, i == 3, None)],
                           reads=[onesrf[:], l3sc[:]], writes=[WRK[:]])
            op_ts(p, bcc, WRK[:, 0:4], -1.0, ALU.mult)
            if a > 0:
                cgall = g1v3[:, 0:4 * a, h]
                op_act(p, bd[:, 0:4 * a], cgall, AF.Exp, scale=-1.0,
                       bias=bcc[:, 0:1])
                op_ts(p, bd[:, 0:4 * a], bd[:, 0:4 * a], 0.125, ALU.mult)
            for idx, (tk, L) in enumerate(offlist):
                G = G10 if L == 10 else G11
                sb = rotS[idx % 3]
                pt_ = B.xb.items[(tk % 2)]
                if tk % 2:
                    op_ts(p, pt_[:, :512], sb[:, :512], bd[:, tk:tk + 1],
                          ALU.mult)
                else:
                    op_act(p, pt_[:, :512], sb[:, :512], AF.Copy,
                           scale=bd[:, tk:tk + 1])
                if idx + 3 < len(offlist):
                    emit_S(offlist[idx + 3][0], sb)
                p.mm_group([(G[0:64, :512], v2[:, tk * 64:(tk + 1) * 64],
                             pt_[:, :512], gfirst[L], glast[L] == tk, None)],
                           reads=[v2[:], pt_[:]], writes=[G[:]])
                gfirst[L] = False
            if glast[10] is not None:
                op_copy(p, g10s[0:64, :], G10[0:64, :512], eng="act")
            if glast[11] is not None:
                op_copy(p, g11s[0:64, :], G11[0:64, :512], eng="act")
            rW = Rot([WRK, G10, G11])
            for j in range(4):
                tk = 4 * a + j
                sb = rW.nxt()
                emit_S(tk, sb)
                sdg = B.xb.items[0]
                op_act(p, sdg[:, 0:128], sb[:, j * 128:(j + 1) * 128],
                       AF.Copy, scale=0.125)
                pcs = B.xb.items[1]
                for i in range(j + 1, 4):
                    col = B.col.nxt()
                    op_act(p, col[:], g1v3[:, 4 * a + j, h:h + 1], AF.Exp,
                           scale=-1.0, bias=bcc[:, i:i + 1])
                    op_ts(p, col[:], col[:], 0.125, ALU.mult)
                    if i % 2:
                        op_ts(p, pcs[:, (i - j - 1) * 128:(i - j) * 128],
                              sb[:, i * 128:(i + 1) * 128], col[:], ALU.mult)
                    else:
                        op_act(p, pcs[:, (i - j - 1) * 128:(i - j) * 128],
                               sb[:, i * 128:(i + 1) * 128], AF.Copy,
                               scale=col[:])
                db = rW.nxt()
                p.mm_group([(db[:, :128], onesrf[:],
                             lrow[:, j * 128:(j + 1) * 128], True, True,
                             None)], reads=[onesrf[:], lrow[:]],
                           writes=[db[:]])
                dxf = B.fs.nxt_not([lpc])
                op_ts(p, dxf[:, :128], db[:, :128],
                      g1v3[:, 4 * a + j, h:h + 1], ALU.subtract)
                op_tt(p, dxf[:, :128], dxf[:, :128], trilneg[:], ALU.add)
                dex = B.xb.items[2]
                op_act(p, dex[:, :128], dxf[:, :128], AF.Exp)
                p.mm_group([(db[:, 128:256], ehs[:],
                             lam[0:8, j * 128:(j + 1) * 128], True, True,
                             None)], reads=[ehs[:], lam[:]], writes=[db[:]])
                op_tt(p, dex[:, 128:256], db[:, 128:256], had[:], ALU.mult)
                p.mm_group([(db[:, 256:384], had[:], dex[:, 128:256],
                             True, True, None)],
                           reads=[had[:], dex[:]], writes=[db[:]])
                op_tt(p, sdg[:, 128:256], sdg[:, 0:128], dex[:, :128],
                      ALU.mult)
                p.op("dve", lambda e, sdg=sdg, db=db: e.tensor_tensor(
                    sdg[:, 256:384], sdg[:, 128:256], db[:, 256:384],
                    ALU.mult), reads=[sdg[:], db[:]], writes=[sdg[:]])
                p.mm_group([(Gd[0:64, j * 128:(j + 1) * 128],
                             v2[:, tk * 64:(tk + 1) * 64], sdg[:, 256:384],
                             j == 0, j == 3, None)],
                           reads=[v2[:], sdg[:]], writes=[Gd[:]])
                for i in range(j + 1, 4):
                    L = 8 + int(math.log2(i ^ j))
                    G = G8 if L == 8 else G9
                    last = inlist[L][-1] == (i, j)
                    p.mm_group([(G[0:64, i * 128:(i + 1) * 128],
                                 v2[:, tk * 64:(tk + 1) * 64],
                                 pcs[:, (i - j - 1) * 128:(i - j) * 128],
                                 gfirst[L], last, None)],
                               reads=[v2[:], pcs[:]], writes=[G[:]])
                    gfirst[L] = False
            acc = B.fs.nxt_not([lpc])
            op_copy(p, acc[0:64, :], Gd[0:64, :512], eng="act")
            tmp = B.fs.nxt_not([lpc, acc])
            fbc = B.fs.nxt_not([lpc, acc, tmp])
            if a > 0:
                op_act(p, f1r[:, :512], lrow[:, :512], AF.Exp,
                       bias=nrf[0:1, 0:1])
                cb = rW.nxt()
                p.mm_group([(cb[:, :512], onesr[:], f1r[:, :512],
                             True, True, None)],
                           reads=[onesr[:], f1r[:]], writes=[cb[:]])
                op_copy(p, fbc[:, :512], cb[:, :512], eng="act")
                for (L, Gs) in ((10, g10s), (11, g11s)):
                    if glast[L] is None:
                        continue
                    cb = rW.nxt()
                    p.mm_group([(cb[0:64, :512],
                                 e12[:, (L - 8) * 64:(L - 8 + 1) * 64],
                                 lam[:, 0:512], True, True,
                                 None)], reads=[e12[:], lam[:]],
                               writes=[cb[:]])
                    op_tt(p, tmp[0:64, :], cb[0:64, :512], fbc[0:64, :512],
                          ALU.mult)
                    p.op("dve", lambda e, Gs=Gs, tmp=tmp: e.tensor_tensor(
                        tmp[0:64, :], tmp[0:64, :], Gs[0:64, :], ALU.mult),
                        reads=[tmp[:], Gs[:]], writes=[tmp[:]])
                    op_tt(p, acc[0:64, :], acc[0:64, :], tmp[0:64, :],
                          ALU.add)
            for i4 in range(4):
                op_act(p, f2r[:, i4 * 128:(i4 + 1) * 128],
                       lrow[:, i4 * 128:(i4 + 1) * 128], AF.Exp,
                       bias=nrf[0:1, i4:i4 + 1] if i4 > 0 else nrf[0:1, 0:1])
            cb = rW.nxt()
            p.mm_group([(cb[:, :512], onesr[:], f2r[:, :512], True, True,
                         None)], reads=[onesr[:], f2r[:]], writes=[cb[:]])
            op_copy(p, fbc[:, :512], cb[:, :512], eng="act")
            for (L, G, ranges) in ((8, G8, [(128, 256), (384, 512)]),
                                   (9, G9, [(256, 512)])):
                for (lo, hi) in ranges:
                    cb = rW.nxt()
                    p.mm_group([(cb[0:64, lo:hi],
                                 e12[:, (L - 8) * 64:(L - 8 + 1) * 64],
                                 lam[:, lo:hi],
                                 True, True, None)],
                               reads=[e12[:], lam[:]], writes=[cb[:]])
                    op_tt(p, tmp[0:64, lo:hi], cb[0:64, lo:hi],
                          fbc[0:64, lo:hi], ALU.mult)
                    p.op("dve", lambda e, G=G, tmp=tmp, lo=lo, hi=hi:
                         e.tensor_tensor(tmp[0:64, lo:hi], tmp[0:64, lo:hi],
                                         G[0:64, lo:hi], ALU.mult),
                         reads=[tmp[:], G[:]], writes=[tmp[:]])
                    op_tt(p, acc[0:64, lo:hi], acc[0:64, lo:hi],
                          tmp[0:64, lo:hi], ALU.add)
            ro = (h % 2) * 64
            cch = h // 2
            op_copy(p, B.A2[ro:ro + 64, cch * T + a * 512:
                            cch * T + (a + 1) * 512], acc[0:64, :512],
                    eng="act")
            p.barrier()
    emit_load_wT(p, B, "w_l3_outT", B.A3, 0, D, D)
    gr = B.ft.nxt()
    p.dma(gr[0:1, :], B.din("w_l3_lng"))
    br_ = B.ft.nxt_not([gr])
    p.dma(br_[0:1, :], B.din("w_l3_lnb"))
    grow = p._sbufs.get("l1_grow") or p.sbuf("l1_grow", [1, D], BF16)
    op_copy(p, grow[:], gr[0:1, :])
    p.barrier()
    for j in (0, 512):
        psb = B.pb.nxt()
        p.mm_group([(psb[:, :512], onesr[:], grow[:, j:j + 512],
                     True, True, None)], reads=[onesr[:], grow[:]],
                   writes=[psb[:]])
        op_copy(p, B.gbc[:, j:j + 512], psb[:, :512], eng="act")
    op_copy(p, grow[:], br_[0:1, :])
    p.barrier()
    for j in (0, 512):
        psb = B.pb.nxt()
        p.mm_group([(psb[:, :512], onesr[:], grow[:, j:j + 512],
                     True, True, None)], reads=[onesr[:], grow[:]],
                   writes=[psb[:]])
        op_copy(p, B.bbc[:, j:j + 512], psb[:, :512], eng="act")
    p.barrier()
    for i in range(NT):
        xi = B.ft.nxt()
        p.dma(xi[:], xin[i * P:(i + 1) * P, :])
        if not first:
            xo = B.ft.nxt_not([xi])
            p.dma(xo[:], xout[i * P:(i + 1) * P, :], q="pool")
        else:
            xo = B.ft.nxt_not([xi])
            p.dma(xo[:], xin[i * P:(i + 1) * P, :], q="pool")
        o = B.ft.nxt_not([xi, xo])
        for j in (0, 512):
            ps = B.pb.nxt()
            p.mm_group([(ps[:, :512], xt_chunk(B.A2, c, i * P, P),
                         B.A3[:, c * 1024 + j: c * 1024 + j + 512],
                         c == 0, c == 7, None) for c in range(8)],
                       reads=[B.A2[:], B.A3[:]], writes=[ps[:]])
            op_copy(p, o[:, j:j + 512], ps[:, :512])
        m = B.col.nxt(); ss = B.col.nxt(); rs = B.col.nxt()
        op_red(p, m[:], xi[:], ALU.add)
        op_ts(p, m[:], m[:], 1.0 / D, ALU.mult)
        op_ts(p, xi[:], xi[:], m[:], ALU.subtract)
        sq = B.fs.nxt_not([lpc])
        op_act(p, sq[:, :512], xi[:, :512], AF.Square, accum_out=ss[:])
        ss2 = B.col.nxt()
        op_act(p, sq[:, :512], xi[:, 512:], AF.Square, accum_out=ss2[:])
        op_tt(p, ss[:], ss[:], ss2[:], ALU.add)
        op_ts(p, ss[:], ss[:], 1.0 / D, ALU.mult)
        op_act(p, ss[:], ss[:], AF.Ln, bias=B.epscol[:])
        op_act(p, rs[:], ss[:], AF.Exp, scale=-0.5)
        op_ts(p, xi[:], xi[:], rs[:], ALU.mult)
        op_tt(p, o[:], o[:], xi[:], ALU.add)
        m2 = B.col.nxt(); st = B.col.nxt(); r2 = B.col.nxt()
        op_red(p, m2[:], o[:], ALU.add)
        op_ts(p, m2[:], m2[:], 1.0 / D, ALU.mult)
        op_ts(p, o[:], o[:], m2[:], ALU.subtract)
        op_act(p, sq[:, :512], o[:, :512], AF.Square, accum_out=st[:])
        st2 = B.col.nxt()
        op_act(p, sq[:, :512], o[:, 512:], AF.Square, accum_out=st2[:])
        op_tt(p, st[:], st[:], st2[:], ALU.add)
        op_ts(p, st[:], st[:], 1.0 / D, ALU.mult)
        op_act(p, st[:], st[:], AF.Ln, bias=B.epscol[:])
        op_act(p, r2[:], st[:], AF.Exp, scale=-0.5)
        op_ts(p, o[:], o[:], r2[:], ALU.mult)
        op_tt(p, o[:], o[:], B.gbc[:], ALU.mult)
        op_tt(p, o[:], o[:], B.bbc[:], ALU.add)
        op_tt(p, o[:], o[:], xo[:], ALU.add)
        p.dma(xout[i * P:(i + 1) * P, :], o[:], q="pool")
        p.barrier()
    p.barrier()


# version 11
# speedup vs baseline: 1.0132x; 1.0132x over previous
import math
from contextlib import ExitStack

import numpy as np
import ml_dtypes

import concourse.bass as bass
import concourse.mybir as mybir
from concourse.bass_utils import run_bass_kernel_spmd

F32 = mybir.dt.float32
BF16 = mybir.dt.bfloat16
I32 = mybir.dt.int32
AF = mybir.ActivationFunctionType
ALU = mybir.AluOpType
AX = mybir.AxisListType

NCORES = 8
P = 128
T = 4096
D = 1024
NT = T // P

bf16 = ml_dtypes.bfloat16



PE_SELF_SKIP = True
ENGINES = ("pe", "act", "dve", "pool", "sp")
_ENG_ATTR = {"pe": "tensor", "act": "scalar", "dve": "vector",
             "pool": "gpsimd", "sp": "sync"}


class Prog:
    def __init__(self, nc):
        self.nc = nc
        self.es = ExitStack()
        self.sem = {}
        for e in ENGINES:
            self.sem[e] = self.es.enter_context(nc.semaphore(f"sem_{e}"))
        self.dsem = {}
        for q in ("sp", "pool", "act", "dve"):
            self.dsem[q + ".ld"] = self.es.enter_context(
                nc.semaphore(f"dsem_{q}_ld"))
            self.dsem[q + ".st"] = self.es.enter_context(
                nc.semaphore(f"dsem_{q}_st"))
        self.cnt = {e: 0 for e in ENGINES}
        self.dcnt = {q: 0 for q in self.dsem}
        self.last_w = {}
        self.readers = {}
        self.phase = 0
        self.ops = []
        self.phase_dma = {q: 0 for q in self.dsem}
        self._sbufs = {}
        self._psums = {}
        self._frees = []

    def sbuf(self, name, shape, dt=F32):
        t = self.es.enter_context(self.nc.sbuf_tensor(name, shape, dt))
        self._sbufs[name] = t
        return t

    def psum(self, name, shape, dt=F32):
        bank_elems = 512 if dt == F32 else 1024
        assert len(shape) == 2 and shape[0] == 128 and shape[1] <= bank_elems, shape
        t = self.es.enter_context(self.nc.psum_tensor(name, [128, bank_elems],
                                                      dt))
        self._psums[name] = t
        return t

    def dram(self, name, shape, dt=F32, kind=None):
        if kind is None:
            h = self.nc.dram_tensor(name, shape, dt)
        else:
            h = self.nc.dram_tensor(name, shape, dt, kind=kind)
        return h.ap()

    @staticmethod
    def _key(ap):
        t = ap.tensor if hasattr(ap, "tensor") else ap
        return t.name

    def _waits_for(self, eng, reads, writes, is_dma=False):
        waits = {}

        def add_wait(tag):
            kind, name, seq, ph = tag
            if ph < self.phase:
                return
            if PE_SELF_SKIP and kind == "eng" and name == "pe" \
                    and eng == "pe":
                return
            if kind == "eng":
                waits[("e", name)] = max(waits.get(("e", name), 0), seq)
            else:
                waits[("d", name)] = max(waits.get(("d", name), 0),
                                         self.dcnt[name])

        for ap in reads:
            k = self._key(ap)
            w = self.last_w.get(k)
            if w is not None:
                add_wait(w)
        for ap in writes:
            k = self._key(ap)
            w = self.last_w.get(k)
            if w is not None:
                add_wait(w)
            for (kind, name), (seq, ph) in self.readers.get(k, {}).items():
                add_wait((kind, name, seq, ph))
        return waits

    def _record(self, eng_tag, reads, writes):
        for ap in reads:
            k = self._key(ap)
            d = self.readers.setdefault(k, {})
            kk = (eng_tag[0], eng_tag[1])
            old = d.get(kk)
            if old is None or old[0] < eng_tag[2]:
                d[kk] = (eng_tag[2], self.phase)
        for ap in writes:
            k = self._key(ap)
            self.last_w[k] = eng_tag + (self.phase,)
            self.readers[k] = {}

    def op(self, eng, fn, reads=(), writes=(), inc_extra=0):
        waits = self._waits_for(eng, reads, writes)
        self.cnt[eng] += 1
        seq = self.cnt[eng]
        self.ops.append((eng, waits, fn, seq))
        self._record(("eng", eng, seq), reads, writes)
        return seq

    def mm_group(self, mms, reads=(), writes=()):
        waits = self._waits_for("pe", reads, writes)
        self.cnt["pe"] += 1
        seq = self.cnt["pe"]
        def fn(e):
            last = None
            for (ps, lhsT, rhs, st, sp, tp) in mms:
                last = e.matmul(ps, lhsT, rhs, start=st, stop=sp,
                                tile_position=tp, skip_group_check=True)
            return last
        self.ops.append(("pe", waits, fn, seq))
        self._record(("eng", "pe", seq), reads, writes)
        return seq

    def dma(self, dst, src, q="sp", no_track=False):
        eng = q
        is_store = "Dram" in type(dst.tensor).__name__ or \
                   getattr(dst.tensor, "base_partition", None) is None
        qs = q + (".st" if is_store else ".ld")
        waits = self._waits_for(eng, [src], [] if no_track else [dst])
        self.dcnt[qs] += 1
        self.phase_dma[qs] += 1
        seq = self.dcnt[qs]
        def fn(e):
            return e.dma_start(dst, src)
        self.ops.append((eng, waits, fn, ("dma", qs, seq)))
        self._record(("dma", qs, seq), [src], [] if no_track else [dst])
        return seq

    def mark_load(self, dst, q):
        qs = q + ".ld"
        self.last_w[self._key(dst)] = ("dma", qs, self.dcnt[qs], self.phase)
        self.readers[self._key(dst)] = {}

    def barrier(self):
        per_eng = {e: [] for e in ENGINES}
        for (eng, waits, fn, seq) in self.ops:
            per_eng[eng].append((waits, fn, seq))
        for qs, n in self.phase_dma.items():
            if n:
                per_eng[qs.split(".")[0]].append(
                    ({("d", qs): self.dcnt[qs]}, None, None))
        with self.nc.Block() as blk:
            for eng in ENGINES:
                lst = per_eng[eng]
                if not lst:
                    continue
                dec = getattr(blk, _ENG_ATTR[eng])
                def make(lst=lst, eng=eng):
                    def body(e):
                        for (waits, fn, seq) in lst:
                            for (kind, name), v in waits.items():
                                if kind == "e":
                                    e.wait_ge(self.sem[name], v)
                                else:
                                    e.wait_ge(self.dsem[name], 16 * v)
                            if fn is None:
                                continue
                            inst = fn(e)
                            if isinstance(seq, tuple):
                                inst.then_inc(self.dsem[seq[1]], 16)
                            else:
                                inst.then_inc(self.sem[eng], 1)
                    return body
                dec(make())
        self.ops = []
        self.phase_dma = {q: 0 for q in self.dsem}
        self.phase += 1

    def close(self):
        assert not self.ops, "un-flushed ops; call barrier()"
        self.es.close()



def op_tt(p, out, a, b, alu, eng="dve"):
    p.op(eng, lambda e: e.tensor_tensor(out, a, b, alu),
         reads=[a, b], writes=[out])


def op_ts(p, out, a, s1, alu, s2=None, alu2=None, eng="dve"):
    rd = [a] + ([s1] if hasattr(s1, "tensor") else []) \
             + ([s2] if hasattr(s2, "tensor") else [])
    if alu2 is None:
        p.op(eng, lambda e: e.tensor_scalar(out, a, s1, None, alu),
             reads=rd, writes=[out])
    else:
        p.op(eng, lambda e: e.tensor_scalar(out, a, s1, s2, alu, alu2),
             reads=rd, writes=[out])


def op_stt(p, out, in0, scal, in1, alu0, alu1, eng="dve"):
    rd = [in0, in1] + ([scal] if hasattr(scal, "tensor") else [])
    p.op(eng, lambda e: e.scalar_tensor_tensor(out, in0, scal, in1, alu0, alu1),
         reads=rd, writes=[out])


def op_copy(p, out, a, eng="dve"):
    if eng == "act":
        p.op(eng, lambda e: e.activation(out, a, AF.Copy),
             reads=[a], writes=[out])
    else:
        p.op(eng, lambda e: e.tensor_copy(out, a), reads=[a], writes=[out])


def op_act(p, out, a, func, bias=None, scale=None, eng="act", accum_out=None):
    kw = {}
    rd = [a]
    if bias is not None:
        kw["bias"] = bias
        if hasattr(bias, "tensor"):
            rd.append(bias)
    if scale is not None:
        kw["scale"] = scale
        if hasattr(scale, "tensor"):
            rd.append(scale)
    wr = [out]
    if accum_out is not None:
        kw["accum_out"] = accum_out
        wr = [out, accum_out]
    p.op(eng, lambda e: e.activation(out, a, func, **kw), reads=rd, writes=wr)


def op_red(p, out, a, alu, absval=False, negate=None, eng="dve"):
    kw = {}
    if absval:
        kw["apply_absolute_value"] = True
    if negate is not None:
        kw["negate"] = negate
    p.op(eng, lambda e: e.tensor_reduce(out, a, AX.X, alu, **kw),
         reads=[a], writes=[out])


def op_scan(p, out, a_coef, b, init, eng="dve"):
    rd = [a_coef, b] + ([init] if hasattr(init, "tensor") else [])
    p.op(eng, lambda e: e.tensor_tensor_scan(out, a_coef, b, init,
                                             ALU.mult, ALU.add),
         reads=rd, writes=[out])


def op_memset(p, ap, v, eng="pool"):
    p.op(eng, lambda e: e.memset(ap, v), reads=[], writes=[ap])


def op_transpose(p, out_psum, in_sbuf, ident, eng="pe"):
    p.op(eng, lambda e: e.transpose(out_psum, in_sbuf, ident),
         reads=[in_sbuf, ident], writes=[out_psum])


def ceil_div(a, b):
    return (a + b - 1) // b



def stage_inputs(inputs):
    f32 = np.float32
    g = inputs
    common = {}

    def _tr(a):
        return np.ascontiguousarray(np.asarray(a, f32).T)
    T_ = _tr

    def asf(a):
        return np.ascontiguousarray(np.asarray(a, f32))

    common["c_ident"] = np.eye(128, dtype=bf16)
    common["c_pihalf"] = np.full((128, 1), np.pi / 2, f32)
    cm = np.zeros((128, 896), bf16)
    for s in range(128):
        cm[s, s + 384:] = 1.0
    common["c_cmask"] = cm
    o2 = np.zeros((128, 65), bf16)
    o2[0:64, 0] = 1.0
    o2[64:128, 64] = 1.0
    common["c_ones2"] = o2
    common["c_onesr"] = np.ones((1, 128), bf16)
    e8 = np.zeros((8, 512), bf16)
    for e in range(8):
        e8[e, e * 64:(e + 1) * 64] = 1.0
    common["c_e8"] = e8
    common["c_identf"] = np.eye(128, dtype=f32)
    common["c_onesrf"] = np.ones((1, 128), f32)
    tri = np.fromfunction(lambda s, t: (s <= t), (128, 128))
    common["c_trilneg"] = np.where(tri, 0.0, -1e30).astype(bf16)
    common["c_triu1f"] = np.fromfunction(
        lambda ta, t: (ta <= t), (128, 128)).astype(f32)
    Hm = np.array([[1.0]])
    while Hm.shape[0] < 128:
        Hm = np.block([[Hm, Hm], [Hm, -Hm]])
    levs = np.zeros(128, np.int64)
    for u in range(1, 128):
        levs[u] = int(np.floor(np.log2(u))) + 1
    E = np.zeros((8, 128))
    for u in range(128):
        E[levs[u], u] = 1.0
    common["c_ehs"] = ((E @ Hm) / 128.0).astype(bf16)
    common["c_had"] = Hm.astype(bf16)
    e12 = np.zeros((12, 4 * 64), bf16)
    for k in range(4):
        e12[8 + k, k * 64:(k + 1) * 64] = 1.0
    common["c_e12"] = e12
    common["w_l3_lpT"] = np.ascontiguousarray(
        np.asarray(g["L3_attn_Lp"], f32).T)
    common["w_l3_alogR"] = asf(g["L3_attn_A_log"]).reshape(1, 16)
    common["w_l3_dtbR"] = asf(g["L3_attn_dt_bias"]).reshape(1, 16)

    common["w_l0_qT"] = _tr(g["L0_attn_Wq"])
    common["w_l0_kT"] = _tr(g["L0_attn_Wk"])
    common["w_l0_vT"] = _tr(g["L0_attn_Wv"])
    common["w_l0_oT"] = _tr(g["L0_attn_Wo"])
    common["w_l0_w1aT"] = _tr(g["L0_ffn_w1_A"])
    common["w_l0_w1b"] = asf(g["L0_ffn_w1_B"])
    common["w_l0_w2aT"] = _tr(g["L0_ffn_w2_A"])
    common["w_l0_w2b"] = asf(g["L0_ffn_w2_B"])
    common["w_l1_inw"] = asf(g["L1_attn_in_w"])
    common["w_l1_outw"] = asf(g["L1_attn_out_w"])
    common["w_l1_inb"] = np.ascontiguousarray(
        np.asarray(g["L1_attn_in_b"], f32).reshape(48, 128).T)
    common["w_l1_outb"] = asf(g["L1_attn_out_b"]).reshape(1, D)
    common["w_l1_gamma"] = asf(g["L1_attn_norm_gamma"]).reshape(1, D)
    common["w_l1_beta"] = asf(g["L1_attn_norm_beta"]).reshape(1, D)
    common["w_l1_convw"] = np.ascontiguousarray(
        np.asarray(g["L1_attn_conv_w"], f32).reshape(2048, 3)
        .reshape(16, 128, 3).transpose(1, 0, 2).reshape(128, 48))
    common["w_l1_convb"] = np.ascontiguousarray(
        np.asarray(g["L1_attn_conv_b"], f32).reshape(16, 128).T)
    def chan_major(a):
        return np.ascontiguousarray(
            np.asarray(a, f32).reshape(16, 128, 64).transpose(1, 0, 2)
            .reshape(128, 16 * 64))
    common["w_l1_alog"] = chan_major(g["L1_attn_A_log"])
    common["w_l1_bp"] = chan_major(g["L1_attn_B_p"])
    common["w_l1_cp"] = chan_major(g["L1_attn_C_p"])
    common["w_l1_dp"] = np.ascontiguousarray(
        np.asarray(g["L1_attn_D_p"], f32).reshape(16, 128).T)
    common["w_l1_theta"] = asf(g["L1_ffn_theta"])
    common["w_l1_phi"] = asf(g["L1_ffn_phi"])
    common["w_l1_thn"] = asf(g["L1_ffn_theta_noise"])
    common["w_l1_phn"] = asf(g["L1_ffn_phi_noise"])
    common["w_l2_qT"] = _tr(g["L2_attn_Wq"])
    common["w_l2_kT"] = _tr(g["L2_attn_Wk"])
    common["w_l2_vT"] = _tr(g["L2_attn_Wv"])
    common["w_l2_oT"] = _tr(g["L2_attn_Wo"])
    common["w_l2_bo"] = asf(g["L2_attn_bo"]).reshape(1, D)
    qsr = np.asarray(g["L2_attn_q_scale"], f32).reshape(16, 64)
    ksr = np.asarray(g["L2_attn_k_scale"], f32).reshape(16, 64)
    qs2 = np.zeros((128, 8), f32)
    ks2 = np.zeros((128, 8), f32)
    for i in range(8):
        qs2[0:64, i] = qsr[2 * i]
        qs2[64:128, i] = qsr[2 * i + 1]
        ks2[0:64, i] = ksr[2 * i]
        ks2[64:128, i] = ksr[2 * i + 1]
    common["w_l2_qs2"] = qs2
    common["w_l2_ks2"] = ks2
    common["w_l2_wrT"] = _tr(g["L2_ffn_W_r"])
    common["w_l2_wkT"] = _tr(g["L2_ffn_W_k"])
    common["w_l2_wvT"] = _tr(g["L2_ffn_W_v"])
    common["w_l2_mur"] = asf(g["L2_ffn_mu_r"]).reshape(8, 128).T
    common["w_l2_muk"] = asf(g["L2_ffn_mu_k"]).reshape(8, 128).T
    common["w_l3_qT"] = _tr(g["L3_attn_Wq"])
    common["w_l3_kT"] = _tr(g["L3_attn_Wk"])
    common["w_l3_vT"] = _tr(g["L3_attn_Wv"])
    common["w_l3_dlT"] = _tr(g["L3_attn_Wdl"])
    common["w_l3_dtT"] = _tr(g["L3_attn_Wdt"])
    common["w_l3_outT"] = _tr(g["L3_attn_Wout"])
    common["w_l3_alog"] = asf(g["L3_attn_A_log"]).reshape(16, 1)
    common["w_l3_dtb"] = asf(g["L3_attn_dt_bias"]).reshape(16, 1)
    common["w_l3_lp"] = asf(g["L3_attn_Lp"]).reshape(192, 1)
    common["w_l3_lng"] = asf(g["L3_attn_ln_g"]).reshape(1, D)
    common["w_l3_lnb"] = asf(g["L3_attn_ln_b"]).reshape(1, D)
    common["w_l3_gate"] = asf(g["L3_ffn_w_gate"])
    common["w_l3_up"] = asf(g["L3_ffn_w_up"])
    common["w_l3_down"] = asf(g["L3_ffn_w_down"])

    x_full = np.asarray(g["L0_attn_x"], f32)
    ew = np.asarray(g["L0_ffn_expert_weights"], f32)
    ei = np.asarray(g["L0_ffn_expert_indices"])

    in_maps = []
    for c in range(NCORES):
        m = dict(common)
        m["x"] = np.ascontiguousarray(x_full[c])
        coeff = np.zeros((T, 8), f32)
        t0 = c * T
        if t0 < ew.shape[0]:
            sl = slice(t0, min(t0 + T, ew.shape[0]))
            eic = ei[sl]
            ewc = ew[sl]
            for k in range(eic.shape[1]):
                np.add.at(coeff, (np.arange(eic.shape[0]), eic[:, k]),
                          ewc[:, k])
        m["l0_coeffT"] = np.ascontiguousarray(coeff.T)
        in_maps.append(m)
    return in_maps



class Rot:
    def __init__(self, items):
        self.items = items
        self.i = 0

    def nxt(self):
        r = self.items[self.i % len(self.items)]
        self.i += 1
        return r

    def nxt_not(self, excl):
        for _ in range(len(self.items) + 1):
            r = self.nxt()
            if all(r is not e for e in excl):
                return r
        raise RuntimeError("pool exhausted")


class Bufs:
    def __init__(self, p, np_inputs_shapes):
        self.p = p
        self.din_shapes = np_inputs_shapes
        self.dins = {}
        self.used_inputs = set()
        self.XT = p.sbuf("XT", [P, 8 * T], BF16)
        self.A2 = p.sbuf("A2", [P, 8 * T], BF16)
        self.A3 = p.sbuf("A3", [P, 2 * T], BF16)
        self.ft = Rot([p.sbuf(f"ft{i}", [P, 1024]) for i in range(4)])
        self.fs = Rot([p.sbuf(f"fs{i}", [P, 512]) for i in range(4)])
        self.xb = Rot([p.sbuf(f"xb{i}", [P, 1024], BF16) for i in range(3)])
        self.col = Rot([p.sbuf(f"col{i}", [P, 1]) for i in range(10)])
        self.gbc = p.sbuf("gbc", [P, 1024], BF16)
        self.bbc = p.sbuf("bbc", [P, 1024], BF16)
        self.g1 = p.sbuf("g1", [P, 512])
        self.sh1 = p.sbuf("sh1", [12, 512], BF16)
        self.pb = Rot([p.psum(f"pb{i}", [P, 512]) for i in range(6)])
        self.pt = Rot([p.psum(f"pt{i}", [P, 1024], BF16) for i in range(2)])
        self.ident = p.sbuf("ident", [P, P], BF16)
        self.pihalf = p.sbuf("pihalf", [P, 1])
        self.epscol = p.sbuf("epscol", [P, 1])

    def din(self, name):
        if name not in self.dins:
            shape, dt = self.din_shapes[name]
            self.dins[name] = self.p.dram(name, list(shape), dt,
                                          kind="ExternalInput")
            self.used_inputs.add(name)
        return self.dins[name]

    def load_consts(self):
        p = self.p
        p.dma(self.ident[:], self.din("c_ident"))
        p.dma(self.pihalf[:], self.din("c_pihalf"))
        op_memset(p, self.epscol[:], 1e-5)


def op_multi(p, eng, fns, reads=(), writes=()):
    waits = p._waits_for(eng, reads, writes)
    p.cnt[eng] += 1
    seq = p.cnt[eng]

    def fn(e):
        last = None
        for f in fns:
            last = f(e)
        return last
    p.ops.append((eng, waits, fn, seq))
    p._record(("eng", eng, seq), reads, writes)
    return seq


def xt_chunk(XT, c, t0, tn, T_=T):
    return XT[:, c * T_ + t0: c * T_ + t0 + tn]


def emit_ln_xt(p, B, x_dram, XT=None, group=2, extra_tile_fn=None):
    XT = XT if XT is not None else B.XT
    XTv = XT[:].rearrange("p (c t) -> p c t", t=T)
    for i0 in range(0, NT, group):
        tiles = range(i0, min(i0 + group, NT))
        fts = {}
        for i in tiles:
            fts[i] = B.ft.nxt()
            p.dma(fts[i][:], x_dram[i * P:(i + 1) * P, :],
                  q="sp" if i % 2 else "pool")
        for i in tiles:
            xt = fts[i]
            xc = B.ft.nxt()
            m = B.col.nxt(); ss = B.col.nxt(); rs = B.col.nxt()
            op_red(p, m[:], xt[:], ALU.add)
            op_ts(p, m[:], m[:], 1.0 / D, ALU.mult)
            op_ts(p, xc[:], xt[:], m[:], ALU.subtract)
            op_act(p, xt[:], xc[:], AF.Square, accum_out=ss[:])
            op_ts(p, ss[:], ss[:], 1.0 / D, ALU.mult)
            op_act(p, ss[:], ss[:], AF.Ln, bias=B.epscol[:])
            op_act(p, rs[:], ss[:], AF.Exp, scale=-0.5)
            xnb = B.xb.nxt()
            op_ts(p, xnb[:], xc[:], rs[:], ALU.mult)
            if extra_tile_fn is not None:
                extra_tile_fn(i, xc, rs, m)
            ptb = B.pt.nxt()
            op_multi(p, "pe",
                     [(lambda e, c=c, xnb=xnb, ptb=ptb:
                       e.transpose(ptb[:, c * P:(c + 1) * P],
                                   xnb[:, c * P:(c + 1) * P], B.ident[:]))
                      for c in range(8)],
                     reads=[xnb[:], B.ident[:]], writes=[ptb[:]])
            op_copy(p, XTv[:, :, i * P:(i + 1) * P],
                    ptb[:].rearrange("p (c t) -> p c t", t=P))
        p.barrier()


def emit_load_wT(p, B, w_name, dst, dst_ofs, K, N, nstride=None, q="sp",
                 col0=0):
    nstride = nstride or N
    wd = B.din(w_name)
    kc = K // P
    jobs = [(c, n0, min(1024, N - n0))
            for c in range(kc) for n0 in range(0, N, 1024)]
    for g0 in range(0, len(jobs), 4):
        grp = jobs[g0:g0 + 4]
        stgs = []
        for (c, n0, nw) in grp:
            stg = B.ft.nxt()
            stgs.append(stg)
            p.dma(stg[:, :nw], wd[c * P:(c + 1) * P, col0 + n0:col0 + n0 + nw],
                  q=q)
        for (c, n0, nw), stg in zip(grp, stgs):
            op_copy(p, dst[:, dst_ofs + c * nstride + n0:
                           dst_ofs + c * nstride + n0 + nw],
                    stg[:, :nw], eng="dve" if (c + n0 // 1024) % 2 == 0 else "act")
        p.barrier()



def emit_l1ffn(p, B, xin, xout):
    for c in range(8):
        th = B.ft.nxt(); tn = B.ft.nxt()
        p.dma(th[:], B.din("w_l1_theta")[c * P:(c + 1) * P, :])
        p.dma(tn[:], B.din("w_l1_thn")[c * P:(c + 1) * P, :], q="pool")
        ph = B.ft.nxt(); pn = B.ft.nxt()
        p.dma(ph[:], B.din("w_l1_phi")[c * P:(c + 1) * P, :])
        p.dma(pn[:], B.din("w_l1_phn")[c * P:(c + 1) * P, :], q="pool")
        op_tt(p, th[:], th[:], tn[:], ALU.add)
        op_act(p, B.A2[:, c * 1024:(c + 1) * 1024], th[:], AF.Sin,
               bias=B.pihalf[:])
        op_tt(p, ph[:], ph[:], pn[:], ALU.add)
        op_act(p, B.A2[:, 8192 + c * 1024: 8192 + (c + 1) * 1024], ph[:],
               AF.Sin)
        p.barrier()
    for i0 in range(0, NT, 4):
        tiles = range(i0, min(i0 + 4, NT))
        xres = {}
        for i in tiles:
            xres[i] = B.ft.nxt()
            p.dma(xres[i][:], xin[i * P:(i + 1) * P, :])
        for i in tiles:
            for j in (0, 512):
                psA = B.pb.nxt(); psB = B.pb.nxt()
                p.mm_group([(psA[:, :512], xt_chunk(B.XT, c, i * P, P),
                             B.A2[:, c * 1024 + j: c * 1024 + j + 512],
                             c == 0, c == 7, None) for c in range(8)],
                           reads=[B.XT[:], B.A2[:]], writes=[psA[:]])
                p.mm_group([(psB[:, :512], xt_chunk(B.XT, c, i * P, P),
                             B.A2[:, 8192 + c * 1024 + j: 8192 + c * 1024 + j + 512],
                             c == 0, c == 7, None) for c in range(8)],
                           reads=[B.XT[:], B.A2[:]], writes=[psB[:]])
                fA = B.fs.nxt(); fB = B.fs.nxt()
                op_act(p, fA[:], psA[:, :512], AF.Square)
                op_act(p, fB[:], psB[:, :512], AF.Square)
                op_tt(p, fA[:], fA[:], fB[:], ALU.add)
                op_act(p, fA[:], fA[:], AF.Sqrt)
                op_tt(p, xres[i][:, j:j + 512], xres[i][:, j:j + 512],
                      fA[:], ALU.add)
            p.dma(xout[i * P:(i + 1) * P, :], xres[i][:], q="pool")
        p.barrier()



BLOCK_SPEC_DEFAULT = [(0, "fa"), (1, "fa"), (2, "fa"), (3, "fa")]


def emit_block(p, B, k, branches, xin, xout):
    emit_ln_xt(p, B, xin)
    first = True
    if k == 1:
        if "f" in branches:
            emit_l1ffn(p, B, xin, xout)
            first = False
        if "a" in branches:
            emit_l1attn(p, B, xin, xout, first)
            first = False
    elif k == 0:
        if "a" in branches:
            emit_l0attn(p, B, xin, xout, first)
            first = False
        if "f" in branches:
            emit_l0ffn(p, B, xin, xout, first)
            first = False
    elif k == 2:
        if "a" in branches:
            emit_l2attn(p, B, xin, xout, first)
            first = False
        if "f" in branches:
            emit_l2ffn(p, B, xin, xout, first)
            first = False
    elif k == 3:
        if "a" in branches:
            emit_l3attn(p, B, xin, xout, first)
            first = False
        if "f" in branches:
            emit_l3ffn(p, B, xin, xout, first)
            first = False
    assert not first, f"no branches for block {k}"


def build_program(spec, shapes):
    nc = bass.Bass("TRN2", target_bir_lowering=False, debug=False)
    p = Prog(nc)
    B = Bufs(p, shapes)
    x_in = p.dram("x", [T, D], F32, kind="ExternalInput")
    out = p.dram("out", [T, D], F32, kind="ExternalOutput")
    scratch = [p.dram(f"xs{i}", [T, D], F32) for i in range(2)]
    B.load_consts()
    p.barrier()
    cur = x_in
    for idx, (bk, branches) in enumerate(spec):
        dst = out if idx == len(spec) - 1 else scratch[idx % 2]
        emit_block(p, B, bk, branches, cur, dst)
        cur = dst
    p.close()
    nc.finalize()
    return nc, B.used_inputs


def _shapes_of(m):
    out = {}
    for k, v in m.items():
        dt = {np.dtype(np.float32): F32, np.dtype(bf16): BF16,
              np.dtype(np.int32): I32}[v.dtype]
        out[k] = (v.shape, dt)
    return out


KERNEL_SPEC = BLOCK_SPEC_DEFAULT


def kernel(**inputs):
    in_maps = stage_inputs(inputs)
    shapes = _shapes_of(in_maps[0])
    nc, used = build_program(KERNEL_SPEC, shapes)
    used = set(used) | {"x"}
    ims = [{k: m[k] for k in m if k in used} for m in in_maps]
    res = run_bass_kernel_spmd(nc, ims, core_ids=list(range(NCORES)))
    return np.stack([np.asarray(r["out"], np.float32)
                     for r in res.results], axis=0)



def emit_l2ffn(p, B, xin, xout, first):
    QT = T // 4
    emit_load_wT(p, B, "w_l2_wrT", B.A2, 0, D, D)
    emit_load_wT(p, B, "w_l2_wkT", B.A2, 8192, D, D)
    emit_load_wT(p, B, "w_l2_wvT", B.A2, 16384, D, D)
    RQ = 24576
    mur = p.sbuf("l2_mur", [P, 8])
    muk = p.sbuf("l2_muk", [P, 8])
    bcol = p.sbuf("l2_bcol", [P, 8])
    p.dma(mur[:], B.din("w_l2_mur"))
    p.dma(muk[:], B.din("w_l2_muk"))
    p.barrier()

    def build_mix(q, mu, dst_ofs):
        t0 = q * QT
        for c in range(8):
            dst = B.A3[:, dst_ofs + c * QT: dst_ofs + (c + 1) * QT]
            tmp = B.xb.nxt()
            if t0 == 0:
                op_ts(p, dst[:, 0:1], B.XT[:, c * T: c * T + 1],
                      mu[:, c:c + 1], ALU.mult)
                cur = B.XT[:, c * T + 1: c * T + QT]
                prv = B.XT[:, c * T: c * T + QT - 1]
                op_tt(p, tmp[:, :QT - 1], cur, prv, ALU.subtract)
                op_stt(p, dst[:, 1:], tmp[:, :QT - 1], mu[:, c:c + 1], prv,
                       ALU.mult, ALU.add)
            else:
                pcol = bcol[:, c:c + 1]
                op_tt(p, tmp[:, 0:1], B.XT[:, c * T + t0: c * T + t0 + 1],
                      pcol, ALU.subtract)
                op_stt(p, dst[:, 0:1], tmp[:, 0:1], mu[:, c:c + 1], pcol,
                       ALU.mult, ALU.add)
                cur = B.XT[:, c * T + t0 + 1: c * T + t0 + QT]
                prv = B.XT[:, c * T + t0: c * T + t0 + QT - 1]
                op_tt(p, tmp[:, 1:QT], cur, prv, ALU.subtract)
                op_stt(p, dst[:, 1:], tmp[:, 1:QT], mu[:, c:c + 1], prv,
                       ALU.mult, ALU.add)

    for q in range(4):
        t0 = q * QT
        build_mix(q, mur, 0)
        for it in range(QT // P):
            for j in (0, 512):
                ps = B.pb.nxt()
                p.mm_group([(ps[:, :512],
                             B.A3[:, c * QT + it * P: c * QT + (it + 1) * P],
                             B.A2[:, c * 1024 + j: c * 1024 + j + 512],
                             c == 0, c == 7, None) for c in range(8)],
                           reads=[B.A3[:], B.A2[:]], writes=[ps[:]])
                op_act(p, B.A2[:, RQ + it * 1024 + j: RQ + it * 1024 + j + 512],
                       ps[:, :512], AF.Sigmoid)
        for c in range(8):
            op_copy(p, bcol[:, c:c + 1],
                    B.XT[:, c * T + t0 + QT - 1: c * T + t0 + QT], eng="act")
        p.barrier()
        build_mix(q, muk, 0)
        for dc in range(8):
            for jt in (0, 512):
                ps = B.pb.nxt()
                p.mm_group([(ps[:, :512],
                             B.A2[:, 8192 + c * 1024 + dc * P:
                                  8192 + c * 1024 + (dc + 1) * P],
                             B.A3[:, c * QT + jt: c * QT + jt + 512],
                             c == 0, c == 7, None) for c in range(8)],
                           reads=[B.A3[:], B.A2[:]], writes=[ps[:]])
                fr = B.fs.nxt()
                op_act(p, fr[:], ps[:, :512], AF.Relu)
                op_act(p, B.XT[:, dc * T + t0 + jt: dc * T + t0 + jt + 512],
                       fr[:], AF.Square)
        p.barrier()
        for it0 in range(0, QT // P, 4):
          xres_t = {}
          for it in range(it0, it0 + 4):
            gi = t0 // P + it
            xres_t[it] = B.ft.nxt()
            p.dma(xres_t[it][:], (xin if first else xout)[gi * P:(gi + 1) * P, :])
          for it in range(it0, it0 + 4):
            gi = t0 // P + it
            xres = xres_t[it]
            for j in (0, 512):
                ps = B.pb.nxt()
                p.mm_group([(ps[:, :512],
                             B.XT[:, c * T + t0 + it * P:
                                  c * T + t0 + (it + 1) * P],
                             B.A2[:, 16384 + c * 1024 + j:
                                  16384 + c * 1024 + j + 512],
                             c == 0, c == 7, None) for c in range(8)],
                           reads=[B.XT[:], B.A2[:]], writes=[ps[:]])
                fv = B.fs.nxt()
                op_tt(p, fv[:], ps[:, :512],
                      B.A2[:, RQ + it * 1024 + j: RQ + it * 1024 + j + 512],
                      ALU.mult)
                op_tt(p, xres[:, j:j + 512], xres[:, j:j + 512], fv[:],
                      ALU.add)
            p.dma(xout[gi * P:(gi + 1) * P, :], xres[:], q="pool")
          p.barrier()



def _flash_tks(p, B, psS, psO_A, psO_B, tqc, ntk, ncA, ncB, v2, cmask=None):
    def emit_S(tk, pA, pB):
        p.mm_group(
            [(pA[:, :512], B.A3[0:64, tk * 128: tk * 128 + 128],
              B.A3[0:64, T + tqc * 512: T + (tqc + 1) * 512],
              True, True, (0, 0)),
             (pB[:, :512], B.A3[64:128, tk * 128: tk * 128 + 128],
              B.A3[64:128, T + tqc * 512: T + (tqc + 1) * 512],
              True, True, (64, 0))],
            reads=[B.A3[:]], writes=[pA[:], pB[:]])
    banks = []
    for tk in range(min(2, ntk)):
        pA, pB = psS.nxt(), psS.nxt()
        banks.append((pA, pB))
        emit_S(tk, pA, pB)
    for tk in range(ntk):
        pA, pB = banks[tk % 2]
        fsA = B.fs.items[(2 * tk) % 4]
        fsB = B.fs.items[(2 * tk + 1) % 4]
        op_copy(p, fsA[:, :512], pA[:, :512], eng="dve")
        op_copy(p, fsB[:, :512], pB[:, :512], eng="dve")
        ptA = B.xb.nxt()
        op_act(p, ptA[:, :512], fsA[:, :512], AF.Exp, bias=ncA[:],
               scale=0.125)
        op_act(p, ptA[:, 512:1024], fsB[:, :512], AF.Exp, bias=ncB[:],
               scale=0.125)
        if cmask is not None:
            diag_off = tk * 128 - tqc * 512
            if 0 <= diag_off < 512:
                msl = cmask[:, 384 - diag_off: 896 - diag_off]
                op_tt(p, ptA[:, :512], ptA[:, :512], msl, ALU.mult)
                op_tt(p, ptA[:, 512:1024], ptA[:, 512:1024], msl, ALU.mult)
        if tk + 2 < ntk:
            emit_S(tk + 2, pA, pB)
        p.mm_group([(psO_A[0:65, :512], v2[:, tk * 65: tk * 65 + 65],
                     ptA[:, :512], tk == 0, tk == ntk - 1, None)],
                   reads=[v2[:], ptA[:]], writes=[psO_A[:]])
        p.mm_group([(psO_B[0:65, :512],
                     v2[:, 2080 + tk * 65: 2080 + tk * 65 + 65],
                     ptA[:, 512:1024], tk == 0, tk == ntk - 1, None)],
                   reads=[v2[:], ptA[:]], writes=[psO_B[:]])


def emit_l0attn(p, B, xin, xout, first, wq="w_l0_qT", wk="w_l0_kT",
                wv="w_l0_vT", wo="w_l0_oT"):
    v2 = p.sbuf("l0_v2", [P, 2 * 32 * 65], BF16)
    cmask = p.sbuf("l0_cmask", [P, 896], BF16)
    ones2 = p.sbuf("l0_ones2", [P, 65], BF16)
    onesr = p.sbuf("l0_onesr", [1, P], BF16)
    mxq = p.sbuf("l0_mxq", [P, 1])
    mxk = p.sbuf("l0_mxk", [P, 1])
    ncol = p.sbuf("l0_ncol", [P, 1])
    nb16 = p.sbuf("l0_nb16", [1, 2], BF16)
    lrow = p.sbuf("l0_lrow", [1, 512])
    lb16 = p.sbuf("l0_lb16", [1, 512], BF16)
    lbc = p.sbuf("l0_lbc", [P, 512], BF16)
    p.dma(cmask[:], B.din("c_cmask"))
    p.dma(ones2[:], B.din("c_ones2"))
    p.dma(onesr[:], B.din("c_onesr"))
    p.barrier()

    psO_A, psO_B = B.pb.items[0], B.pb.items[1]
    psS = Rot(B.pb.items[2:6])

    for pr in range(8):
        hA, hB = 2 * pr, 2 * pr + 1
        wq_t = B.xb.items[0]; wk_t = B.xb.items[1]; wv_t = B.xb.items[2]
        for wname, wt in ((wq, wq_t), (wk, wk_t), (wv, wv_t)):
            for g in range(2):
                stgs = []
                for c in range(4 * g, 4 * g + 4):
                    stg = B.fs.nxt()
                    stgs.append((c, stg))
                    p.dma(stg[:, :128],
                          B.din(wname)[c * P:(c + 1) * P,
                                       pr * 128:(pr + 1) * 128],
                          q="sp" if c % 2 else "pool")
                for c, stg in stgs:
                    op_copy(p, wt[:, c * 128:(c + 1) * 128], stg[:, :128],
                            eng="dve" if c % 2 else "act")
                p.barrier()
        for (wt, dst_ofs) in ((wq_t, T), (wk_t, 0)):
            for tq in range(8):
                ps = psS.nxt()
                p.mm_group([(ps[:, :512], wt[:, c * 128:(c + 1) * 128],
                             xt_chunk(B.XT, c, tq * 512, 512),
                             c == 0, c == 7, None) for c in range(8)],
                           reads=[wt[:], B.XT[:]], writes=[ps[:]])
                op_copy(p, B.A3[:, dst_ofs + tq * 512: dst_ofs + (tq + 1) * 512],
                        ps[:, :512], eng="act")
        op_memset(p, v2[:], 1.0)
        for tq in range(8):
            ps = psS.nxt()
            p.mm_group([(ps[:, :512], wv_t[:, c * 128:(c + 1) * 128],
                         xt_chunk(B.XT, c, tq * 512, 512),
                         c == 0, c == 7, None) for c in range(8)],
                       reads=[wv_t[:], B.XT[:]], writes=[ps[:]])
            vtb = B.xb.items[0] if tq % 2 == 0 else B.xb.items[1]
            op_copy(p, vtb[:, :512], ps[:, :512])
            for q in range(4):
                it = tq * 4 + q
                ptb = B.pt.nxt()
                op_transpose(p, ptb[:, :P], vtb[:, q * 128:(q + 1) * 128],
                             B.ident[:])
                op_copy(p, v2[:, it * 65: it * 65 + 64],
                        ptb[:, 0:64], eng="act")
                op_copy(p, v2[:, 2080 + it * 65: 2080 + it * 65 + 64],
                        ptb[:, 64:128], eng="act")
        p.barrier()
        for (src_ofs, mx) in ((T, mxq), (0, mxk)):
            op_memset(p, mx[:], 0.0, eng="pool")
            for g in range(4):
                sq = B.xb.items[2]
                op_act(p, sq[:], B.A3[:, src_ofs + g * 1024:
                                      src_ofs + (g + 1) * 1024], AF.Square)
                for j in (0, 512):
                    psn = psS.nxt()
                    p.mm_group([(psn[0:65, :512], ones2[:], sq[:, j:j + 512],
                                 True, True, None)],
                               reads=[ones2[:], sq[:]], writes=[psn[:]])
                    r = B.col.nxt()
                    op_red(p, r[0:1, :], psn[0:1, :512], ALU.max)
                    op_red(p, r[64:65, :], psn[64:65, :512], ALU.max)
                    op_tt(p, mx[0:1, :], mx[0:1, :], r[0:1, :], ALU.max)
                    op_tt(p, mx[64:65, :], mx[64:65, :], r[64:65, :], ALU.max)
        op_tt(p, ncol[0:1, :], mxq[0:1, :], mxk[0:1, :], ALU.mult)
        op_tt(p, ncol[64:65, :], mxq[64:65, :], mxk[64:65, :], ALU.mult)
        op_act(p, ncol[0:1, :], ncol[0:1, :], AF.Sqrt)
        op_act(p, ncol[64:65, :], ncol[64:65, :], AF.Sqrt)
        op_ts(p, ncol[0:1, :], ncol[0:1, :], -0.125, ALU.mult)
        op_ts(p, ncol[64:65, :], ncol[64:65, :], -0.125, ALU.mult)
        op_copy(p, nb16[:, 0:1], ncol[0:1, :])
        op_copy(p, nb16[:, 1:2], ncol[64:65, :])
        ncA = B.col.nxt(); ncB = B.col.nxt()
        psn = psS.nxt()
        p.mm_group([(psn[:, 0:1], onesr[:], nb16[:, 0:1], True, True, None),
                    (psn[:, 1:2], onesr[:], nb16[:, 1:2], False, True, None)],
                   reads=[onesr[:], nb16[:]], writes=[psn[:]])
        op_copy(p, ncA[:], psn[:, 0:1])
        op_copy(p, ncB[:], psn[:, 1:2])
        p.barrier()
        for tqc in range(8):
            ntk = 4 * tqc + 4
            _flash_tks(p, B, psS, psO_A, psO_B, tqc, ntk, ncA, ncB, v2,
                       cmask=cmask)
            for (psO, h) in ((psO_A, hA), (psO_B, hB)):
                c = h // 2
                ro = (h % 2) * 64
                op_act(p, lrow[:, :512], psO[64:65, :512], AF.Ln)
                op_act(p, lrow[:, :512], lrow[:, :512], AF.Exp, scale=-1.0)
                op_copy(p, lb16[:, :512], lrow[:, :512])
                psb = psS.nxt()
                p.mm_group([(psb[0:64, :512], onesr[:, 0:64], lb16[:, :512],
                             True, True, None)],
                           reads=[onesr[:], lb16[:]], writes=[psb[:]])
                op_copy(p, lbc[0:64, :512], psb[0:64, :512], eng="act")
                p.op("dve",
                     lambda e, c=c, ro=ro, tqc=tqc, psO=psO:
                     e.tensor_tensor(
                         B.A2[ro:ro + 64,
                              c * T + tqc * 512: c * T + (tqc + 1) * 512],
                         psO[0:64, :512], lbc[0:64, :512], ALU.mult),
                     reads=[psO[:], lbc[:]], writes=[B.A2[:]])
            p.barrier()
    emit_load_wT(p, B, wo, B.A3, 0, D, D)
    for i0 in range(0, NT, 4):
        xres_t = {}
        for i in range(i0, i0 + 4):
            xres_t[i] = B.ft.nxt()
            p.dma(xres_t[i][:], (xin if first else xout)[i * P:(i + 1) * P, :],
                  q="sp" if i % 2 else "pool")
        for i in range(i0, i0 + 4):
            xres = xres_t[i]
            for j in (0, 512):
                ps = B.pb.nxt()
                p.mm_group([(ps[:, :512], xt_chunk(B.A2, c, i * P, P),
                             B.A3[:, c * 1024 + j: c * 1024 + j + 512],
                             c == 0, c == 7, None) for c in range(8)],
                           reads=[B.A2[:], B.A3[:]], writes=[ps[:]])
                op_tt(p, xres[:, j:j + 512], xres[:, j:j + 512], ps[:, :512],
                      ALU.add)
            p.dma(xout[i * P:(i + 1) * P, :], xres[:], q="pool")
        p.barrier()



def emit_l2attn(p, B, xin, xout, first):
    v2 = p._sbufs.get("l0_v2") or p.sbuf("l0_v2", [P, 2 * 32 * 65], BF16)
    ones2 = p._sbufs["l0_ones2"] if "l0_ones2" in p._sbufs else \
        p.sbuf("l0_ones2", [P, 65], BF16)
    onesr = p._sbufs["l0_onesr"] if "l0_onesr" in p._sbufs else \
        p.sbuf("l0_onesr", [1, P], BF16)
    mxq = p._sbufs.get("l0_mxq") or p.sbuf("l0_mxq", [P, 1])
    mxk = p._sbufs.get("l0_mxk") or p.sbuf("l0_mxk", [P, 1])
    ncol = p._sbufs.get("l0_ncol") or p.sbuf("l0_ncol", [P, 1])
    nb16 = p._sbufs.get("l0_nb16") or p.sbuf("l0_nb16", [1, 2], BF16)
    lrow = p._sbufs.get("l0_lrow") or p.sbuf("l0_lrow", [1, 512])
    lb16 = p._sbufs.get("l0_lb16") or p.sbuf("l0_lb16", [1, 512], BF16)
    lbc = p._sbufs.get("l0_lbc") or p.sbuf("l0_lbc", [P, 512], BF16)
    qs2 = p.sbuf("l2_qs2", [P, 8])
    ks2 = p.sbuf("l2_ks2", [P, 8])
    frow = lrow
    fb16 = lb16
    if "l0_ones2" not in p._sbufs or True:
        p.dma(ones2[:], B.din("c_ones2"))
        p.dma(onesr[:], B.din("c_onesr"))
    p.dma(qs2[:], B.din("w_l2_qs2"))
    p.dma(ks2[:], B.din("w_l2_ks2"))
    p.barrier()

    psO_A, psO_B = B.pb.items[0], B.pb.items[1]
    psS = Rot(B.pb.items[2:6])

    for pr in range(8):
        hA, hB = 2 * pr, 2 * pr + 1
        wq_t = B.xb.items[0]; wk_t = B.xb.items[1]; wv_t = B.xb.items[2]
        for wname, wt in (("w_l2_qT", wq_t), ("w_l2_kT", wk_t),
                          ("w_l2_vT", wv_t)):
            for g in range(2):
                stgs = []
                for c in range(4 * g, 4 * g + 4):
                    stg = B.fs.nxt()
                    stgs.append((c, stg))
                    p.dma(stg[:, :128],
                          B.din(wname)[c * P:(c + 1) * P,
                                       pr * 128:(pr + 1) * 128],
                          q="sp" if c % 2 else "pool")
                for c, stg in stgs:
                    op_copy(p, wt[:, c * 128:(c + 1) * 128], stg[:, :128],
                            eng="dve" if c % 2 else "act")
                p.barrier()
        for (wt, dst_ofs, sc) in ((wq_t, T, qs2), (wk_t, 0, ks2)):
            for tq in range(8):
                ps = psS.nxt()
                p.mm_group([(ps[:, :512], wt[:, c * 128:(c + 1) * 128],
                             xt_chunk(B.XT, c, tq * 512, 512),
                             c == 0, c == 7, None) for c in range(8)],
                           reads=[wt[:], B.XT[:]], writes=[ps[:]])
                op_copy(p, B.A3[:, dst_ofs + tq * 512: dst_ofs + (tq + 1) * 512],
                        ps[:, :512], eng="act")
        op_memset(p, v2[:], 1.0)
        for tq in range(8):
            ps = psS.nxt()
            p.mm_group([(ps[:, :512], wv_t[:, c * 128:(c + 1) * 128],
                         xt_chunk(B.XT, c, tq * 512, 512),
                         c == 0, c == 7, None) for c in range(8)],
                       reads=[wv_t[:], B.XT[:]], writes=[ps[:]])
            vtb = B.xb.items[0] if tq % 2 == 0 else B.xb.items[1]
            op_copy(p, vtb[:, :512], ps[:, :512])
            for q in range(4):
                it = tq * 4 + q
                ptb = B.pt.nxt()
                op_transpose(p, ptb[:, :P], vtb[:, q * 128:(q + 1) * 128],
                             B.ident[:])
                op_copy(p, v2[:, it * 65: it * 65 + 64],
                        ptb[:, 0:64], eng="act")
                op_copy(p, v2[:, 2080 + it * 65: 2080 + it * 65 + 64],
                        ptb[:, 64:128], eng="act")
        p.barrier()
        for (dst_ofs, sc) in ((T, qs2), (0, ks2)):
            for tq in range(8):
                sq = B.xb.nxt()
                op_act(p, sq[:, :512],
                       B.A3[:, dst_ofs + tq * 512: dst_ofs + (tq + 1) * 512],
                       AF.Square)
                psn = psS.nxt()
                p.mm_group([(psn[0:65, :512], ones2[:], sq[:, :512],
                             True, True, None)],
                           reads=[ones2[:], sq[:]], writes=[psn[:]])
                for hh, row in ((0, 0), (64, 64)):
                    op_act(p, frow[:, :512], psn[row:row + 1, :512], AF.Ln)
                    op_ts(p, frow[:, :512], frow[:, :512], 0.5, ALU.mult,
                          -27.631021, ALU.max)
                    op_act(p, frow[:, :512], frow[:, :512], AF.Exp,
                           scale=-1.0)
                    op_copy(p, fb16[:, :512], frow[:, :512])
                    psb = psS.nxt()
                    p.mm_group([(psb[:, :512], onesr[:], fb16[:, :512],
                                 True, True, None)],
                               reads=[onesr[:], fb16[:]], writes=[psb[:]])
                    fbc = B.fs.nxt()
                    op_copy(p, fbc[:, :512], psb[:, :512], eng="act")
                    sl = B.A3[hh:hh + 64,
                              dst_ofs + tq * 512: dst_ofs + (tq + 1) * 512]
                    op_stt(p, sl, sl, sc[hh:hh + 64, pr:pr + 1],
                           fbc[hh:hh + 64, :512], ALU.mult, ALU.mult)
        p.barrier()
        for (src_ofs, mx) in ((T, mxq), (0, mxk)):
            op_memset(p, mx[:], 0.0, eng="pool")
            for g in range(4):
                sq = B.xb.items[2]
                op_act(p, sq[:], B.A3[:, src_ofs + g * 1024:
                                      src_ofs + (g + 1) * 1024], AF.Square)
                for j in (0, 512):
                    psn = psS.nxt()
                    p.mm_group([(psn[0:65, :512], ones2[:], sq[:, j:j + 512],
                                 True, True, None)],
                               reads=[ones2[:], sq[:]], writes=[psn[:]])
                    r = B.col.nxt()
                    op_red(p, r[0:1, :], psn[0:1, :512], ALU.max)
                    op_red(p, r[64:65, :], psn[64:65, :512], ALU.max)
                    op_tt(p, mx[0:1, :], mx[0:1, :], r[0:1, :], ALU.max)
                    op_tt(p, mx[64:65, :], mx[64:65, :], r[64:65, :], ALU.max)
        op_tt(p, ncol[0:1, :], mxq[0:1, :], mxk[0:1, :], ALU.mult)
        op_tt(p, ncol[64:65, :], mxq[64:65, :], mxk[64:65, :], ALU.mult)
        op_act(p, ncol[0:1, :], ncol[0:1, :], AF.Sqrt)
        op_act(p, ncol[64:65, :], ncol[64:65, :], AF.Sqrt)
        op_ts(p, ncol[0:1, :], ncol[0:1, :], -0.125, ALU.mult)
        op_ts(p, ncol[64:65, :], ncol[64:65, :], -0.125, ALU.mult)
        op_copy(p, nb16[:, 0:1], ncol[0:1, :])
        op_copy(p, nb16[:, 1:2], ncol[64:65, :])
        ncA = B.col.nxt(); ncB = B.col.nxt()
        psn = psS.nxt()
        p.mm_group([(psn[:, 0:1], onesr[:], nb16[:, 0:1], True, True, None),
                    (psn[:, 1:2], onesr[:], nb16[:, 1:2], False, True, None)],
                   reads=[onesr[:], nb16[:]], writes=[psn[:]])
        op_copy(p, ncA[:], psn[:, 0:1])
        op_copy(p, ncB[:], psn[:, 1:2])
        p.barrier()
        for tqc in range(8):
            ntk = 32
            _flash_tks(p, B, psS, psO_A, psO_B, tqc, ntk, ncA, ncB, v2)
            for (psO, h) in ((psO_A, hA), (psO_B, hB)):
                c = h // 2
                ro = (h % 2) * 64
                op_act(p, lrow[:, :512], psO[64:65, :512], AF.Ln)
                op_act(p, lrow[:, :512], lrow[:, :512], AF.Exp, scale=-1.0)
                op_copy(p, lb16[:, :512], lrow[:, :512])
                psb = psS.nxt()
                p.mm_group([(psb[0:64, :512], onesr[:, 0:64], lb16[:, :512],
                             True, True, None)],
                           reads=[onesr[:], lb16[:]], writes=[psb[:]])
                op_copy(p, lbc[0:64, :512], psb[0:64, :512], eng="act")
                p.op("dve",
                     lambda e, c=c, ro=ro, tqc=tqc, psO=psO:
                     e.tensor_tensor(
                         B.A2[ro:ro + 64,
                              c * T + tqc * 512: c * T + (tqc + 1) * 512],
                         psO[0:64, :512], lbc[0:64, :512], ALU.mult),
                     reads=[psO[:], lbc[:]], writes=[B.A2[:]])
            p.barrier()
    emit_load_wT(p, B, "w_l2_oT", B.A3, 0, D, D)
    bor = p._sbufs.get("l1_grow") or p.sbuf("l1_grow", [1, D], BF16)
    bof = B.ft.nxt()
    p.dma(bof[0:1, :], B.din("w_l2_bo"))
    op_copy(p, bor[:], bof[0:1, :])
    p.barrier()
    for j in (0, 512):
        psb = B.pb.nxt()
        p.mm_group([(psb[:, :512], onesr[:], bor[:, j:j + 512],
                     True, True, None)],
                   reads=[onesr[:], bor[:]], writes=[psb[:]])
        op_copy(p, B.bbc[:, j:j + 512], psb[:, :512], eng="act")
    p.barrier()
    for i0 in range(0, NT, 4):
        xres_t = {}
        for i in range(i0, i0 + 4):
            xres_t[i] = B.ft.nxt()
            p.dma(xres_t[i][:], (xin if first else xout)[i * P:(i + 1) * P, :],
                  q="sp" if i % 2 else "pool")
        for i in range(i0, i0 + 4):
            xres = xres_t[i]
            for j in (0, 512):
                ps = B.pb.nxt()
                p.mm_group([(ps[:, :512], xt_chunk(B.A2, c, i * P, P),
                             B.A3[:, c * 1024 + j: c * 1024 + j + 512],
                             c == 0, c == 7, None) for c in range(8)],
                           reads=[B.A2[:], B.A3[:]], writes=[ps[:]])
                op_tt(p, xres[:, j:j + 512], xres[:, j:j + 512], ps[:, :512],
                      ALU.add)
                op_tt(p, xres[:, j:j + 512], xres[:, j:j + 512],
                      B.bbc[:, j:j + 512], ALU.add)
            p.dma(xout[i * P:(i + 1) * P, :], xres[:], q="pool")
        p.barrier()



L1_TAPS = 20
L1_HALO = L1_TAPS + 2
QTL = 1024


def emit_l1attn(p, B, xin, xout, first):
    QT = 1024
    cw = p.sbuf("l1_cw", [P, 48])
    cb = p.sbuf("l1_cb", [P, 16])
    inb = p.sbuf("l1_inb", [P, 48])
    dp = p.sbuf("l1_dp", [P, 16])
    onesr = p._sbufs.get("l0_onesr") or p.sbuf("l0_onesr", [1, P], BF16)
    grow = p.sbuf("l1_grow", [1, D], BF16)
    u2 = p.sbuf("l1_u2", [P, QTL + L1_HALO + 2], BF16)
    p.dma(cw[:], B.din("w_l1_convw"))
    p.dma(cb[:], B.din("w_l1_convb"))
    p.dma(inb[:], B.din("w_l1_inb"))
    p.dma(dp[:], B.din("w_l1_dp"))
    p.dma(onesr[:], B.din("c_onesr"))
    gr = B.ft.nxt()
    p.dma(gr[0:1, :], B.din("w_l1_gamma"))
    br = B.ft.nxt()
    p.dma(br[0:1, :], B.din("w_l1_beta"))
    op_copy(p, grow[:], gr[0:1, :])
    p.barrier()
    for j in (0, 512):
        psb = B.pb.nxt()
        p.mm_group([(psb[:, :512], onesr[:], grow[:, j:j + 512],
                     True, True, None)], reads=[onesr[:], grow[:]],
                   writes=[psb[:]])
        op_copy(p, B.gbc[:, j:j + 512], psb[:, :512], eng="act")
    op_copy(p, grow[:], br[0:1, :])
    p.barrier()
    for j in (0, 512):
        psb = B.pb.nxt()
        p.mm_group([(psb[:, :512], onesr[:], grow[:, j:j + 512],
                     True, True, None)], reads=[onesr[:], grow[:]],
                   writes=[psb[:]])
        op_copy(p, B.bbc[:, j:j + 512], psb[:, :512], eng="act")
    p.barrier()
    fA = B.ft.nxt(); fB = B.ft.nxt(); fC = B.ft.nxt(); fE = B.ft.nxt()
    p.dma(fA[:], B.din("w_l1_alog"))
    p.dma(fB[:], B.din("w_l1_bp"))
    p.dma(fC[:], B.din("w_l1_cp"))
    op_act(p, fA[:], fA[:], AF.Exp)
    op_ts(p, fA[:], fA[:], -1.0, ALU.mult)
    op_act(p, fA[:], fA[:], AF.Exp)
    op_tt(p, fB[:], fB[:], fC[:], ALU.mult)
    op_copy(p, fE[:], fB[:])
    g1v = B.g1[:, 0:16 * L1_TAPS].rearrange("p (c d) -> p c d", d=L1_TAPS)
    for d in range(L1_TAPS):
        p.op("dve",
             lambda e, d=d: e.tensor_reduce(
                 g1v[:, :, d:d + 1],
                 fE[:].rearrange("p (c s) -> p c s", s=64), AX.X, ALU.add),
             reads=[fE[:]], writes=[B.g1[:]])
        if d < L1_TAPS - 1:
            op_tt(p, fE[:], fE[:], fA[:], ALU.mult)
    p.op("dve", lambda e: e.tensor_tensor(g1v[:, :, 0], g1v[:, :, 0],
                                          dp[:], ALU.add),
         reads=[B.g1[:], dp[:]], writes=[B.g1[:]])
    p.barrier()
    A2v = B.A2[:].rearrange("p (c t) -> p c t", t=T)
    for i0 in range(0, NT, 1):
        tiles = range(i0, min(i0 + 1, NT))
        fts = {}
        for i in tiles:
            fts[i] = B.ft.nxt()
            p.dma(fts[i][:], xin[i * P:(i + 1) * P, :])
        for i in tiles:
            xt = fts[i]
            xc = B.ft.nxt()
            m = B.col.nxt(); ss = B.col.nxt(); rs = B.col.nxt()
            op_red(p, m[:], xt[:], ALU.add)
            op_ts(p, m[:], m[:], 1.0 / D, ALU.mult)
            op_ts(p, xc[:], xt[:], m[:], ALU.subtract)
            op_act(p, xt[:], xc[:], AF.Square, accum_out=ss[:])
            op_ts(p, ss[:], ss[:], 1.0 / D, ALU.mult)
            op_act(p, ss[:], ss[:], AF.Ln, bias=B.epscol[:])
            op_act(p, rs[:], ss[:], AF.Exp, scale=-0.5)
            op_ts(p, xc[:], xc[:], rs[:], ALU.mult)
            m2 = B.col.nxt(); ss2 = B.col.nxt(); rs2 = B.col.nxt()
            op_red(p, m2[:], xc[:], ALU.add)
            op_ts(p, m2[:], m2[:], 1.0 / D, ALU.mult)
            op_ts(p, xt[:], xc[:], m2[:], ALU.subtract)
            op_act(p, xc[:], xt[:], AF.Square, accum_out=ss2[:])
            op_ts(p, ss2[:], ss2[:], 1.0 / D, ALU.mult)
            op_act(p, ss2[:], ss2[:], AF.Sqrt, bias=B.epscol[:])
            p.op("dve", lambda e, a=rs2, b=ss2: e.reciprocal(a[:], b[:]),
                 reads=[ss2[:]], writes=[rs2[:]])
            op_ts(p, xt[:], xt[:], rs2[:], ALU.mult)
            op_tt(p, xt[:], xt[:], B.gbc[:], ALU.mult)
            xnb = B.xb.nxt()
            op_tt(p, xnb[:], xt[:], B.bbc[:], ALU.add)
            ptb = B.pt.nxt()
            op_multi(p, "pe",
                     [(lambda e, c=c, xnb=xnb, ptb=ptb:
                       e.transpose(ptb[:, c * P:(c + 1) * P],
                                   xnb[:, c * P:(c + 1) * P], B.ident[:]))
                      for c in range(8)],
                     reads=[xnb[:], B.ident[:]], writes=[ptb[:]])
            op_copy(p, A2v[:, :, i * P:(i + 1) * P],
                    ptb[:].rearrange("p (c t) -> p c t", t=P))
        p.barrier()
    emit_load_wT(p, B, "w_l1_outw", B.XT, 0, 2048, 1024)
    ZOF = 16 * 1024
    inw = B.din("w_l1_inw")
    for q in range(4):
        t0 = q * QT
        h0 = max(0, t0 - L1_HALO)
        hw = t0 + QT - h0
        for c in range(16):
            wslc = {}
            for bi, cc in ((0, c), (1, 16 + c), (2, 32 + c)):
                stg = B.ft.nxt()
                wslc[bi] = stg
                qq = "sp" if bi % 2 else "pool"
                for k in range(8):
                    p.dma(stg[:, k * 128:(k + 1) * 128],
                          inw[k * P:(k + 1) * P, cc * 128:(cc + 1) * 128],
                          q=qq, no_track=True)
                p.mark_load(stg[:], qq)
            wu = B.A3[:, 4096:5120]
            wv = B.A3[:, 5120:6144]
            wg = B.A3[:, 6144:7168]
            op_copy(p, wu, wslc[0][:])
            op_copy(p, wv, wslc[1][:], eng="act")
            op_copy(p, wg, wslc[2][:])
            rext = 1 if t0 + QT < T else 0
            uwidth = hw + rext
            nsub = ceil_div(uwidth, 512)
            ups = []
            for s in range(nsub):
                n0 = h0 + s * 512
                nw = min(512, h0 + uwidth - n0)
                ps = B.pb.nxt()
                p.mm_group([(ps[:, :nw], wu[:, k * 128:(k + 1) * 128],
                             xt_chunk(B.A2, k, n0, nw), k == 0, k == 7, None)
                            for k in range(8)],
                           reads=[B.A3[:], B.A2[:]], writes=[ps[:]])
                ups.append((ps, n0, nw))
            for (ps, n0, nw) in ups:
                op_ts(p, B.A3[:, 2048 + (n0 - h0): 2048 + (n0 - h0) + nw],
                      ps[:, :nw], inb[:, c:c + 1], ALU.add)
            ucv = B.A3[:, 2048:2048 + uwidth]
            tmp = u2
            op_ts(p, tmp[:, :hw], ucv[:, 0:hw], cw[:, 16 + c:17 + c],
                  ALU.mult)
            if h0 == 0:
                op_stt(p, tmp[:, 1:hw], ucv[:, 0:hw - 1], cw[:, c:c + 1],
                       tmp[:, 1:hw], ALU.mult, ALU.add)
            else:
                op_stt(p, tmp[:, 1:hw], ucv[:, 0:hw - 1], cw[:, c:c + 1],
                       tmp[:, 1:hw], ALU.mult, ALU.add)
            if rext:
                op_stt(p, tmp[:, 0:hw], ucv[:, 1:hw + 1],
                       cw[:, 32 + c:33 + c], tmp[:, 0:hw], ALU.mult, ALU.add)
            else:
                op_stt(p, tmp[:, 0:hw - 1], ucv[:, 1:hw],
                       cw[:, 32 + c:33 + c], tmp[:, 0:hw - 1],
                       ALU.mult, ALU.add)
            up = B.A3[:, 0:uwidth]
            op_act(p, up[:, :hw], tmp[:, :hw], AF.Silu, bias=cb[:, c:c + 1])
            ofs = t0 - h0
            yg = [B.xb.nxt(), B.xb.nxt()]
            NG = [(0, 10, "dve"), (10, L1_TAPS, "dve")]
            for gi, (d0, d1, eng) in enumerate(NG):
                g = yg[gi]
                for d in range(d0, d1):
                    lo = ofs - d
                    gc = B.g1[:, c * L1_TAPS + d: c * L1_TAPS + d + 1]
                    if lo >= 0:
                        if d == d0:
                            op_ts(p, g[:, :QT], up[:, lo: lo + QT], gc,
                                  ALU.mult, eng=eng)
                        else:
                            op_stt(p, g[:, :QT], up[:, lo: lo + QT], gc,
                                   g[:, :QT], ALU.mult, ALU.add, eng=eng)
                    else:
                        sk = -lo
                        if d == d0:
                            op_memset(p, g[:, 0:sk], 0.0)
                            if sk < QT:
                                op_ts(p, g[:, sk:QT], up[:, 0: QT - sk], gc,
                                      ALU.mult, eng=eng)
                        elif sk < QT:
                            op_stt(p, g[:, sk:QT], up[:, 0: QT - sk], gc,
                                   g[:, sk:QT], ALU.mult, ALU.add, eng=eng)
            for (wsl, act, dst_ofs, bc) in ((wv, AF.Silu, 0, 16 + c),
                                            (wg, AF.Sigmoid, 1024, 32 + c)):
                for s in range(2):
                    n0 = t0 + s * 512
                    ps = B.pb.nxt()
                    p.mm_group([(ps[:, :512], wsl[:, k * 128:(k + 1) * 128],
                                 xt_chunk(B.A2, k, n0, 512),
                                 k == 0, k == 7, None) for k in range(8)],
                               reads=[B.A3[:], B.A2[:]], writes=[ps[:]])
                    op_act(p, B.A3[:, 2048 + dst_ofs + s * 512:
                                   2048 + dst_ofs + s * 512 + 512],
                           ps[:, :512], act, bias=inb[:, bc:bc + 1])
            zf = B.xb.nxt_not(yg)
            for s in range(2):
                zs = zf[:, s * 512:(s + 1) * 512]
                op_tt(p, zs, yg[0][:, s * 512:(s + 1) * 512],
                      yg[1][:, s * 512:(s + 1) * 512], ALU.add)
                op_tt(p, zs, zs,
                      B.A3[:, 2048 + 1024 + s * 512: 2048 + 1024 + s * 512 + 512],
                      ALU.mult)
                op_tt(p, B.XT[:, ZOF + c * QT + s * 512:
                              ZOF + c * QT + s * 512 + 512], zs,
                      B.A3[:, 2048 + s * 512: 2048 + s * 512 + 512], ALU.add)
            p.barrier()
        for it0 in range(0, QT // P, 4):
            xres_t = {}
            for it in range(it0, it0 + 4):
                gi = t0 // P + it
                xres_t[it] = B.ft.nxt()
                p.dma(xres_t[it][:],
                      (xin if first else xout)[gi * P:(gi + 1) * P, :],
                      q="sp" if it % 2 else "pool")
            for it in range(it0, it0 + 4):
                gi = t0 // P + it
                xres = xres_t[it]
                for j in (0, 512):
                    ps = B.pb.nxt()
                    p.mm_group([(ps[:, :512],
                                 B.XT[:, ZOF + cc * QT + it * P:
                                      ZOF + cc * QT + (it + 1) * P],
                                 B.XT[:, cc * 1024 + j: cc * 1024 + j + 512],
                                 cc == 0, cc == 15, None)
                                for cc in range(16)],
                               reads=[B.XT[:]], writes=[ps[:]])
                    op_tt(p, xres[:, j:j + 512], xres[:, j:j + 512],
                          ps[:, :512], ALU.add)
                p.dma(xout[gi * P:(gi + 1) * P, :], xres[:], q="pool")
            p.barrier()
    ob = B.ft.nxt()
    p.dma(ob[0:1, :], B.din("w_l1_outb"))
    op_copy(p, grow[:], ob[0:1, :])
    p.barrier()
    for j in (0, 512):
        psb = B.pb.nxt()
        p.mm_group([(psb[:, :512], onesr[:], grow[:, j:j + 512],
                     True, True, None)], reads=[onesr[:], grow[:]],
                   writes=[psb[:]])
        op_copy(p, B.bbc[:, j:j + 512], psb[:, :512], eng="act")
    p.barrier()
    for i0 in range(0, NT, 2):
        xres_t = {}
        xin_t = {}
        for i in range(i0, i0 + 2):
            xres_t[i] = B.ft.nxt()
            p.dma(xres_t[i][:], xout[i * P:(i + 1) * P, :])
            xin_t[i] = B.ft.nxt()
            p.dma(xin_t[i][:], xin[i * P:(i + 1) * P, :], q="pool")
        for i in range(i0, i0 + 2):
            xres = xres_t[i]
            xt = xin_t[i]
            m = B.col.nxt(); ss = B.col.nxt(); rs = B.col.nxt()
            op_red(p, m[:], xt[:], ALU.add)
            op_ts(p, m[:], m[:], 1.0 / D, ALU.mult)
            op_ts(p, xt[:], xt[:], m[:], ALU.subtract)
            sq = B.fs.nxt()
            op_act(p, sq[:, :512], xt[:, :512], AF.Square, accum_out=ss[:])
            ss2 = B.col.nxt()
            op_act(p, sq[:, :512], xt[:, 512:], AF.Square, accum_out=ss2[:])
            op_tt(p, ss[:], ss[:], ss2[:], ALU.add)
            op_ts(p, ss[:], ss[:], 1.0 / D, ALU.mult)
            op_act(p, ss[:], ss[:], AF.Ln, bias=B.epscol[:])
            op_act(p, rs[:], ss[:], AF.Exp, scale=-0.5)
            op_ts(p, xt[:], xt[:], rs[:], ALU.mult)
            op_tt(p, xres[:], xres[:], xt[:], ALU.add)
            op_tt(p, xres[:], xres[:], B.bbc[:], ALU.add)
            p.dma(xout[i * P:(i + 1) * P, :], xres[:], q="pool")
        p.barrier()



def emit_quant(p, B, out_bf, src, W1, W2, W3, scl, rsc, F):
    G = F // 16
    s3 = src.rearrange("p (g k) -> p g k", k=16)
    p.op("dve", lambda e: e.tensor_reduce(
        scl.rearrange("p (g o) -> p g o", o=1), s3, AX.X, ALU.max,
        apply_absolute_value=True), reads=[src], writes=[scl])
    op_ts(p, scl, scl, 1.0 / 6.0, ALU.mult, 1e-12, ALU.max)
    p.op("dve", lambda e: e.reciprocal(rsc, scl), reads=[scl], writes=[rsc])
    op_act(p, W1, src, AF.Abs)
    rb = rsc.rearrange("p (g o) -> p g o", o=1)[:, :G].broadcast_to((P, G, 16))
    w13 = W1.rearrange("p (g k) -> p g k", k=16)
    p.op("dve", lambda e: e.tensor_tensor(w13, w13, rb, ALU.mult),
         reads=[W1, rsc], writes=[W1])
    op_ts(p, W2, W1, 0.25, ALU.is_gt)
    for m in (0.75, 1.25, 1.75):
        op_stt(p, W2, W1, m, W2, ALU.is_gt, ALU.add)
    op_ts(p, W3, W1, 2.5, ALU.is_gt)
    for m in (3.5, 5.0, 5.0):
        op_stt(p, W3, W1, m, W3, ALU.is_gt, ALU.add)
    op_stt(p, W2, W2, 0.5, W3, ALU.mult, ALU.add)
    op_act(p, W1, src, AF.Sign)
    op_tt(p, W2, W2, W1, ALU.mult)
    sb = scl.rearrange("p (g o) -> p g o", o=1)[:, :G].broadcast_to((P, G, 16))
    o3 = out_bf.rearrange("p (g k) -> p g k", k=16)
    w23 = W2.rearrange("p (g k) -> p g k", k=16)
    p.op("dve", lambda e: e.tensor_tensor(o3, w23, sb, ALU.mult),
         reads=[W2, scl], writes=[out_bf])


L3F_PARTS = [(0, 768), (768, 768), (1536, 768), (2304, 512)]


def emit_l3ffn(p, B, xin, xout, first):
    qsc = B.g1[:, 0:64]
    qrs = B.g1[:, 64:128]
    v2f = p._sbufs.get("l0_v2") or p.sbuf("l0_v2", [P, 2 * 32 * 65], BF16)
    qa = v2f[:, 0:1024]
    qb = v2f[:, 1024:2048]
    hqt = v2f[:, 2048:2816]
    XTv = B.XT[:].rearrange("p (c t) -> p c t", t=T)
    for i0 in range(0, NT, 2):
        tiles = range(i0, min(i0 + 2, NT))
        fts = {}
        for i in tiles:
            fts[i] = B.ft.nxt()
            p.dma(fts[i][:], xin[i * P:(i + 1) * P, :])
        for i in tiles:
            xt = fts[i]
            xc = B.ft.nxt_not(list(fts.values()))
            m = B.col.nxt(); ss = B.col.nxt(); rs = B.col.nxt()
            op_red(p, m[:], xt[:], ALU.add)
            op_ts(p, m[:], m[:], 1.0 / D, ALU.mult)
            op_ts(p, xc[:], xt[:], m[:], ALU.subtract)
            op_act(p, xt[:], xc[:], AF.Square, accum_out=ss[:])
            op_ts(p, ss[:], ss[:], 1.0 / D, ALU.mult)
            op_act(p, ss[:], ss[:], AF.Ln, bias=B.epscol[:])
            op_act(p, rs[:], ss[:], AF.Exp, scale=-0.5)
            op_ts(p, xc[:], xc[:], rs[:], ALU.mult)
            xnb = B.xb.nxt()
            w2 = B.ft.nxt_not([xt, xc] + list(fts.values()))
            emit_quant(p, B, xnb[:], xc[:], xt[:], w2[:], qa, qsc, qrs, 1024)
            ptb = B.pt.nxt()
            op_multi(p, "pe",
                     [(lambda e, c=c, xnb=xnb, ptb=ptb:
                       e.transpose(ptb[:, c * P:(c + 1) * P],
                                   xnb[:, c * P:(c + 1) * P], B.ident[:]))
                      for c in range(8)],
                     reads=[xnb[:], B.ident[:]], writes=[ptb[:]])
            op_copy(p, XTv[:, :, i * P:(i + 1) * P],
                    ptb[:].rearrange("p (c t) -> p c t", t=P))
        p.barrier()
    for (i0p, ipw) in L3F_PARTS:
        nrc = ipw // 128
        for (wname, aofs) in (("w_l3_gate", 0), ("w_l3_up", 8 * ipw)):
            for r in range(nrc):
                stg = B.ft.nxt()
                p.dma(stg[:], B.din(wname)[i0p + r * P: i0p + (r + 1) * P, :])
                p.barrier()
                w2 = B.ft.nxt(); w3 = B.ft.nxt()
                emit_quant(p, B, qa, stg[:], w2[:], w3[:], qb, qsc, qrs, 1024)
                ptb = B.pt.nxt()
                op_multi(p, "pe",
                         [(lambda e, c=c, ptb=ptb:
                           e.transpose(ptb[:, c * P:(c + 1) * P],
                                       qa[:, c * P:(c + 1) * P], B.ident[:]))
                          for c in range(8)],
                         reads=[qa, B.ident[:]], writes=[ptb[:]])
                for c in range(8):
                    op_copy(p, B.A2[:, aofs + c * ipw + r * P:
                                    aofs + c * ipw + (r + 1) * P],
                            ptb[:, c * P:(c + 1) * P],
                            eng="act" if c % 2 else "dve")
                p.barrier()
        WDOF = 16 * ipw
        for r in range(8):
            stg = B.ft.nxt()
            p.dma(stg[:, :ipw],
                  B.din("w_l3_down")[r * P:(r + 1) * P, i0p:i0p + ipw])
            p.barrier()
            w2 = B.ft.nxt(); w3 = B.ft.nxt()
            emit_quant(p, B, qa[:, :ipw], stg[:, :ipw], w2[:, :ipw],
                       w3[:, :ipw], qb[:, :ipw], qsc[:, :ipw // 16], qrs[:, :ipw // 16], ipw)
            ptb = B.pt.nxt()
            op_multi(p, "pe",
                     [(lambda e, c=c, ptb=ptb:
                       e.transpose(ptb[:, c * P:(c + 1) * P],
                                   qa[:, c * P:(c + 1) * P], B.ident[:]))
                      for c in range(nrc)],
                     reads=[qa, B.ident[:]], writes=[ptb[:]])
            for c in range(nrc):
                op_copy(p, B.A2[:, WDOF + c * 1024 + r * P:
                                WDOF + c * 1024 + (r + 1) * P],
                        ptb[:, c * P:(c + 1) * P],
                        eng="act" if c % 2 else "dve")
            p.barrier()
        for i0 in range(0, NT, 1):
            xres_t = {}
            for i in range(i0, i0 + 1):
                xres_t[i] = B.ft.nxt()
                p.dma(xres_t[i][:],
                      (xin if first and i0p == 0 else xout)
                      [i * P:(i + 1) * P, :])
            for i in range(i0, i0 + 1):
                evs = {}
                for (aofs, nm) in ((0, "g"), (8 * ipw, "u")):
                    pss = []
                    for n0 in range(0, ipw, 512):
                        nw = min(512, ipw - n0)
                        ps = B.pb.nxt()
                        p.mm_group(
                            [(ps[:, :nw], xt_chunk(B.XT, c, i * P, P),
                              B.A2[:, aofs + c * ipw + n0:
                                   aofs + c * ipw + n0 + nw],
                              c == 0, c == 7, None) for c in range(8)],
                            reads=[B.XT[:], B.A2[:]], writes=[ps[:]])
                        pss.append((ps, n0, nw))
                    evs[nm] = pss
                fg = B.ft.nxt_not(list(xres_t.values()))
                fu = B.ft.nxt_not(list(xres_t.values()) + [fg])
                for (ps, n0, nw) in evs["g"]:
                    op_act(p, fg[:, n0:n0 + nw], ps[:, :nw], AF.Silu)
                for (ps, n0, nw) in evs["u"]:
                    op_copy(p, fu[:, n0:n0 + nw], ps[:, :nw], eng="act")
                op_tt(p, fg[:, :ipw], fg[:, :ipw], fu[:, :ipw], ALU.mult)
                w3 = B.ft.nxt_not(list(xres_t.values()) + [fg, fu])
                emit_quant(p, B, qa[:, :ipw], fg[:, :ipw], fu[:, :ipw],
                           w3[:, :ipw], qb[:, :ipw], qsc[:, :ipw // 16],
                           qrs[:, :ipw // 16], ipw)
                ptb = B.pt.nxt()
                op_multi(p, "pe",
                         [(lambda e, c=c, ptb=ptb:
                           e.transpose(ptb[:, c * P:(c + 1) * P],
                                       qa[:, c * P:(c + 1) * P], B.ident[:]))
                          for c in range(nrc)],
                         reads=[qa, B.ident[:]], writes=[ptb[:]])
                op_copy(p, hqt[:, :nrc * P],
                        ptb[:, :nrc * P])
                xres = xres_t[i]
                for j in (0, 512):
                    ps = B.pb.nxt()
                    p.mm_group(
                        [(ps[:, :512], hqt[:, c * P:(c + 1) * P],
                          B.A2[:, WDOF + c * 1024 + j:
                               WDOF + c * 1024 + j + 512],
                          c == 0, c == nrc - 1, None) for c in range(nrc)],
                        reads=[hqt, B.A2[:]], writes=[ps[:]])
                    op_tt(p, xres[:, j:j + 512], xres[:, j:j + 512],
                          ps[:, :512], ALU.add)
                p.dma(xout[i * P:(i + 1) * P, :], xres[:], q="pool")
            p.barrier()



def emit_l0ffn(p, B, xin, xout, first):
    v2 = p._sbufs.get("l0_v2") or p.sbuf("l0_v2", [P, 2 * 32 * 65], BF16)
    e8c = p.sbuf("moe_e8", [8, 512], BF16)
    cf8b = B.sh1[0:8, :]
    p.dma(e8c[:], B.din("c_e8"))
    p.barrier()
    emit_load_wT(p, B, "w_l0_w1aT", B.A3, 0, D, 512)
    for erc in range(4):
        for tq in range(8):
            ps = B.pb.nxt()
            p.mm_group([(ps[:, :512],
                         B.A3[:, c * 512 + erc * P: c * 512 + (erc + 1) * P],
                         xt_chunk(B.XT, c, tq * 512, 512),
                         c == 0, c == 7, None) for c in range(8)],
                       reads=[B.A3[:], B.XT[:]], writes=[ps[:]])
            op_copy(p, B.A2[:, erc * T + tq * 512: erc * T + (tq + 1) * 512],
                    ps[:, :512], eng="act")
        p.barrier()
    w1b = B.din("w_l0_w1b")
    w2aT = B.din("w_l0_w2aT")
    w2b = B.din("w_l0_w2b")
    for pr in range(4):
        for piece in range(4):
            stg = B.ft.nxt()
            p.dma(stg[:], w1b[pr * P:(pr + 1) * P,
                              piece * 1024:(piece + 1) * 1024],
                  q="sp" if piece % 2 else "pool")
            op_copy(p, B.A2[:, 16384 + pr * 4096 + piece * 1024:
                            16384 + pr * 4096 + (piece + 1) * 1024],
                    stg[:], eng="dve" if piece % 2 else "act")
            if piece % 2 == 1:
                p.barrier()
    p.barrier()
    w2aT_r = w2aT.rearrange("(f p) e -> p f e", p=P)
    for pr in range(4):
        for fg in range(4):
            stg = B.ft.nxt()
            p.dma(stg[:].rearrange("p (f e) -> p f e", e=128),
                  w2aT_r[:, fg * 8:(fg + 1) * 8, pr * 128:(pr + 1) * 128],
                  q="sp" if fg % 2 else "pool")
            op_copy(p, B.XT[:, pr * 4096 + fg * 1024:
                            pr * 4096 + (fg + 1) * 1024],
                    stg[:], eng="dve" if fg % 2 else "act")
            if fg % 2 == 1:
                p.barrier()
    for ee in range(8):
        stg = B.ft.nxt()
        p.dma(stg[0:64, :], w2b[ee * 64:(ee + 1) * 64, :],
              q="sp" if ee % 2 else "pool")
        op_copy(p, B.XT[0:64, 16384 + ee * 1024: 16384 + (ee + 1) * 1024],
                stg[0:64, :], eng="act" if ee % 2 else "dve")
        if ee % 2 == 1:
            p.barrier()
    p.barrier()
    coeffT = B.din("l0_coeffT")
    for tq in range(8):
        xres_t = {}
        for ts in range(4):
            xres_t[ts] = B.ft.items[ts]
            p.dma(xres_t[ts][:],
                  (xin if first else xout)[(tq * 4 + ts) * P:
                                           (tq * 4 + ts + 1) * P, :])
        cf8f = B.fs.nxt()
        p.dma(cf8f[0:8, :], coeffT[:, tq * 512:(tq + 1) * 512], q="pool")
        op_copy(p, cf8b, cf8f[0:8, :])
        for pr in range(4):
            psH3A = B.pb.items[2]; psH3B = B.pb.items[3]
            psH2 = [(B.pb.items[0], B.pb.items[1]),
                    (B.pb.items[4], B.pb.items[5])]

            def emit_H2(f):
                pA, pB = psH2[f % 2]
                p.mm_group(
                    [(pA[:, :512],
                      B.A2[0:64, 16384 + pr * 4096 + f * 128:
                           16384 + pr * 4096 + f * 128 + 128],
                      B.A2[0:64, pr * T + tq * 512: pr * T + (tq + 1) * 512],
                      True, True, (0, 0)),
                     (pB[:, :512],
                      B.A2[64:128, 16384 + pr * 4096 + f * 128:
                           16384 + pr * 4096 + f * 128 + 128],
                      B.A2[64:128, pr * T + tq * 512: pr * T + (tq + 1) * 512],
                      True, True, (64, 0))],
                    reads=[B.A2[:]], writes=[pA[:], pB[:]])
            emit_H2(0)
            emit_H2(1)
            for f in range(32):
                pA, pB = psH2[f % 2]
                fsA = B.fs.items[(2 * f) % 4]
                fsB = B.fs.items[(2 * f + 1) % 4]
                op_copy(p, fsA[:, :512], pA[:, :512], eng="dve")
                op_copy(p, fsB[:, :512], pB[:, :512], eng="dve")
                gx = B.xb.items[f % 3]
                ga = gx[:, :512]
                gb = gx[:, 512:1024]
                op_act(p, ga, fsA[:, :512], AF.Gelu)
                op_act(p, gb, fsB[:, :512], AF.Gelu)
                if f + 2 < 32:
                    emit_H2(f + 2)
                p.mm_group([(psH3A[0:64, :512],
                             B.XT[:, pr * 4096 + f * 128: pr * 4096 + f * 128 + 64],
                             ga, f == 0, f == 31, None)],
                           reads=[B.XT[:], gx[:]], writes=[psH3A[:]])
                p.mm_group([(psH3B[0:64, :512],
                             B.XT[:, pr * 4096 + f * 128 + 64:
                                  pr * 4096 + f * 128 + 128],
                             gb, f == 0, f == 31, None)],
                           reads=[B.XT[:], gx[:]], writes=[psH3B[:]])
            psBC = B.pb.items[4]
            rotY = Rot([B.pb.items[5], B.pb.items[0], B.pb.items[1]])
            for (ee, psH3) in ((2 * pr, psH3A), (2 * pr + 1, psH3B)):
                vof = 1024 * (ee % 2)
                p.mm_group([(psBC[0:64, :512],
                             e8c[:, ee * 64:(ee + 1) * 64],
                             cf8b, True, True, None)],
                           reads=[e8c[:], B.sh1[:]], writes=[psBC[:]])
                op_copy(p, v2[0:64, vof:vof + 512], psBC[0:64, :512],
                        eng="act")
                p.op("dve",
                     lambda e, psH3=psH3, vof=vof:
                     e.tensor_tensor(v2[0:64, vof + 512:vof + 1024],
                                     psH3[0:64, :512], v2[0:64, vof:vof + 512],
                                     ALU.mult),
                     reads=[psH3[:], v2[:]], writes=[v2[:]])
                for ts in range(4):
                    for dc in (0, 512):
                        psY = rotY.nxt()
                        p.mm_group(
                            [(psY[:, :512],
                              v2[0:64, vof + 512 + ts * 128:
                                 vof + 512 + (ts + 1) * 128],
                              B.XT[0:64, 16384 + ee * 1024 + dc:
                                   16384 + ee * 1024 + dc + 512],
                              True, True, None)],
                            reads=[v2[:], B.XT[:]], writes=[psY[:]])
                        op_tt(p, xres_t[ts][:, dc:dc + 512],
                              xres_t[ts][:, dc:dc + 512], psY[:, :512],
                              ALU.add)
        for ts in range(4):
            p.dma(xout[(tq * 4 + ts) * P:(tq * 4 + ts + 1) * P, :],
                  xres_t[ts][:], q="pool")
        p.barrier()



def emit_l3attn(p, B, xin, xout, first):
    lam = B.sh1
    wdl = p.sbuf("l3_wdl", [P, 96], BF16)
    wdt = p.sbuf("l3_wdt", [P, 128], BF16)
    bdc = p.sbuf("l3_bdc", [P, 33])
    bd = bdc[:, 0:28]
    bcc = bdc[:, 28:32]
    l3sc = p.sbuf("l3_sc", [1, 24])
    carry = l3sc[0:1, 0:16]
    csav = l3sc[0:1, 16:18]
    nrf = l3sc[0:1, 18:22]
    f1r = p._sbufs.get("l0_lb16") or p.sbuf("l0_lb16", [1, 512], BF16)
    f2r = f1r
    g10s = p.sbuf("l3_g10s", [64, 512], BF16)
    g11s = p.sbuf("l3_g11s", [64, 512], BF16)
    onecol = bdc[:, 32:33]
    identf = p.sbuf("c_identf_t", [P, P])
    onesrf = p.sbuf("c_onesrf_t", [1, P])
    trilneg = p.sbuf("c_trilneg_t", [P, P], BF16)
    triu1f = p.sbuf("c_triu1f_t", [P, P])
    ehs = p.sbuf("c_ehs_t", [8, P], BF16)
    had = p.sbuf("c_had_t", [P, P], BF16)
    e12 = p.sbuf("c_e12_t", [12, 256], BF16)
    v2 = p._sbufs.get("l0_v2") or p.sbuf("l0_v2", [P, 2 * 32 * 65], BF16)
    onesr = p._sbufs.get("l0_onesr") or p.sbuf("l0_onesr", [1, P], BF16)
    lrow = p._sbufs.get("l0_lrow") or p.sbuf("l0_lrow", [1, 512])
    for (t, nm) in ((identf, "c_identf"), (onesrf, "c_onesrf"),
                    (trilneg, "c_trilneg"), (triu1f, "c_triu1f"),
                    (ehs, "c_ehs"), (had, "c_had"), (e12, "c_e12")):
        p.dma(t[:], B.din(nm))
    p.dma(onesr[:], B.din("c_onesr"), q="pool")
    op_memset(p, onecol, 1.0)
    p.barrier()
    emit_load_wT(p, B, "w_l3_dtT", wdt, 0, D, 16, nstride=16)
    arow = B.fs.nxt()
    p.dma(arow[0:1, 0:16], B.din("w_l3_alogR"))
    dbr = B.fs.nxt()
    p.dma(dbr[0:1, 0:16], B.din("w_l3_dtbR"))
    op_act(p, arow[0:1, 0:16], arow[0:1, 0:16], AF.Exp)
    op_ts(p, arow[0:1, 0:16], arow[0:1, 0:16], -1.0, ALU.mult)
    abc = B.fs.nxt(); dbc = B.fs.nxt()
    ps = B.pb.items[5]
    p.mm_group([(ps[:, 0:16], onesrf[:], arow[0:1, 0:16], True, True, None)],
               reads=[onesrf[:], arow[:]], writes=[ps[:]])
    op_copy(p, abc[:, 0:16], ps[:, 0:16], eng="act")
    p.mm_group([(ps[:, 0:16], onesrf[:], dbr[0:1, 0:16], True, True, None)],
               reads=[onesrf[:], dbr[:]], writes=[ps[:]])
    op_copy(p, dbc[:, 0:16], ps[:, 0:16], eng="act")
    op_memset(p, carry, 0.0)
    p.barrier()
    for i in range(NT):
        ps = B.pb.items[5]
        p.mm_group([(ps[:, 0:16], xt_chunk(B.XT, c, i * P, P),
                     wdt[:, c * 16:(c + 1) * 16], c == 0, c == 7, None)
                    for c in range(8)], reads=[wdt[:], B.XT[:]],
                   writes=[ps[:]])
        t0f = B.fs.nxt_not([abc, dbc])
        t1f = B.fs.nxt_not([abc, dbc, t0f])
        op_tt(p, t0f[:, 0:16], ps[:, 0:16], dbc[:, 0:16], ALU.add)
        op_act(p, t1f[:, 0:16], t0f[:, 0:16], AF.Abs)
        op_ts(p, t1f[:, 0:16], t1f[:, 0:16], -1.0, ALU.mult)
        op_act(p, t1f[:, 0:16], t1f[:, 0:16], AF.Exp)
        op_act(p, t1f[:, 0:16], t1f[:, 0:16], AF.Ln, bias=onecol)
        op_act(p, t0f[:, 0:16], t0f[:, 0:16], AF.Relu)
        op_tt(p, t0f[:, 0:16], t0f[:, 0:16], t1f[:, 0:16], ALU.add)
        op_tt(p, t0f[:, 0:16], t0f[:, 0:16], abc[:, 0:16], ALU.mult)
        pc = B.pb.items[4]
        p.mm_group([(pc[:, 0:16], triu1f[:], t0f[:, 0:16], True, False, None),
                    (pc[:, 0:16], onesrf[:], carry, False, True,
                     None)],
                   reads=[triu1f[:], t0f[:], onesrf[:], l3sc[:]],
                   writes=[pc[:]])
        op_copy(p, B.g1[:, i * 16:(i + 1) * 16], pc[:, 0:16])
        p.mm_group([(ps[0:1, 0:16], onecol[:], t0f[:, 0:16], True, True,
                     None)], reads=[onecol[:], t0f[:]], writes=[ps[:]])
        p.op("dve", lambda e, ps=ps: e.tensor_tensor(
            carry, carry, ps[0:1, 0:16], ALU.add),
            reads=[l3sc[:], ps[:]], writes=[l3sc[:]])
        if i % 4 == 3:
            p.barrier()
    p.barrier()
    G8, G9, G10, G11, Gd, WRK = (B.pb.items[0], B.pb.items[1], B.pb.items[2],
                                 B.pb.items[3], B.pb.items[4], B.pb.items[5])
    lpall = p.sbuf("l3_lpall", [12, 16])
    p.dma(lpall[:], B.din("w_l3_lpT"))
    p.barrier()
    lpc = B.fs.nxt()
    for h in range(16):
        wq_t = B.xb.items[0]; wk_t = B.xb.items[1]; wv_t = B.xb.items[2]
        for (wname, wt) in (("w_l3_qT", wq_t), ("w_l3_kT", wk_t),
                            ("w_l3_vT", wv_t)):
            for g4 in range(2):
                stgs = []
                for c in range(4 * g4, 4 * g4 + 4):
                    stg = B.ft.nxt()
                    stgs.append((c, stg))
                    p.dma(stg[:, :64],
                          B.din(wname)[c * P:(c + 1) * P,
                                       h * 64:(h + 1) * 64],
                          q="sp" if c % 2 else "pool")
                for c, stg in stgs:
                    op_copy(p, wt[:, c * 64:(c + 1) * 64], stg[:, :64],
                            eng="dve" if c % 2 else "act")
                p.barrier()
        for (wt, dst_ofs) in ((wq_t, T), (wk_t, 0)):
            for tq in range(8):
                wb = WRK if tq % 2 == 0 else Gd
                p.mm_group([(wb[0:64, :512], wt[:, c * 64:(c + 1) * 64],
                             xt_chunk(B.XT, c, tq * 512, 512),
                             c == 0, c == 7, None) for c in range(8)],
                           reads=[wt[:], B.XT[:]], writes=[wb[:]])
                op_copy(p, B.A3[0:64, dst_ofs + tq * 512:
                                dst_ofs + (tq + 1) * 512],
                        wb[0:64, :512], eng="act" if tq % 2 else "dve")
        for tq in range(8):
            wb = WRK if tq % 2 == 0 else Gd
            p.mm_group([(wb[0:64, :512], wv_t[:, c * 64:(c + 1) * 64],
                         xt_chunk(B.XT, c, tq * 512, 512),
                         c == 0, c == 7, None) for c in range(8)],
                       reads=[wv_t[:], B.XT[:]], writes=[wb[:]])
            vtb = B.xb.items[0] if tq % 2 == 0 else B.xb.items[1]
            op_copy(p, vtb[0:64, :512], wb[0:64, :512])
            for q4 in range(4):
                it = tq * 4 + q4
                ptb = B.pt.nxt()
                op_transpose(p, ptb[:, 0:64],
                             vtb[0:64, q4 * 128:(q4 + 1) * 128],
                             B.ident[0:64, 0:64])
                op_copy(p, v2[:, it * 64:(it + 1) * 64],
                        ptb[:, 0:64], eng="act")
        p.barrier()
        for g4 in range(2):
            stgs = []
            for c in range(4 * g4, 4 * g4 + 4):
                stg = B.ft.nxt()
                stgs.append((c, stg))
                p.dma(stg[:, :12],
                      B.din("w_l3_dlT")[c * P:(c + 1) * P,
                                        h * 12:(h + 1) * 12],
                      q="sp" if c % 2 else "pool")
            for c, stg in stgs:
                op_copy(p, wdl[:, c * 12:(c + 1) * 12], stg[:, :12],
                        eng="dve" if c % 2 else "act")
            p.barrier()
        op_memset(p, csav, 0.0)
        p.barrier()
        g1v3 = B.g1[:, 0:512].rearrange("p (i hh) -> p i hh", hh=16)
        for a in range(8):
            gfirst = {8: True, 9: True, 10: True, 11: True}
            glast = {}
            offlist = [(tk, min(10 + int(math.log2(a ^ (tk // 4))), 11))
                       for tk in range(4 * a)] if a > 0 else []
            for L in (10, 11):
                lst = [tk for tk, LL in offlist if LL == L]
                glast[L] = lst[-1] if lst else None
            inlist = {8: [(1, 0), (3, 2)], 9: [(2, 0), (3, 0), (2, 1), (3, 1)]}
            rotS = [Gd, G8, G9]

            def emit_S(tk, sb):
                p.mm_group([(sb[:, :512],
                             B.A3[0:64, tk * 128:(tk + 1) * 128],
                             B.A3[0:64, T + a * 512: T + (a + 1) * 512],
                             True, True, None)],
                           reads=[B.A3[:]], writes=[sb[:]])
            p.mm_group([(WRK[0:12, :512], wdl[:, c * 12:(c + 1) * 12],
                         xt_chunk(B.XT, c, a * 512, 512),
                         c == 0, c == 7, None) for c in range(8)],
                       reads=[wdl[:], B.XT[:]], writes=[WRK[:]])
            for i3 in range(min(3, len(offlist))):
                emit_S(offlist[i3][0], rotS[i3 % 3])
            tA = B.fs.nxt_not([lpc]); tB = B.fs.nxt_not([lpc, tA])
            op_ts(p, tA[0:12, :], WRK[0:12, :512], lpall[:, h:h + 1], ALU.mult)
            op_act(p, tB[0:12, :], tA[0:12, :], AF.Abs)
            op_ts(p, tB[0:12, :], tB[0:12, :], -1.0, ALU.mult)
            op_act(p, tB[0:12, :], tB[0:12, :], AF.Exp)
            op_act(p, tB[0:12, :], tB[0:12, :], AF.Ln, bias=onecol[0:12, 0:1])
            op_act(p, tA[0:12, :], tA[0:12, :], AF.Relu)
            op_tt(p, tA[0:12, :], tA[0:12, :], tB[0:12, :], ALU.add)
            op_copy(p, lam[:, 0:512], tA[0:12, :])
            for j4 in range(4):
                p.op("pe", lambda e, j4=j4: e.transpose(
                    WRK[0:1, j4 * 128:(j4 + 1) * 128],
                    B.g1[:, (4 * a + j4) * 16 + h:(4 * a + j4) * 16 + h + 1],
                    identf[:]), reads=[B.g1[:], identf[:]], writes=[WRK[:]])
            for j4 in range(4):
                op_copy(p, lrow[:, j4 * 128:(j4 + 1) * 128],
                        WRK[0:1, j4 * 128:(j4 + 1) * 128], eng="act")
            op_ts(p, nrf[0:1, 0:1], csav[0:1, 0:1], -1.0, ALU.mult)
            for i in (1, 2, 3):
                op_ts(p, nrf[0:1, i:i + 1], lrow[:, i * 128 - 1:i * 128],
                      -1.0, ALU.mult)
            op_copy(p, csav[0:1, 0:1], lrow[:, 511:512])
            for i in range(4):
                p.mm_group([(WRK[:, i:i + 1], onesrf[:], nrf[0:1, i:i + 1],
                             i == 0, i == 3, None)],
                           reads=[onesrf[:], l3sc[:]], writes=[WRK[:]])
            op_ts(p, bcc, WRK[:, 0:4], -1.0, ALU.mult)
            if a > 0:
                cgall = g1v3[:, 0:4 * a, h]
                op_act(p, bd[:, 0:4 * a], cgall, AF.Exp, scale=-1.0,
                       bias=bcc[:, 0:1])
                op_ts(p, bd[:, 0:4 * a], bd[:, 0:4 * a], 0.125, ALU.mult)
            for idx, (tk, L) in enumerate(offlist):
                G = G10 if L == 10 else G11
                sb = rotS[idx % 3]
                pt_ = B.xb.items[(tk % 2)]
                op_ts(p, pt_[:, :512], sb[:, :512], bd[:, tk:tk + 1],
                      ALU.mult)
                if idx + 3 < len(offlist):
                    emit_S(offlist[idx + 3][0], sb)
                p.mm_group([(G[0:64, :512], v2[:, tk * 64:(tk + 1) * 64],
                             pt_[:, :512], gfirst[L], glast[L] == tk, None)],
                           reads=[v2[:], pt_[:]], writes=[G[:]])
                gfirst[L] = False
            if glast[10] is not None:
                op_copy(p, g10s[0:64, :], G10[0:64, :512], eng="act")
            if glast[11] is not None:
                op_copy(p, g11s[0:64, :], G11[0:64, :512], eng="act")
            rW = Rot([WRK, G10, G11])
            for j in range(4):
                tk = 4 * a + j
                sb = rW.nxt()
                emit_S(tk, sb)
                sdg = B.xb.items[0]
                op_act(p, sdg[:, 0:128], sb[:, j * 128:(j + 1) * 128],
                       AF.Copy, scale=0.125)
                pcs = B.xb.items[1]
                for i in range(j + 1, 4):
                    col = B.col.nxt()
                    op_act(p, col[:], g1v3[:, 4 * a + j, h:h + 1], AF.Exp,
                           scale=-1.0, bias=bcc[:, i:i + 1])
                    op_ts(p, col[:], col[:], 0.125, ALU.mult)
                    op_ts(p, pcs[:, (i - j - 1) * 128:(i - j) * 128],
                          sb[:, i * 128:(i + 1) * 128], col[:], ALU.mult)
                db = rW.nxt()
                p.mm_group([(db[:, :128], onesrf[:],
                             lrow[:, j * 128:(j + 1) * 128], True, True,
                             None)], reads=[onesrf[:], lrow[:]],
                           writes=[db[:]])
                dxf = B.fs.nxt_not([lpc])
                op_ts(p, dxf[:, :128], db[:, :128],
                      g1v3[:, 4 * a + j, h:h + 1], ALU.subtract)
                op_tt(p, dxf[:, :128], dxf[:, :128], trilneg[:], ALU.add)
                dex = B.xb.items[2]
                op_act(p, dex[:, :128], dxf[:, :128], AF.Exp)
                p.mm_group([(db[:, 128:256], ehs[:],
                             lam[0:8, j * 128:(j + 1) * 128], True, True,
                             None)], reads=[ehs[:], lam[:]], writes=[db[:]])
                op_tt(p, dex[:, 128:256], db[:, 128:256], had[:], ALU.mult)
                p.mm_group([(db[:, 256:384], had[:], dex[:, 128:256],
                             True, True, None)],
                           reads=[had[:], dex[:]], writes=[db[:]])
                op_tt(p, sdg[:, 128:256], sdg[:, 0:128], dex[:, :128],
                      ALU.mult)
                p.op("dve", lambda e, sdg=sdg, db=db: e.tensor_tensor(
                    sdg[:, 256:384], sdg[:, 128:256], db[:, 256:384],
                    ALU.mult), reads=[sdg[:], db[:]], writes=[sdg[:]])
                p.mm_group([(Gd[0:64, j * 128:(j + 1) * 128],
                             v2[:, tk * 64:(tk + 1) * 64], sdg[:, 256:384],
                             j == 0, j == 3, None)],
                           reads=[v2[:], sdg[:]], writes=[Gd[:]])
                for i in range(j + 1, 4):
                    L = 8 + int(math.log2(i ^ j))
                    G = G8 if L == 8 else G9
                    last = inlist[L][-1] == (i, j)
                    p.mm_group([(G[0:64, i * 128:(i + 1) * 128],
                                 v2[:, tk * 64:(tk + 1) * 64],
                                 pcs[:, (i - j - 1) * 128:(i - j) * 128],
                                 gfirst[L], last, None)],
                               reads=[v2[:], pcs[:]], writes=[G[:]])
                    gfirst[L] = False
            acc = B.fs.nxt_not([lpc])
            op_copy(p, acc[0:64, :], Gd[0:64, :512], eng="act")
            tmp = B.fs.nxt_not([lpc, acc])
            fbc = B.fs.nxt_not([lpc, acc, tmp])
            if a > 0:
                op_act(p, f1r[:, :512], lrow[:, :512], AF.Exp,
                       bias=nrf[0:1, 0:1])
                cb = rW.nxt()
                p.mm_group([(cb[:, :512], onesr[:], f1r[:, :512],
                             True, True, None)],
                           reads=[onesr[:], f1r[:]], writes=[cb[:]])
                op_copy(p, fbc[:, :512], cb[:, :512], eng="act")
                for (L, Gs) in ((10, g10s), (11, g11s)):
                    if glast[L] is None:
                        continue
                    cb = rW.nxt()
                    p.mm_group([(cb[0:64, :512],
                                 e12[:, (L - 8) * 64:(L - 8 + 1) * 64],
                                 lam[:, 0:512], True, True,
                                 None)], reads=[e12[:], lam[:]],
                               writes=[cb[:]])
                    op_tt(p, tmp[0:64, :], cb[0:64, :512], fbc[0:64, :512],
                          ALU.mult)
                    p.op("dve", lambda e, Gs=Gs, tmp=tmp: e.tensor_tensor(
                        tmp[0:64, :], tmp[0:64, :], Gs[0:64, :], ALU.mult),
                        reads=[tmp[:], Gs[:]], writes=[tmp[:]])
                    op_tt(p, acc[0:64, :], acc[0:64, :], tmp[0:64, :],
                          ALU.add)
            for i4 in range(4):
                op_act(p, f2r[:, i4 * 128:(i4 + 1) * 128],
                       lrow[:, i4 * 128:(i4 + 1) * 128], AF.Exp,
                       bias=nrf[0:1, i4:i4 + 1] if i4 > 0 else nrf[0:1, 0:1])
            cb = rW.nxt()
            p.mm_group([(cb[:, :512], onesr[:], f2r[:, :512], True, True,
                         None)], reads=[onesr[:], f2r[:]], writes=[cb[:]])
            op_copy(p, fbc[:, :512], cb[:, :512], eng="act")
            for (L, G, ranges) in ((8, G8, [(128, 256), (384, 512)]),
                                   (9, G9, [(256, 512)])):
                for (lo, hi) in ranges:
                    cb = rW.nxt()
                    p.mm_group([(cb[0:64, lo:hi],
                                 e12[:, (L - 8) * 64:(L - 8 + 1) * 64],
                                 lam[:, lo:hi],
                                 True, True, None)],
                               reads=[e12[:], lam[:]], writes=[cb[:]])
                    op_tt(p, tmp[0:64, lo:hi], cb[0:64, lo:hi],
                          fbc[0:64, lo:hi], ALU.mult)
                    p.op("dve", lambda e, G=G, tmp=tmp, lo=lo, hi=hi:
                         e.tensor_tensor(tmp[0:64, lo:hi], tmp[0:64, lo:hi],
                                         G[0:64, lo:hi], ALU.mult),
                         reads=[tmp[:], G[:]], writes=[tmp[:]])
                    op_tt(p, acc[0:64, lo:hi], acc[0:64, lo:hi],
                          tmp[0:64, lo:hi], ALU.add)
            ro = (h % 2) * 64
            cch = h // 2
            op_copy(p, B.A2[ro:ro + 64, cch * T + a * 512:
                            cch * T + (a + 1) * 512], acc[0:64, :512],
                    eng="act")
            p.barrier()
    emit_load_wT(p, B, "w_l3_outT", B.A3, 0, D, D)
    gr = B.ft.nxt()
    p.dma(gr[0:1, :], B.din("w_l3_lng"))
    br_ = B.ft.nxt_not([gr])
    p.dma(br_[0:1, :], B.din("w_l3_lnb"))
    grow = p._sbufs.get("l1_grow") or p.sbuf("l1_grow", [1, D], BF16)
    op_copy(p, grow[:], gr[0:1, :])
    p.barrier()
    for j in (0, 512):
        psb = B.pb.nxt()
        p.mm_group([(psb[:, :512], onesr[:], grow[:, j:j + 512],
                     True, True, None)], reads=[onesr[:], grow[:]],
                   writes=[psb[:]])
        op_copy(p, B.gbc[:, j:j + 512], psb[:, :512], eng="act")
    op_copy(p, grow[:], br_[0:1, :])
    p.barrier()
    for j in (0, 512):
        psb = B.pb.nxt()
        p.mm_group([(psb[:, :512], onesr[:], grow[:, j:j + 512],
                     True, True, None)], reads=[onesr[:], grow[:]],
                   writes=[psb[:]])
        op_copy(p, B.bbc[:, j:j + 512], psb[:, :512], eng="act")
    p.barrier()
    for i in range(NT):
        xi = B.ft.nxt()
        p.dma(xi[:], xin[i * P:(i + 1) * P, :])
        if not first:
            xo = B.ft.nxt_not([xi])
            p.dma(xo[:], xout[i * P:(i + 1) * P, :], q="pool")
        else:
            xo = B.ft.nxt_not([xi])
            p.dma(xo[:], xin[i * P:(i + 1) * P, :], q="pool")
        o = B.ft.nxt_not([xi, xo])
        for j in (0, 512):
            ps = B.pb.nxt()
            p.mm_group([(ps[:, :512], xt_chunk(B.A2, c, i * P, P),
                         B.A3[:, c * 1024 + j: c * 1024 + j + 512],
                         c == 0, c == 7, None) for c in range(8)],
                       reads=[B.A2[:], B.A3[:]], writes=[ps[:]])
            op_copy(p, o[:, j:j + 512], ps[:, :512])
        m = B.col.nxt(); ss = B.col.nxt(); rs = B.col.nxt()
        op_red(p, m[:], xi[:], ALU.add)
        op_ts(p, m[:], m[:], 1.0 / D, ALU.mult)
        op_ts(p, xi[:], xi[:], m[:], ALU.subtract)
        sq = B.fs.nxt_not([lpc])
        op_act(p, sq[:, :512], xi[:, :512], AF.Square, accum_out=ss[:])
        ss2 = B.col.nxt()
        op_act(p, sq[:, :512], xi[:, 512:], AF.Square, accum_out=ss2[:])
        op_tt(p, ss[:], ss[:], ss2[:], ALU.add)
        op_ts(p, ss[:], ss[:], 1.0 / D, ALU.mult)
        op_act(p, ss[:], ss[:], AF.Ln, bias=B.epscol[:])
        op_act(p, rs[:], ss[:], AF.Exp, scale=-0.5)
        op_ts(p, xi[:], xi[:], rs[:], ALU.mult)
        op_tt(p, o[:], o[:], xi[:], ALU.add)
        m2 = B.col.nxt(); st = B.col.nxt(); r2 = B.col.nxt()
        op_red(p, m2[:], o[:], ALU.add)
        op_ts(p, m2[:], m2[:], 1.0 / D, ALU.mult)
        op_ts(p, o[:], o[:], m2[:], ALU.subtract)
        op_act(p, sq[:, :512], o[:, :512], AF.Square, accum_out=st[:])
        st2 = B.col.nxt()
        op_act(p, sq[:, :512], o[:, 512:], AF.Square, accum_out=st2[:])
        op_tt(p, st[:], st[:], st2[:], ALU.add)
        op_ts(p, st[:], st[:], 1.0 / D, ALU.mult)
        op_act(p, st[:], st[:], AF.Ln, bias=B.epscol[:])
        op_act(p, r2[:], st[:], AF.Exp, scale=-0.5)
        op_ts(p, o[:], o[:], r2[:], ALU.mult)
        op_tt(p, o[:], o[:], B.gbc[:], ALU.mult)
        op_tt(p, o[:], o[:], B.bbc[:], ALU.add)
        op_tt(p, o[:], o[:], xo[:], ALU.add)
        p.dma(xout[i * P:(i + 1) * P, :], o[:], q="pool")
        p.barrier()
    p.barrier()


# version 12 (final)
# speedup vs baseline: 1.0302x; 1.0168x over previous
import math
from contextlib import ExitStack

import numpy as np
import ml_dtypes

import concourse.bass as bass
import concourse.mybir as mybir
from concourse.bass_utils import run_bass_kernel_spmd

F32 = mybir.dt.float32
BF16 = mybir.dt.bfloat16
I32 = mybir.dt.int32
AF = mybir.ActivationFunctionType
ALU = mybir.AluOpType
AX = mybir.AxisListType

NCORES = 8
P = 128
T = 4096
D = 1024
NT = T // P

bf16 = ml_dtypes.bfloat16



PE_SELF_SKIP = True
ENGINES = ("pe", "act", "dve", "pool", "sp")
_ENG_ATTR = {"pe": "tensor", "act": "scalar", "dve": "vector",
             "pool": "gpsimd", "sp": "sync"}


class Prog:
    def __init__(self, nc):
        self.nc = nc
        self.es = ExitStack()
        self.sem = {}
        for e in ENGINES:
            self.sem[e] = self.es.enter_context(nc.semaphore(f"sem_{e}"))
        self.dsem = {}
        for q in ("sp", "pool", "act", "dve"):
            self.dsem[q + ".ld"] = self.es.enter_context(
                nc.semaphore(f"dsem_{q}_ld"))
            self.dsem[q + ".st"] = self.es.enter_context(
                nc.semaphore(f"dsem_{q}_st"))
        self.cnt = {e: 0 for e in ENGINES}
        self.dcnt = {q: 0 for q in self.dsem}
        self.last_w = {}
        self.readers = {}
        self.phase = 0
        self.ops = []
        self.phase_dma = {q: 0 for q in self.dsem}
        self._sbufs = {}
        self._psums = {}
        self._frees = []

    def sbuf(self, name, shape, dt=F32):
        t = self.es.enter_context(self.nc.sbuf_tensor(name, shape, dt))
        self._sbufs[name] = t
        return t

    def psum(self, name, shape, dt=F32):
        bank_elems = 512 if dt == F32 else 1024
        assert len(shape) == 2 and shape[0] == 128 and shape[1] <= bank_elems, shape
        t = self.es.enter_context(self.nc.psum_tensor(name, [128, bank_elems],
                                                      dt))
        self._psums[name] = t
        return t

    def dram(self, name, shape, dt=F32, kind=None):
        if kind is None:
            h = self.nc.dram_tensor(name, shape, dt)
        else:
            h = self.nc.dram_tensor(name, shape, dt, kind=kind)
        return h.ap()

    @staticmethod
    def _key(ap):
        t = ap.tensor if hasattr(ap, "tensor") else ap
        return t.name

    def _waits_for(self, eng, reads, writes, is_dma=False):
        waits = {}

        def add_wait(tag):
            kind, name, seq, ph = tag
            if ph < self.phase:
                return
            if PE_SELF_SKIP and kind == "eng" and name == "pe" \
                    and eng == "pe":
                return
            if kind == "eng":
                waits[("e", name)] = max(waits.get(("e", name), 0), seq)
            else:
                waits[("d", name)] = max(waits.get(("d", name), 0),
                                         self.dcnt[name])

        for ap in reads:
            k = self._key(ap)
            w = self.last_w.get(k)
            if w is not None:
                add_wait(w)
        for ap in writes:
            k = self._key(ap)
            w = self.last_w.get(k)
            if w is not None:
                add_wait(w)
            for (kind, name), (seq, ph) in self.readers.get(k, {}).items():
                add_wait((kind, name, seq, ph))
        return waits

    def _record(self, eng_tag, reads, writes):
        for ap in reads:
            k = self._key(ap)
            d = self.readers.setdefault(k, {})
            kk = (eng_tag[0], eng_tag[1])
            old = d.get(kk)
            if old is None or old[0] < eng_tag[2]:
                d[kk] = (eng_tag[2], self.phase)
        for ap in writes:
            k = self._key(ap)
            self.last_w[k] = eng_tag + (self.phase,)
            self.readers[k] = {}

    def op(self, eng, fn, reads=(), writes=(), inc_extra=0):
        waits = self._waits_for(eng, reads, writes)
        self.cnt[eng] += 1
        seq = self.cnt[eng]
        self.ops.append((eng, waits, fn, seq))
        self._record(("eng", eng, seq), reads, writes)
        return seq

    def mm_group(self, mms, reads=(), writes=()):
        waits = self._waits_for("pe", reads, writes)
        self.cnt["pe"] += 1
        seq = self.cnt["pe"]
        def fn(e):
            last = None
            for (ps, lhsT, rhs, st, sp, tp) in mms:
                last = e.matmul(ps, lhsT, rhs, start=st, stop=sp,
                                tile_position=tp, skip_group_check=True)
            return last
        self.ops.append(("pe", waits, fn, seq))
        self._record(("eng", "pe", seq), reads, writes)
        return seq

    def dma(self, dst, src, q="sp", no_track=False):
        eng = q
        is_store = "Dram" in type(dst.tensor).__name__ or \
                   getattr(dst.tensor, "base_partition", None) is None
        qs = q + (".st" if is_store else ".ld")
        waits = self._waits_for(eng, [src], [] if no_track else [dst])
        self.dcnt[qs] += 1
        self.phase_dma[qs] += 1
        seq = self.dcnt[qs]
        def fn(e):
            return e.dma_start(dst, src)
        self.ops.append((eng, waits, fn, ("dma", qs, seq)))
        self._record(("dma", qs, seq), [src], [] if no_track else [dst])
        return seq

    def mark_load(self, dst, q):
        qs = q + ".ld"
        self.last_w[self._key(dst)] = ("dma", qs, self.dcnt[qs], self.phase)
        self.readers[self._key(dst)] = {}

    def barrier(self):
        per_eng = {e: [] for e in ENGINES}
        for (eng, waits, fn, seq) in self.ops:
            per_eng[eng].append((waits, fn, seq))
        for qs, n in self.phase_dma.items():
            if n:
                per_eng[qs.split(".")[0]].append(
                    ({("d", qs): self.dcnt[qs]}, None, None))
        with self.nc.Block() as blk:
            for eng in ENGINES:
                lst = per_eng[eng]
                if not lst:
                    continue
                dec = getattr(blk, _ENG_ATTR[eng])
                def make(lst=lst, eng=eng):
                    def body(e):
                        for (waits, fn, seq) in lst:
                            for (kind, name), v in waits.items():
                                if kind == "e":
                                    e.wait_ge(self.sem[name], v)
                                else:
                                    e.wait_ge(self.dsem[name], 16 * v)
                            if fn is None:
                                continue
                            inst = fn(e)
                            if isinstance(seq, tuple):
                                inst.then_inc(self.dsem[seq[1]], 16)
                            else:
                                inst.then_inc(self.sem[eng], 1)
                    return body
                dec(make())
        self.ops = []
        self.phase_dma = {q: 0 for q in self.dsem}
        self.phase += 1

    def close(self):
        assert not self.ops, "un-flushed ops; call barrier()"
        self.es.close()



def op_tt(p, out, a, b, alu, eng="dve"):
    p.op(eng, lambda e: e.tensor_tensor(out, a, b, alu),
         reads=[a, b], writes=[out])


def op_ts(p, out, a, s1, alu, s2=None, alu2=None, eng="dve"):
    rd = [a] + ([s1] if hasattr(s1, "tensor") else []) \
             + ([s2] if hasattr(s2, "tensor") else [])
    if alu2 is None:
        p.op(eng, lambda e: e.tensor_scalar(out, a, s1, None, alu),
             reads=rd, writes=[out])
    else:
        p.op(eng, lambda e: e.tensor_scalar(out, a, s1, s2, alu, alu2),
             reads=rd, writes=[out])


def op_stt(p, out, in0, scal, in1, alu0, alu1, eng="dve"):
    rd = [in0, in1] + ([scal] if hasattr(scal, "tensor") else [])
    p.op(eng, lambda e: e.scalar_tensor_tensor(out, in0, scal, in1, alu0, alu1),
         reads=rd, writes=[out])


def op_copy(p, out, a, eng="dve"):
    if eng == "act":
        p.op(eng, lambda e: e.activation(out, a, AF.Copy),
             reads=[a], writes=[out])
    else:
        p.op(eng, lambda e: e.tensor_copy(out, a), reads=[a], writes=[out])


def op_act(p, out, a, func, bias=None, scale=None, eng="act", accum_out=None):
    kw = {}
    rd = [a]
    if bias is not None:
        kw["bias"] = bias
        if hasattr(bias, "tensor"):
            rd.append(bias)
    if scale is not None:
        kw["scale"] = scale
        if hasattr(scale, "tensor"):
            rd.append(scale)
    wr = [out]
    if accum_out is not None:
        kw["accum_out"] = accum_out
        wr = [out, accum_out]
    p.op(eng, lambda e: e.activation(out, a, func, **kw), reads=rd, writes=wr)


def op_red(p, out, a, alu, absval=False, negate=None, eng="dve"):
    kw = {}
    if absval:
        kw["apply_absolute_value"] = True
    if negate is not None:
        kw["negate"] = negate
    p.op(eng, lambda e: e.tensor_reduce(out, a, AX.X, alu, **kw),
         reads=[a], writes=[out])


def op_scan(p, out, a_coef, b, init, eng="dve"):
    rd = [a_coef, b] + ([init] if hasattr(init, "tensor") else [])
    p.op(eng, lambda e: e.tensor_tensor_scan(out, a_coef, b, init,
                                             ALU.mult, ALU.add),
         reads=rd, writes=[out])


def op_memset(p, ap, v, eng="pool"):
    p.op(eng, lambda e: e.memset(ap, v), reads=[], writes=[ap])


def op_transpose(p, out_psum, in_sbuf, ident, eng="pe"):
    p.op(eng, lambda e: e.transpose(out_psum, in_sbuf, ident),
         reads=[in_sbuf, ident], writes=[out_psum])


def ceil_div(a, b):
    return (a + b - 1) // b



def stage_inputs(inputs):
    f32 = np.float32
    g = inputs
    common = {}

    def _tr(a):
        return np.ascontiguousarray(np.asarray(a, f32).T)
    T_ = _tr

    def asf(a):
        return np.ascontiguousarray(np.asarray(a, f32))

    common["c_ident"] = np.eye(128, dtype=bf16)
    common["c_pihalf"] = np.full((128, 1), np.pi / 2, f32)
    cm = np.zeros((128, 896), bf16)
    for s in range(128):
        cm[s, s + 384:] = 1.0
    common["c_cmask"] = cm
    o2 = np.zeros((128, 65), bf16)
    o2[0:64, 0] = 1.0
    o2[64:128, 64] = 1.0
    common["c_ones2"] = o2
    common["c_onesr"] = np.ones((1, 128), bf16)
    e8 = np.zeros((8, 512), bf16)
    for e in range(8):
        e8[e, e * 64:(e + 1) * 64] = 1.0
    common["c_e8"] = e8
    common["c_identf"] = np.eye(128, dtype=f32)
    common["c_onesrf"] = np.ones((1, 128), f32)
    tri = np.fromfunction(lambda s, t: (s <= t), (128, 128))
    common["c_trilneg"] = np.where(tri, 0.0, -1e30).astype(bf16)
    common["c_triu1f"] = np.fromfunction(
        lambda ta, t: (ta <= t), (128, 128)).astype(f32)
    Hm = np.array([[1.0]])
    while Hm.shape[0] < 128:
        Hm = np.block([[Hm, Hm], [Hm, -Hm]])
    levs = np.zeros(128, np.int64)
    for u in range(1, 128):
        levs[u] = int(np.floor(np.log2(u))) + 1
    E = np.zeros((8, 128))
    for u in range(128):
        E[levs[u], u] = 1.0
    common["c_ehs"] = ((E @ Hm) / 128.0).astype(bf16)
    common["c_had"] = Hm.astype(bf16)
    e12 = np.zeros((12, 4 * 64), bf16)
    for k in range(4):
        e12[8 + k, k * 64:(k + 1) * 64] = 1.0
    common["c_e12"] = e12
    common["w_l3_lpT"] = np.ascontiguousarray(
        np.asarray(g["L3_attn_Lp"], f32).T)
    common["w_l3_alogR"] = asf(g["L3_attn_A_log"]).reshape(1, 16)
    common["w_l3_dtbR"] = asf(g["L3_attn_dt_bias"]).reshape(1, 16)

    common["w_l0_qT"] = _tr(g["L0_attn_Wq"])
    common["w_l0_kT"] = _tr(g["L0_attn_Wk"])
    common["w_l0_vT"] = _tr(g["L0_attn_Wv"])
    common["w_l0_oT"] = _tr(g["L0_attn_Wo"])
    common["w_l0_w1aT"] = _tr(g["L0_ffn_w1_A"])
    common["w_l0_w1b"] = asf(g["L0_ffn_w1_B"])
    common["w_l0_w2aT"] = _tr(g["L0_ffn_w2_A"])
    common["w_l0_w2b"] = asf(g["L0_ffn_w2_B"])
    common["w_l1_inw"] = asf(g["L1_attn_in_w"])
    common["w_l1_outw"] = asf(g["L1_attn_out_w"])
    common["w_l1_inb"] = np.ascontiguousarray(
        np.asarray(g["L1_attn_in_b"], f32).reshape(48, 128).T)
    common["w_l1_outb"] = asf(g["L1_attn_out_b"]).reshape(1, D)
    common["w_l1_gamma"] = asf(g["L1_attn_norm_gamma"]).reshape(1, D)
    common["w_l1_beta"] = asf(g["L1_attn_norm_beta"]).reshape(1, D)
    common["w_l1_convw"] = np.ascontiguousarray(
        np.asarray(g["L1_attn_conv_w"], f32).reshape(2048, 3)
        .reshape(16, 128, 3).transpose(1, 0, 2).reshape(128, 48))
    common["w_l1_convb"] = np.ascontiguousarray(
        np.asarray(g["L1_attn_conv_b"], f32).reshape(16, 128).T)
    def chan_major(a):
        return np.ascontiguousarray(
            np.asarray(a, f32).reshape(16, 128, 64).transpose(1, 0, 2)
            .reshape(128, 16 * 64))
    common["w_l1_alog"] = chan_major(g["L1_attn_A_log"])
    common["w_l1_bp"] = chan_major(g["L1_attn_B_p"])
    common["w_l1_cp"] = chan_major(g["L1_attn_C_p"])
    common["w_l1_dp"] = np.ascontiguousarray(
        np.asarray(g["L1_attn_D_p"], f32).reshape(16, 128).T)
    common["w_l1_theta"] = asf(g["L1_ffn_theta"])
    common["w_l1_phi"] = asf(g["L1_ffn_phi"])
    common["w_l1_thn"] = asf(g["L1_ffn_theta_noise"])
    common["w_l1_phn"] = asf(g["L1_ffn_phi_noise"])
    common["w_l2_qT"] = _tr(g["L2_attn_Wq"])
    common["w_l2_kT"] = _tr(g["L2_attn_Wk"])
    common["w_l2_vT"] = _tr(g["L2_attn_Wv"])
    common["w_l2_oT"] = _tr(g["L2_attn_Wo"])
    common["w_l2_bo"] = asf(g["L2_attn_bo"]).reshape(1, D)
    qsr = np.asarray(g["L2_attn_q_scale"], f32).reshape(16, 64)
    ksr = np.asarray(g["L2_attn_k_scale"], f32).reshape(16, 64)
    qs2 = np.zeros((128, 8), f32)
    ks2 = np.zeros((128, 8), f32)
    for i in range(8):
        qs2[0:64, i] = qsr[2 * i]
        qs2[64:128, i] = qsr[2 * i + 1]
        ks2[0:64, i] = ksr[2 * i]
        ks2[64:128, i] = ksr[2 * i + 1]
    common["w_l2_qs2"] = qs2
    common["w_l2_ks2"] = ks2
    common["w_l2_wrT"] = _tr(g["L2_ffn_W_r"])
    common["w_l2_wkT"] = _tr(g["L2_ffn_W_k"])
    common["w_l2_wvT"] = _tr(g["L2_ffn_W_v"])
    common["w_l2_mur"] = asf(g["L2_ffn_mu_r"]).reshape(8, 128).T
    common["w_l2_muk"] = asf(g["L2_ffn_mu_k"]).reshape(8, 128).T
    common["w_l3_qT"] = _tr(g["L3_attn_Wq"])
    common["w_l3_kT"] = _tr(g["L3_attn_Wk"])
    common["w_l3_vT"] = _tr(g["L3_attn_Wv"])
    common["w_l3_dlT"] = _tr(g["L3_attn_Wdl"])
    common["w_l3_dtT"] = _tr(g["L3_attn_Wdt"])
    common["w_l3_outT"] = _tr(g["L3_attn_Wout"])
    common["w_l3_alog"] = asf(g["L3_attn_A_log"]).reshape(16, 1)
    common["w_l3_dtb"] = asf(g["L3_attn_dt_bias"]).reshape(16, 1)
    common["w_l3_lp"] = asf(g["L3_attn_Lp"]).reshape(192, 1)
    common["w_l3_lng"] = asf(g["L3_attn_ln_g"]).reshape(1, D)
    common["w_l3_lnb"] = asf(g["L3_attn_ln_b"]).reshape(1, D)
    common["w_l3_gate"] = asf(g["L3_ffn_w_gate"])
    common["w_l3_up"] = asf(g["L3_ffn_w_up"])
    common["w_l3_down"] = asf(g["L3_ffn_w_down"])

    x_full = np.asarray(g["L0_attn_x"], f32)
    ew = np.asarray(g["L0_ffn_expert_weights"], f32)
    ei = np.asarray(g["L0_ffn_expert_indices"])

    in_maps = []
    for c in range(NCORES):
        m = dict(common)
        m["x"] = np.ascontiguousarray(x_full[c])
        coeff = np.zeros((T, 8), f32)
        t0 = c * T
        if t0 < ew.shape[0]:
            sl = slice(t0, min(t0 + T, ew.shape[0]))
            eic = ei[sl]
            ewc = ew[sl]
            for k in range(eic.shape[1]):
                np.add.at(coeff, (np.arange(eic.shape[0]), eic[:, k]),
                          ewc[:, k])
        m["l0_coeffT"] = np.ascontiguousarray(coeff.T)
        in_maps.append(m)
    return in_maps



class Rot:
    def __init__(self, items):
        self.items = items
        self.i = 0

    def nxt(self):
        r = self.items[self.i % len(self.items)]
        self.i += 1
        return r

    def nxt_not(self, excl):
        for _ in range(len(self.items) + 1):
            r = self.nxt()
            if all(r is not e for e in excl):
                return r
        raise RuntimeError("pool exhausted")


class Bufs:
    def __init__(self, p, np_inputs_shapes):
        self.p = p
        self.din_shapes = np_inputs_shapes
        self.dins = {}
        self.used_inputs = set()
        self.XT = p.sbuf("XT", [P, 8 * T], BF16)
        self.A2 = p.sbuf("A2", [P, 8 * T], BF16)
        self.A3 = p.sbuf("A3", [P, 2 * T], BF16)
        self.ft = Rot([p.sbuf(f"ft{i}", [P, 1024]) for i in range(4)])
        self.fs = Rot([p.sbuf(f"fs{i}", [P, 512]) for i in range(4)])
        self.xb = Rot([p.sbuf(f"xb{i}", [P, 1024], BF16) for i in range(3)])
        self.col = Rot([p.sbuf(f"col{i}", [P, 1]) for i in range(10)])
        self.gbc = p.sbuf("gbc", [P, 1024], BF16)
        self.bbc = p.sbuf("bbc", [P, 1024], BF16)
        self.g1 = p.sbuf("g1", [P, 512])
        self.sh1 = p.sbuf("sh1", [12, 512], BF16)
        self.pb = Rot([p.psum(f"pb{i}", [P, 512]) for i in range(6)])
        self.pt = Rot([p.psum(f"pt{i}", [P, 1024], BF16) for i in range(2)])
        self.ident = p.sbuf("ident", [P, P], BF16)
        self.pihalf = p.sbuf("pihalf", [P, 1])
        self.epscol = p.sbuf("epscol", [P, 1])

    def din(self, name):
        if name not in self.dins:
            shape, dt = self.din_shapes[name]
            self.dins[name] = self.p.dram(name, list(shape), dt,
                                          kind="ExternalInput")
            self.used_inputs.add(name)
        return self.dins[name]

    def load_consts(self):
        p = self.p
        p.dma(self.ident[:], self.din("c_ident"))
        p.dma(self.pihalf[:], self.din("c_pihalf"))
        op_memset(p, self.epscol[:], 1e-5)


def op_multi(p, eng, fns, reads=(), writes=()):
    waits = p._waits_for(eng, reads, writes)
    p.cnt[eng] += 1
    seq = p.cnt[eng]

    def fn(e):
        last = None
        for f in fns:
            last = f(e)
        return last
    p.ops.append((eng, waits, fn, seq))
    p._record(("eng", eng, seq), reads, writes)
    return seq


def xt_chunk(XT, c, t0, tn, T_=T):
    return XT[:, c * T_ + t0: c * T_ + t0 + tn]


def emit_ln_xt(p, B, x_dram, XT=None, group=2, extra_tile_fn=None):
    XT = XT if XT is not None else B.XT
    XTv = XT[:].rearrange("p (c t) -> p c t", t=T)
    for i0 in range(0, NT, group):
        tiles = range(i0, min(i0 + group, NT))
        fts = {}
        for i in tiles:
            fts[i] = B.ft.nxt()
            p.dma(fts[i][:], x_dram[i * P:(i + 1) * P, :],
                  q="sp" if i % 2 else "pool")
        for i in tiles:
            xt = fts[i]
            xc = B.ft.nxt()
            m = B.col.nxt(); ss = B.col.nxt(); rs = B.col.nxt()
            op_red(p, m[:], xt[:], ALU.add)
            op_ts(p, m[:], m[:], 1.0 / D, ALU.mult)
            op_ts(p, xc[:], xt[:], m[:], ALU.subtract)
            op_act(p, xt[:], xc[:], AF.Square, accum_out=ss[:])
            op_ts(p, ss[:], ss[:], 1.0 / D, ALU.mult)
            op_act(p, ss[:], ss[:], AF.Ln, bias=B.epscol[:])
            op_act(p, rs[:], ss[:], AF.Exp, scale=-0.5)
            xnb = B.xb.nxt()
            op_ts(p, xnb[:], xc[:], rs[:], ALU.mult)
            if extra_tile_fn is not None:
                extra_tile_fn(i, xc, rs, m)
            ptb = B.pt.nxt()
            op_multi(p, "pe",
                     [(lambda e, c=c, xnb=xnb, ptb=ptb:
                       e.transpose(ptb[:, c * P:(c + 1) * P],
                                   xnb[:, c * P:(c + 1) * P], B.ident[:]))
                      for c in range(8)],
                     reads=[xnb[:], B.ident[:]], writes=[ptb[:]])
            op_copy(p, XTv[:, :, i * P:(i + 1) * P],
                    ptb[:].rearrange("p (c t) -> p c t", t=P))
        p.barrier()


def emit_load_wT(p, B, w_name, dst, dst_ofs, K, N, nstride=None, q="sp",
                 col0=0):
    nstride = nstride or N
    wd = B.din(w_name)
    kc = K // P
    jobs = [(c, n0, min(1024, N - n0))
            for c in range(kc) for n0 in range(0, N, 1024)]
    for g0 in range(0, len(jobs), 4):
        grp = jobs[g0:g0 + 4]
        stgs = []
        for (c, n0, nw) in grp:
            stg = B.ft.nxt()
            stgs.append(stg)
            p.dma(stg[:, :nw], wd[c * P:(c + 1) * P, col0 + n0:col0 + n0 + nw],
                  q=q)
        for (c, n0, nw), stg in zip(grp, stgs):
            op_copy(p, dst[:, dst_ofs + c * nstride + n0:
                           dst_ofs + c * nstride + n0 + nw],
                    stg[:, :nw], eng="dve" if (c + n0 // 1024) % 2 == 0 else "act")
        p.barrier()



def emit_l1ffn(p, B, xin, xout):
    for c in range(8):
        th = B.ft.nxt(); tn = B.ft.nxt()
        p.dma(th[:], B.din("w_l1_theta")[c * P:(c + 1) * P, :])
        p.dma(tn[:], B.din("w_l1_thn")[c * P:(c + 1) * P, :], q="pool")
        ph = B.ft.nxt(); pn = B.ft.nxt()
        p.dma(ph[:], B.din("w_l1_phi")[c * P:(c + 1) * P, :])
        p.dma(pn[:], B.din("w_l1_phn")[c * P:(c + 1) * P, :], q="pool")
        op_tt(p, th[:], th[:], tn[:], ALU.add)
        op_act(p, B.A2[:, c * 1024:(c + 1) * 1024], th[:], AF.Sin,
               bias=B.pihalf[:])
        op_tt(p, ph[:], ph[:], pn[:], ALU.add)
        op_act(p, B.A2[:, 8192 + c * 1024: 8192 + (c + 1) * 1024], ph[:],
               AF.Sin)
        p.barrier()
    for i0 in range(0, NT, 4):
        tiles = range(i0, min(i0 + 4, NT))
        xres = {}
        for i in tiles:
            xres[i] = B.ft.nxt()
            p.dma(xres[i][:], xin[i * P:(i + 1) * P, :])
        for i in tiles:
            for j in (0, 512):
                psA = B.pb.nxt(); psB = B.pb.nxt()
                p.mm_group([(psA[:, :512], xt_chunk(B.XT, c, i * P, P),
                             B.A2[:, c * 1024 + j: c * 1024 + j + 512],
                             c == 0, c == 7, None) for c in range(8)],
                           reads=[B.XT[:], B.A2[:]], writes=[psA[:]])
                p.mm_group([(psB[:, :512], xt_chunk(B.XT, c, i * P, P),
                             B.A2[:, 8192 + c * 1024 + j: 8192 + c * 1024 + j + 512],
                             c == 0, c == 7, None) for c in range(8)],
                           reads=[B.XT[:], B.A2[:]], writes=[psB[:]])
                fA = B.fs.nxt(); fB = B.fs.nxt()
                op_act(p, fA[:], psA[:, :512], AF.Square)
                op_act(p, fB[:], psB[:, :512], AF.Square)
                op_tt(p, fA[:], fA[:], fB[:], ALU.add)
                op_act(p, fA[:], fA[:], AF.Sqrt)
                op_tt(p, xres[i][:, j:j + 512], xres[i][:, j:j + 512],
                      fA[:], ALU.add)
            p.dma(xout[i * P:(i + 1) * P, :], xres[i][:], q="pool")
        p.barrier()



BLOCK_SPEC_DEFAULT = [(0, "fa"), (1, "fa"), (2, "fa"), (3, "fa")]


def emit_block(p, B, k, branches, xin, xout):
    emit_ln_xt(p, B, xin)
    first = True
    if k == 1:
        if "f" in branches:
            emit_l1ffn(p, B, xin, xout)
            first = False
        if "a" in branches:
            emit_l1attn(p, B, xin, xout, first)
            first = False
    elif k == 0:
        if "a" in branches:
            emit_l0attn(p, B, xin, xout, first)
            first = False
        if "f" in branches:
            emit_l0ffn(p, B, xin, xout, first)
            first = False
    elif k == 2:
        if "a" in branches:
            emit_l2attn(p, B, xin, xout, first)
            first = False
        if "f" in branches:
            emit_l2ffn(p, B, xin, xout, first)
            first = False
    elif k == 3:
        if "a" in branches:
            emit_l3attn(p, B, xin, xout, first)
            first = False
        if "f" in branches:
            emit_l3ffn(p, B, xin, xout, first)
            first = False
    assert not first, f"no branches for block {k}"


def build_program(spec, shapes):
    nc = bass.Bass("TRN2", target_bir_lowering=False, debug=False)
    p = Prog(nc)
    B = Bufs(p, shapes)
    x_in = p.dram("x", [T, D], F32, kind="ExternalInput")
    out = p.dram("out", [T, D], F32, kind="ExternalOutput")
    scratch = [p.dram(f"xs{i}", [T, D], F32) for i in range(2)]
    B.load_consts()
    p.barrier()
    cur = x_in
    for idx, (bk, branches) in enumerate(spec):
        dst = out if idx == len(spec) - 1 else scratch[idx % 2]
        emit_block(p, B, bk, branches, cur, dst)
        cur = dst
    p.close()
    nc.finalize()
    return nc, B.used_inputs


def _shapes_of(m):
    out = {}
    for k, v in m.items():
        dt = {np.dtype(np.float32): F32, np.dtype(bf16): BF16,
              np.dtype(np.int32): I32}[v.dtype]
        out[k] = (v.shape, dt)
    return out


KERNEL_SPEC = BLOCK_SPEC_DEFAULT


def kernel(**inputs):
    in_maps = stage_inputs(inputs)
    shapes = _shapes_of(in_maps[0])
    nc, used = build_program(KERNEL_SPEC, shapes)
    used = set(used) | {"x"}
    ims = [{k: m[k] for k in m if k in used} for m in in_maps]
    res = run_bass_kernel_spmd(nc, ims, core_ids=list(range(NCORES)))
    return np.stack([np.asarray(r["out"], np.float32)
                     for r in res.results], axis=0)



def emit_l2ffn(p, B, xin, xout, first):
    QT = T // 4
    emit_load_wT(p, B, "w_l2_wrT", B.A2, 0, D, D)
    emit_load_wT(p, B, "w_l2_wkT", B.A2, 8192, D, D)
    emit_load_wT(p, B, "w_l2_wvT", B.A2, 16384, D, D)
    RQ = 24576
    mur = p.sbuf("l2_mur", [P, 8])
    muk = p.sbuf("l2_muk", [P, 8])
    bcol = p.sbuf("l2_bcol", [P, 8])
    p.dma(mur[:], B.din("w_l2_mur"))
    p.dma(muk[:], B.din("w_l2_muk"))
    p.barrier()

    def build_mix(q, mu, dst_ofs):
        t0 = q * QT
        for c in range(8):
            dst = B.A3[:, dst_ofs + c * QT: dst_ofs + (c + 1) * QT]
            tmp = B.xb.nxt()
            if t0 == 0:
                op_ts(p, dst[:, 0:1], B.XT[:, c * T: c * T + 1],
                      mu[:, c:c + 1], ALU.mult)
                cur = B.XT[:, c * T + 1: c * T + QT]
                prv = B.XT[:, c * T: c * T + QT - 1]
                op_tt(p, tmp[:, :QT - 1], cur, prv, ALU.subtract)
                op_stt(p, dst[:, 1:], tmp[:, :QT - 1], mu[:, c:c + 1], prv,
                       ALU.mult, ALU.add)
            else:
                pcol = bcol[:, c:c + 1]
                op_tt(p, tmp[:, 0:1], B.XT[:, c * T + t0: c * T + t0 + 1],
                      pcol, ALU.subtract)
                op_stt(p, dst[:, 0:1], tmp[:, 0:1], mu[:, c:c + 1], pcol,
                       ALU.mult, ALU.add)
                cur = B.XT[:, c * T + t0 + 1: c * T + t0 + QT]
                prv = B.XT[:, c * T + t0: c * T + t0 + QT - 1]
                op_tt(p, tmp[:, 1:QT], cur, prv, ALU.subtract)
                op_stt(p, dst[:, 1:], tmp[:, 1:QT], mu[:, c:c + 1], prv,
                       ALU.mult, ALU.add)

    for q in range(4):
        t0 = q * QT
        build_mix(q, mur, 0)
        for it in range(QT // P):
            for j in (0, 512):
                ps = B.pb.nxt()
                p.mm_group([(ps[:, :512],
                             B.A3[:, c * QT + it * P: c * QT + (it + 1) * P],
                             B.A2[:, c * 1024 + j: c * 1024 + j + 512],
                             c == 0, c == 7, None) for c in range(8)],
                           reads=[B.A3[:], B.A2[:]], writes=[ps[:]])
                op_act(p, B.A2[:, RQ + it * 1024 + j: RQ + it * 1024 + j + 512],
                       ps[:, :512], AF.Sigmoid)
        for c in range(8):
            op_copy(p, bcol[:, c:c + 1],
                    B.XT[:, c * T + t0 + QT - 1: c * T + t0 + QT], eng="act")
        p.barrier()
        build_mix(q, muk, 0)
        for dc in range(8):
            for jt in (0, 512):
                ps = B.pb.nxt()
                p.mm_group([(ps[:, :512],
                             B.A2[:, 8192 + c * 1024 + dc * P:
                                  8192 + c * 1024 + (dc + 1) * P],
                             B.A3[:, c * QT + jt: c * QT + jt + 512],
                             c == 0, c == 7, None) for c in range(8)],
                           reads=[B.A3[:], B.A2[:]], writes=[ps[:]])
                fr = B.fs.nxt()
                op_act(p, fr[:], ps[:, :512], AF.Relu)
                op_act(p, B.XT[:, dc * T + t0 + jt: dc * T + t0 + jt + 512],
                       fr[:], AF.Square)
        p.barrier()
        for it0 in range(0, QT // P, 4):
          xres_t = {}
          for it in range(it0, it0 + 4):
            gi = t0 // P + it
            xres_t[it] = B.ft.nxt()
            p.dma(xres_t[it][:], (xin if first else xout)[gi * P:(gi + 1) * P, :])
          for it in range(it0, it0 + 4):
            gi = t0 // P + it
            xres = xres_t[it]
            for j in (0, 512):
                ps = B.pb.nxt()
                p.mm_group([(ps[:, :512],
                             B.XT[:, c * T + t0 + it * P:
                                  c * T + t0 + (it + 1) * P],
                             B.A2[:, 16384 + c * 1024 + j:
                                  16384 + c * 1024 + j + 512],
                             c == 0, c == 7, None) for c in range(8)],
                           reads=[B.XT[:], B.A2[:]], writes=[ps[:]])
                fv = B.fs.nxt()
                op_tt(p, fv[:], ps[:, :512],
                      B.A2[:, RQ + it * 1024 + j: RQ + it * 1024 + j + 512],
                      ALU.mult)
                op_tt(p, xres[:, j:j + 512], xres[:, j:j + 512], fv[:],
                      ALU.add)
            p.dma(xout[gi * P:(gi + 1) * P, :], xres[:], q="pool")
          p.barrier()



def _flash_tks(p, B, psS, psO_A, psO_B, tqc, ntk, ncA, ncB, v2, cmask=None):
    def emit_S(tk, pA, pB):
        p.mm_group(
            [(pA[:, :512], B.A3[0:64, tk * 128: tk * 128 + 128],
              B.A3[0:64, T + tqc * 512: T + (tqc + 1) * 512],
              True, True, (0, 0)),
             (pB[:, :512], B.A3[64:128, tk * 128: tk * 128 + 128],
              B.A3[64:128, T + tqc * 512: T + (tqc + 1) * 512],
              True, True, (64, 0))],
            reads=[B.A3[:]], writes=[pA[:], pB[:]])
    banks = []
    for tk in range(min(2, ntk)):
        pA, pB = psS.nxt(), psS.nxt()
        banks.append((pA, pB))
        emit_S(tk, pA, pB)
    for tk in range(ntk):
        pA, pB = banks[tk % 2]
        fsA = B.fs.items[(2 * tk) % 4]
        fsB = B.fs.items[(2 * tk + 1) % 4]
        op_copy(p, fsA[:, :512], pA[:, :512], eng="dve")
        op_copy(p, fsB[:, :512], pB[:, :512], eng="dve")
        ptA = B.xb.nxt()
        op_act(p, ptA[:, :512], fsA[:, :512], AF.Exp, bias=ncA[:],
               scale=0.125)
        op_act(p, ptA[:, 512:1024], fsB[:, :512], AF.Exp, bias=ncB[:],
               scale=0.125)
        if cmask is not None:
            diag_off = tk * 128 - tqc * 512
            if 0 <= diag_off < 512:
                msl = cmask[:, 384 - diag_off: 896 - diag_off]
                op_tt(p, ptA[:, :512], ptA[:, :512], msl, ALU.mult)
                op_tt(p, ptA[:, 512:1024], ptA[:, 512:1024], msl, ALU.mult)
        if tk + 2 < ntk:
            emit_S(tk + 2, pA, pB)
        p.mm_group([(psO_A[0:65, :512], v2[:, tk * 65: tk * 65 + 65],
                     ptA[:, :512], tk == 0, tk == ntk - 1, None)],
                   reads=[v2[:], ptA[:]], writes=[psO_A[:]])
        p.mm_group([(psO_B[0:65, :512],
                     v2[:, 2080 + tk * 65: 2080 + tk * 65 + 65],
                     ptA[:, 512:1024], tk == 0, tk == ntk - 1, None)],
                   reads=[v2[:], ptA[:]], writes=[psO_B[:]])


def emit_l0attn(p, B, xin, xout, first, wq="w_l0_qT", wk="w_l0_kT",
                wv="w_l0_vT", wo="w_l0_oT"):
    v2 = p.sbuf("l0_v2", [P, 2 * 32 * 65], BF16)
    cmask = p.sbuf("l0_cmask", [P, 896], BF16)
    ones2 = p.sbuf("l0_ones2", [P, 65], BF16)
    onesr = p.sbuf("l0_onesr", [1, P], BF16)
    mxq = p.sbuf("l0_mxq", [P, 1])
    mxk = p.sbuf("l0_mxk", [P, 1])
    ncol = p.sbuf("l0_ncol", [P, 1])
    nb16 = p.sbuf("l0_nb16", [1, 2], BF16)
    lrow = p.sbuf("l0_lrow", [1, 512])
    lb16 = p.sbuf("l0_lb16", [1, 512], BF16)
    lbc = p.sbuf("l0_lbc", [P, 512], BF16)
    p.dma(cmask[:], B.din("c_cmask"))
    p.dma(ones2[:], B.din("c_ones2"))
    p.dma(onesr[:], B.din("c_onesr"))
    p.barrier()

    psO_A, psO_B = B.pb.items[0], B.pb.items[1]
    psS = Rot(B.pb.items[2:6])

    for pr in range(8):
        hA, hB = 2 * pr, 2 * pr + 1
        wq_t = B.xb.items[0]; wk_t = B.xb.items[1]; wv_t = B.xb.items[2]
        for wname, wt in ((wq, wq_t), (wk, wk_t), (wv, wv_t)):
            for g in range(2):
                stgs = []
                for c in range(4 * g, 4 * g + 4):
                    stg = B.fs.nxt()
                    stgs.append((c, stg))
                    p.dma(stg[:, :128],
                          B.din(wname)[c * P:(c + 1) * P,
                                       pr * 128:(pr + 1) * 128],
                          q="sp" if c % 2 else "pool")
                for c, stg in stgs:
                    op_copy(p, wt[:, c * 128:(c + 1) * 128], stg[:, :128],
                            eng="dve" if c % 2 else "act")
                p.barrier()
        for (wt, dst_ofs) in ((wq_t, T), (wk_t, 0)):
            for tq in range(8):
                ps = psS.nxt()
                p.mm_group([(ps[:, :512], wt[:, c * 128:(c + 1) * 128],
                             xt_chunk(B.XT, c, tq * 512, 512),
                             c == 0, c == 7, None) for c in range(8)],
                           reads=[wt[:], B.XT[:]], writes=[ps[:]])
                op_copy(p, B.A3[:, dst_ofs + tq * 512: dst_ofs + (tq + 1) * 512],
                        ps[:, :512], eng="act")
        op_memset(p, v2[:], 1.0)
        for tq in range(8):
            ps = psS.nxt()
            p.mm_group([(ps[:, :512], wv_t[:, c * 128:(c + 1) * 128],
                         xt_chunk(B.XT, c, tq * 512, 512),
                         c == 0, c == 7, None) for c in range(8)],
                       reads=[wv_t[:], B.XT[:]], writes=[ps[:]])
            vtb = B.xb.items[0] if tq % 2 == 0 else B.xb.items[1]
            op_copy(p, vtb[:, :512], ps[:, :512])
            for q in range(4):
                it = tq * 4 + q
                ptb = B.pt.nxt()
                op_transpose(p, ptb[:, :P], vtb[:, q * 128:(q + 1) * 128],
                             B.ident[:])
                op_copy(p, v2[:, it * 65: it * 65 + 64],
                        ptb[:, 0:64], eng="act")
                op_copy(p, v2[:, 2080 + it * 65: 2080 + it * 65 + 64],
                        ptb[:, 64:128], eng="act")
        p.barrier()
        for (src_ofs, mx) in ((T, mxq), (0, mxk)):
            op_memset(p, mx[:], 0.0, eng="pool")
            for g in range(4):
                sq = B.xb.items[2]
                op_act(p, sq[:], B.A3[:, src_ofs + g * 1024:
                                      src_ofs + (g + 1) * 1024], AF.Square)
                for j in (0, 512):
                    psn = psS.nxt()
                    p.mm_group([(psn[0:65, :512], ones2[:], sq[:, j:j + 512],
                                 True, True, None)],
                               reads=[ones2[:], sq[:]], writes=[psn[:]])
                    r = B.col.nxt()
                    op_red(p, r[0:1, :], psn[0:1, :512], ALU.max)
                    op_red(p, r[64:65, :], psn[64:65, :512], ALU.max)
                    op_tt(p, mx[0:1, :], mx[0:1, :], r[0:1, :], ALU.max)
                    op_tt(p, mx[64:65, :], mx[64:65, :], r[64:65, :], ALU.max)
        op_tt(p, ncol[0:1, :], mxq[0:1, :], mxk[0:1, :], ALU.mult)
        op_tt(p, ncol[64:65, :], mxq[64:65, :], mxk[64:65, :], ALU.mult)
        op_act(p, ncol[0:1, :], ncol[0:1, :], AF.Sqrt)
        op_act(p, ncol[64:65, :], ncol[64:65, :], AF.Sqrt)
        op_ts(p, ncol[0:1, :], ncol[0:1, :], -0.125, ALU.mult)
        op_ts(p, ncol[64:65, :], ncol[64:65, :], -0.125, ALU.mult)
        op_copy(p, nb16[:, 0:1], ncol[0:1, :])
        op_copy(p, nb16[:, 1:2], ncol[64:65, :])
        ncA = B.col.nxt(); ncB = B.col.nxt()
        psn = psS.nxt()
        p.mm_group([(psn[:, 0:1], onesr[:], nb16[:, 0:1], True, True, None),
                    (psn[:, 1:2], onesr[:], nb16[:, 1:2], False, True, None)],
                   reads=[onesr[:], nb16[:]], writes=[psn[:]])
        op_copy(p, ncA[:], psn[:, 0:1])
        op_copy(p, ncB[:], psn[:, 1:2])
        p.barrier()
        for tqc in range(8):
            ntk = 4 * tqc + 4
            _flash_tks(p, B, psS, psO_A, psO_B, tqc, ntk, ncA, ncB, v2,
                       cmask=cmask)
            for (psO, h) in ((psO_A, hA), (psO_B, hB)):
                c = h // 2
                ro = (h % 2) * 64
                op_act(p, lrow[:, :512], psO[64:65, :512], AF.Ln)
                op_act(p, lrow[:, :512], lrow[:, :512], AF.Exp, scale=-1.0)
                op_copy(p, lb16[:, :512], lrow[:, :512])
                psb = psS.nxt()
                p.mm_group([(psb[0:64, :512], onesr[:, 0:64], lb16[:, :512],
                             True, True, None)],
                           reads=[onesr[:], lb16[:]], writes=[psb[:]])
                op_copy(p, lbc[0:64, :512], psb[0:64, :512], eng="act")
                p.op("dve",
                     lambda e, c=c, ro=ro, tqc=tqc, psO=psO:
                     e.tensor_tensor(
                         B.A2[ro:ro + 64,
                              c * T + tqc * 512: c * T + (tqc + 1) * 512],
                         psO[0:64, :512], lbc[0:64, :512], ALU.mult),
                     reads=[psO[:], lbc[:]], writes=[B.A2[:]])
            p.barrier()
    emit_load_wT(p, B, wo, B.A3, 0, D, D)
    for i0 in range(0, NT, 4):
        xres_t = {}
        for i in range(i0, i0 + 4):
            xres_t[i] = B.ft.nxt()
            p.dma(xres_t[i][:], (xin if first else xout)[i * P:(i + 1) * P, :],
                  q="sp" if i % 2 else "pool")
        for i in range(i0, i0 + 4):
            xres = xres_t[i]
            for j in (0, 512):
                ps = B.pb.nxt()
                p.mm_group([(ps[:, :512], xt_chunk(B.A2, c, i * P, P),
                             B.A3[:, c * 1024 + j: c * 1024 + j + 512],
                             c == 0, c == 7, None) for c in range(8)],
                           reads=[B.A2[:], B.A3[:]], writes=[ps[:]])
                op_tt(p, xres[:, j:j + 512], xres[:, j:j + 512], ps[:, :512],
                      ALU.add)
            p.dma(xout[i * P:(i + 1) * P, :], xres[:], q="pool")
        p.barrier()



def emit_l2attn(p, B, xin, xout, first):
    v2 = p._sbufs.get("l0_v2") or p.sbuf("l0_v2", [P, 2 * 32 * 65], BF16)
    ones2 = p._sbufs["l0_ones2"] if "l0_ones2" in p._sbufs else \
        p.sbuf("l0_ones2", [P, 65], BF16)
    onesr = p._sbufs["l0_onesr"] if "l0_onesr" in p._sbufs else \
        p.sbuf("l0_onesr", [1, P], BF16)
    mxq = p._sbufs.get("l0_mxq") or p.sbuf("l0_mxq", [P, 1])
    mxk = p._sbufs.get("l0_mxk") or p.sbuf("l0_mxk", [P, 1])
    ncol = p._sbufs.get("l0_ncol") or p.sbuf("l0_ncol", [P, 1])
    nb16 = p._sbufs.get("l0_nb16") or p.sbuf("l0_nb16", [1, 2], BF16)
    lrow = p._sbufs.get("l0_lrow") or p.sbuf("l0_lrow", [1, 512])
    lb16 = p._sbufs.get("l0_lb16") or p.sbuf("l0_lb16", [1, 512], BF16)
    lbc = p._sbufs.get("l0_lbc") or p.sbuf("l0_lbc", [P, 512], BF16)
    qs2 = p.sbuf("l2_qs2", [P, 8])
    ks2 = p.sbuf("l2_ks2", [P, 8])
    frow = lrow
    fb16 = lb16
    if "l0_ones2" not in p._sbufs or True:
        p.dma(ones2[:], B.din("c_ones2"))
        p.dma(onesr[:], B.din("c_onesr"))
    p.dma(qs2[:], B.din("w_l2_qs2"))
    p.dma(ks2[:], B.din("w_l2_ks2"))
    p.barrier()

    psO_A, psO_B = B.pb.items[0], B.pb.items[1]
    psS = Rot(B.pb.items[2:6])

    for pr in range(8):
        hA, hB = 2 * pr, 2 * pr + 1
        wq_t = B.xb.items[0]; wk_t = B.xb.items[1]; wv_t = B.xb.items[2]
        for wname, wt in (("w_l2_qT", wq_t), ("w_l2_kT", wk_t),
                          ("w_l2_vT", wv_t)):
            for g in range(2):
                stgs = []
                for c in range(4 * g, 4 * g + 4):
                    stg = B.fs.nxt()
                    stgs.append((c, stg))
                    p.dma(stg[:, :128],
                          B.din(wname)[c * P:(c + 1) * P,
                                       pr * 128:(pr + 1) * 128],
                          q="sp" if c % 2 else "pool")
                for c, stg in stgs:
                    op_copy(p, wt[:, c * 128:(c + 1) * 128], stg[:, :128],
                            eng="dve" if c % 2 else "act")
                p.barrier()
        for (wt, dst_ofs, sc) in ((wq_t, T, qs2), (wk_t, 0, ks2)):
            for tq in range(8):
                ps = psS.nxt()
                p.mm_group([(ps[:, :512], wt[:, c * 128:(c + 1) * 128],
                             xt_chunk(B.XT, c, tq * 512, 512),
                             c == 0, c == 7, None) for c in range(8)],
                           reads=[wt[:], B.XT[:]], writes=[ps[:]])
                op_copy(p, B.A3[:, dst_ofs + tq * 512: dst_ofs + (tq + 1) * 512],
                        ps[:, :512], eng="act")
        op_memset(p, v2[:], 1.0)
        for tq in range(8):
            ps = psS.nxt()
            p.mm_group([(ps[:, :512], wv_t[:, c * 128:(c + 1) * 128],
                         xt_chunk(B.XT, c, tq * 512, 512),
                         c == 0, c == 7, None) for c in range(8)],
                       reads=[wv_t[:], B.XT[:]], writes=[ps[:]])
            vtb = B.xb.items[0] if tq % 2 == 0 else B.xb.items[1]
            op_copy(p, vtb[:, :512], ps[:, :512])
            for q in range(4):
                it = tq * 4 + q
                ptb = B.pt.nxt()
                op_transpose(p, ptb[:, :P], vtb[:, q * 128:(q + 1) * 128],
                             B.ident[:])
                op_copy(p, v2[:, it * 65: it * 65 + 64],
                        ptb[:, 0:64], eng="act")
                op_copy(p, v2[:, 2080 + it * 65: 2080 + it * 65 + 64],
                        ptb[:, 64:128], eng="act")
        p.barrier()
        for (dst_ofs, sc) in ((T, qs2), (0, ks2)):
            for tq in range(8):
                sq = B.xb.nxt()
                op_act(p, sq[:, :512],
                       B.A3[:, dst_ofs + tq * 512: dst_ofs + (tq + 1) * 512],
                       AF.Square)
                psn = psS.nxt()
                p.mm_group([(psn[0:65, :512], ones2[:], sq[:, :512],
                             True, True, None)],
                           reads=[ones2[:], sq[:]], writes=[psn[:]])
                for hh, row in ((0, 0), (64, 64)):
                    op_act(p, frow[:, :512], psn[row:row + 1, :512], AF.Ln)
                    op_ts(p, frow[:, :512], frow[:, :512], 0.5, ALU.mult,
                          -27.631021, ALU.max)
                    op_act(p, frow[:, :512], frow[:, :512], AF.Exp,
                           scale=-1.0)
                    op_copy(p, fb16[:, :512], frow[:, :512])
                    psb = psS.nxt()
                    p.mm_group([(psb[:, :512], onesr[:], fb16[:, :512],
                                 True, True, None)],
                               reads=[onesr[:], fb16[:]], writes=[psb[:]])
                    fbc = B.fs.nxt()
                    op_copy(p, fbc[:, :512], psb[:, :512], eng="act")
                    sl = B.A3[hh:hh + 64,
                              dst_ofs + tq * 512: dst_ofs + (tq + 1) * 512]
                    op_stt(p, sl, sl, sc[hh:hh + 64, pr:pr + 1],
                           fbc[hh:hh + 64, :512], ALU.mult, ALU.mult)
        p.barrier()
        for (src_ofs, mx) in ((T, mxq), (0, mxk)):
            op_memset(p, mx[:], 0.0, eng="pool")
            for g in range(4):
                sq = B.xb.items[2]
                op_act(p, sq[:], B.A3[:, src_ofs + g * 1024:
                                      src_ofs + (g + 1) * 1024], AF.Square)
                for j in (0, 512):
                    psn = psS.nxt()
                    p.mm_group([(psn[0:65, :512], ones2[:], sq[:, j:j + 512],
                                 True, True, None)],
                               reads=[ones2[:], sq[:]], writes=[psn[:]])
                    r = B.col.nxt()
                    op_red(p, r[0:1, :], psn[0:1, :512], ALU.max)
                    op_red(p, r[64:65, :], psn[64:65, :512], ALU.max)
                    op_tt(p, mx[0:1, :], mx[0:1, :], r[0:1, :], ALU.max)
                    op_tt(p, mx[64:65, :], mx[64:65, :], r[64:65, :], ALU.max)
        op_tt(p, ncol[0:1, :], mxq[0:1, :], mxk[0:1, :], ALU.mult)
        op_tt(p, ncol[64:65, :], mxq[64:65, :], mxk[64:65, :], ALU.mult)
        op_act(p, ncol[0:1, :], ncol[0:1, :], AF.Sqrt)
        op_act(p, ncol[64:65, :], ncol[64:65, :], AF.Sqrt)
        op_ts(p, ncol[0:1, :], ncol[0:1, :], -0.125, ALU.mult)
        op_ts(p, ncol[64:65, :], ncol[64:65, :], -0.125, ALU.mult)
        op_copy(p, nb16[:, 0:1], ncol[0:1, :])
        op_copy(p, nb16[:, 1:2], ncol[64:65, :])
        ncA = B.col.nxt(); ncB = B.col.nxt()
        psn = psS.nxt()
        p.mm_group([(psn[:, 0:1], onesr[:], nb16[:, 0:1], True, True, None),
                    (psn[:, 1:2], onesr[:], nb16[:, 1:2], False, True, None)],
                   reads=[onesr[:], nb16[:]], writes=[psn[:]])
        op_copy(p, ncA[:], psn[:, 0:1])
        op_copy(p, ncB[:], psn[:, 1:2])
        p.barrier()
        for tqc in range(8):
            ntk = 32
            _flash_tks(p, B, psS, psO_A, psO_B, tqc, ntk, ncA, ncB, v2)
            for (psO, h) in ((psO_A, hA), (psO_B, hB)):
                c = h // 2
                ro = (h % 2) * 64
                op_act(p, lrow[:, :512], psO[64:65, :512], AF.Ln)
                op_act(p, lrow[:, :512], lrow[:, :512], AF.Exp, scale=-1.0)
                op_copy(p, lb16[:, :512], lrow[:, :512])
                psb = psS.nxt()
                p.mm_group([(psb[0:64, :512], onesr[:, 0:64], lb16[:, :512],
                             True, True, None)],
                           reads=[onesr[:], lb16[:]], writes=[psb[:]])
                op_copy(p, lbc[0:64, :512], psb[0:64, :512], eng="act")
                p.op("dve",
                     lambda e, c=c, ro=ro, tqc=tqc, psO=psO:
                     e.tensor_tensor(
                         B.A2[ro:ro + 64,
                              c * T + tqc * 512: c * T + (tqc + 1) * 512],
                         psO[0:64, :512], lbc[0:64, :512], ALU.mult),
                     reads=[psO[:], lbc[:]], writes=[B.A2[:]])
            p.barrier()
    emit_load_wT(p, B, "w_l2_oT", B.A3, 0, D, D)
    bor = p._sbufs.get("l1_grow") or p.sbuf("l1_grow", [1, D], BF16)
    bof = B.ft.nxt()
    p.dma(bof[0:1, :], B.din("w_l2_bo"))
    op_copy(p, bor[:], bof[0:1, :])
    p.barrier()
    for j in (0, 512):
        psb = B.pb.nxt()
        p.mm_group([(psb[:, :512], onesr[:], bor[:, j:j + 512],
                     True, True, None)],
                   reads=[onesr[:], bor[:]], writes=[psb[:]])
        op_copy(p, B.bbc[:, j:j + 512], psb[:, :512], eng="act")
    p.barrier()
    for i0 in range(0, NT, 4):
        xres_t = {}
        for i in range(i0, i0 + 4):
            xres_t[i] = B.ft.nxt()
            p.dma(xres_t[i][:], (xin if first else xout)[i * P:(i + 1) * P, :],
                  q="sp" if i % 2 else "pool")
        for i in range(i0, i0 + 4):
            xres = xres_t[i]
            for j in (0, 512):
                ps = B.pb.nxt()
                p.mm_group([(ps[:, :512], xt_chunk(B.A2, c, i * P, P),
                             B.A3[:, c * 1024 + j: c * 1024 + j + 512],
                             c == 0, c == 7, None) for c in range(8)],
                           reads=[B.A2[:], B.A3[:]], writes=[ps[:]])
                op_tt(p, xres[:, j:j + 512], xres[:, j:j + 512], ps[:, :512],
                      ALU.add)
                op_tt(p, xres[:, j:j + 512], xres[:, j:j + 512],
                      B.bbc[:, j:j + 512], ALU.add)
            p.dma(xout[i * P:(i + 1) * P, :], xres[:], q="pool")
        p.barrier()



L1_TAPS = 20
L1_HALO = L1_TAPS + 2
QTL = 1024


def emit_l1attn(p, B, xin, xout, first):
    QT = 1024
    cw = p.sbuf("l1_cw", [P, 48])
    cb = p.sbuf("l1_cb", [P, 16])
    inb = p.sbuf("l1_inb", [P, 48])
    dp = p.sbuf("l1_dp", [P, 16])
    onesr = p._sbufs.get("l0_onesr") or p.sbuf("l0_onesr", [1, P], BF16)
    grow = p.sbuf("l1_grow", [1, D], BF16)
    u2 = p.sbuf("l1_u2", [P, QTL + L1_HALO + 2], BF16)
    p.dma(cw[:], B.din("w_l1_convw"))
    p.dma(cb[:], B.din("w_l1_convb"))
    p.dma(inb[:], B.din("w_l1_inb"))
    p.dma(dp[:], B.din("w_l1_dp"))
    p.dma(onesr[:], B.din("c_onesr"))
    gr = B.ft.nxt()
    p.dma(gr[0:1, :], B.din("w_l1_gamma"))
    br = B.ft.nxt()
    p.dma(br[0:1, :], B.din("w_l1_beta"))
    op_copy(p, grow[:], gr[0:1, :])
    p.barrier()
    for j in (0, 512):
        psb = B.pb.nxt()
        p.mm_group([(psb[:, :512], onesr[:], grow[:, j:j + 512],
                     True, True, None)], reads=[onesr[:], grow[:]],
                   writes=[psb[:]])
        op_copy(p, B.gbc[:, j:j + 512], psb[:, :512], eng="act")
    op_copy(p, grow[:], br[0:1, :])
    p.barrier()
    for j in (0, 512):
        psb = B.pb.nxt()
        p.mm_group([(psb[:, :512], onesr[:], grow[:, j:j + 512],
                     True, True, None)], reads=[onesr[:], grow[:]],
                   writes=[psb[:]])
        op_copy(p, B.bbc[:, j:j + 512], psb[:, :512], eng="act")
    p.barrier()
    fA = B.ft.nxt(); fB = B.ft.nxt(); fC = B.ft.nxt(); fE = B.ft.nxt()
    p.dma(fA[:], B.din("w_l1_alog"))
    p.dma(fB[:], B.din("w_l1_bp"))
    p.dma(fC[:], B.din("w_l1_cp"))
    op_act(p, fA[:], fA[:], AF.Exp)
    op_ts(p, fA[:], fA[:], -1.0, ALU.mult)
    op_act(p, fA[:], fA[:], AF.Exp)
    op_tt(p, fB[:], fB[:], fC[:], ALU.mult)
    op_copy(p, fE[:], fB[:])
    g1v = B.g1[:, 0:16 * L1_TAPS].rearrange("p (c d) -> p c d", d=L1_TAPS)
    for d in range(L1_TAPS):
        p.op("dve",
             lambda e, d=d: e.tensor_reduce(
                 g1v[:, :, d:d + 1],
                 fE[:].rearrange("p (c s) -> p c s", s=64), AX.X, ALU.add),
             reads=[fE[:]], writes=[B.g1[:]])
        if d < L1_TAPS - 1:
            op_tt(p, fE[:], fE[:], fA[:], ALU.mult)
    p.op("dve", lambda e: e.tensor_tensor(g1v[:, :, 0], g1v[:, :, 0],
                                          dp[:], ALU.add),
         reads=[B.g1[:], dp[:]], writes=[B.g1[:]])
    p.barrier()
    A2v = B.A2[:].rearrange("p (c t) -> p c t", t=T)
    for i0 in range(0, NT, 1):
        tiles = range(i0, min(i0 + 1, NT))
        fts = {}
        for i in tiles:
            fts[i] = B.ft.nxt()
            p.dma(fts[i][:], xin[i * P:(i + 1) * P, :])
        for i in tiles:
            xt = fts[i]
            xc = B.ft.nxt()
            m = B.col.nxt(); ss = B.col.nxt(); rs = B.col.nxt()
            op_red(p, m[:], xt[:], ALU.add)
            op_ts(p, m[:], m[:], 1.0 / D, ALU.mult)
            op_ts(p, xc[:], xt[:], m[:], ALU.subtract)
            op_act(p, xt[:], xc[:], AF.Square, accum_out=ss[:])
            op_ts(p, ss[:], ss[:], 1.0 / D, ALU.mult)
            op_act(p, ss[:], ss[:], AF.Ln, bias=B.epscol[:])
            op_act(p, rs[:], ss[:], AF.Exp, scale=-0.5)
            op_ts(p, xc[:], xc[:], rs[:], ALU.mult)
            m2 = B.col.nxt(); ss2 = B.col.nxt(); rs2 = B.col.nxt()
            op_red(p, m2[:], xc[:], ALU.add)
            op_ts(p, m2[:], m2[:], 1.0 / D, ALU.mult)
            op_ts(p, xt[:], xc[:], m2[:], ALU.subtract)
            op_act(p, xc[:], xt[:], AF.Square, accum_out=ss2[:])
            op_ts(p, ss2[:], ss2[:], 1.0 / D, ALU.mult)
            op_act(p, ss2[:], ss2[:], AF.Sqrt, bias=B.epscol[:])
            p.op("dve", lambda e, a=rs2, b=ss2: e.reciprocal(a[:], b[:]),
                 reads=[ss2[:]], writes=[rs2[:]])
            op_ts(p, xt[:], xt[:], rs2[:], ALU.mult)
            op_tt(p, xt[:], xt[:], B.gbc[:], ALU.mult)
            xnb = B.xb.nxt()
            op_tt(p, xnb[:], xt[:], B.bbc[:], ALU.add)
            ptb = B.pt.nxt()
            op_multi(p, "pe",
                     [(lambda e, c=c, xnb=xnb, ptb=ptb:
                       e.transpose(ptb[:, c * P:(c + 1) * P],
                                   xnb[:, c * P:(c + 1) * P], B.ident[:]))
                      for c in range(8)],
                     reads=[xnb[:], B.ident[:]], writes=[ptb[:]])
            op_copy(p, A2v[:, :, i * P:(i + 1) * P],
                    ptb[:].rearrange("p (c t) -> p c t", t=P))
        p.barrier()
    emit_load_wT(p, B, "w_l1_outw", B.XT, 0, 2048, 1024)
    ZOF = 16 * 1024
    inw = B.din("w_l1_inw")
    for q in range(4):
        t0 = q * QT
        h0 = max(0, t0 - L1_HALO)
        hw = t0 + QT - h0
        for c in range(16):
            wslc = {}
            for bi, cc in ((0, c), (1, 16 + c), (2, 32 + c)):
                stg = B.ft.nxt()
                wslc[bi] = stg
                qq = "sp" if bi % 2 else "pool"
                for k in range(8):
                    p.dma(stg[:, k * 128:(k + 1) * 128],
                          inw[k * P:(k + 1) * P, cc * 128:(cc + 1) * 128],
                          q=qq, no_track=True)
                p.mark_load(stg[:], qq)
            wu = B.A3[:, 4096:5120]
            wv = B.A3[:, 5120:6144]
            wg = B.A3[:, 6144:7168]
            op_copy(p, wu, wslc[0][:])
            op_copy(p, wv, wslc[1][:], eng="act")
            op_copy(p, wg, wslc[2][:])
            rext = 1 if t0 + QT < T else 0
            uwidth = hw + rext
            nsub = ceil_div(uwidth, 512)
            ups = []
            for s in range(nsub):
                n0 = h0 + s * 512
                nw = min(512, h0 + uwidth - n0)
                ps = B.pb.nxt()
                p.mm_group([(ps[:, :nw], wu[:, k * 128:(k + 1) * 128],
                             xt_chunk(B.A2, k, n0, nw), k == 0, k == 7, None)
                            for k in range(8)],
                           reads=[B.A3[:], B.A2[:]], writes=[ps[:]])
                ups.append((ps, n0, nw))
            for (ps, n0, nw) in ups:
                op_ts(p, B.A3[:, 2048 + (n0 - h0): 2048 + (n0 - h0) + nw],
                      ps[:, :nw], inb[:, c:c + 1], ALU.add)
            ucv = B.A3[:, 2048:2048 + uwidth]
            tmp = u2
            op_ts(p, tmp[:, :hw], ucv[:, 0:hw], cw[:, 16 + c:17 + c],
                  ALU.mult)
            if h0 == 0:
                op_stt(p, tmp[:, 1:hw], ucv[:, 0:hw - 1], cw[:, c:c + 1],
                       tmp[:, 1:hw], ALU.mult, ALU.add)
            else:
                op_stt(p, tmp[:, 1:hw], ucv[:, 0:hw - 1], cw[:, c:c + 1],
                       tmp[:, 1:hw], ALU.mult, ALU.add)
            if rext:
                op_stt(p, tmp[:, 0:hw], ucv[:, 1:hw + 1],
                       cw[:, 32 + c:33 + c], tmp[:, 0:hw], ALU.mult, ALU.add)
            else:
                op_stt(p, tmp[:, 0:hw - 1], ucv[:, 1:hw],
                       cw[:, 32 + c:33 + c], tmp[:, 0:hw - 1],
                       ALU.mult, ALU.add)
            up = B.A3[:, 0:uwidth]
            op_act(p, up[:, :hw], tmp[:, :hw], AF.Silu, bias=cb[:, c:c + 1])
            ofs = t0 - h0
            yg = [B.xb.nxt(), B.xb.nxt()]
            NG = [(0, 10, "dve"), (10, L1_TAPS, "dve")]
            for gi, (d0, d1, eng) in enumerate(NG):
                g = yg[gi]
                for d in range(d0, d1):
                    lo = ofs - d
                    gc = B.g1[:, c * L1_TAPS + d: c * L1_TAPS + d + 1]
                    if lo >= 0:
                        if d == d0:
                            op_ts(p, g[:, :QT], up[:, lo: lo + QT], gc,
                                  ALU.mult, eng=eng)
                        else:
                            op_stt(p, g[:, :QT], up[:, lo: lo + QT], gc,
                                   g[:, :QT], ALU.mult, ALU.add, eng=eng)
                    else:
                        sk = -lo
                        if d == d0:
                            op_memset(p, g[:, 0:sk], 0.0)
                            if sk < QT:
                                op_ts(p, g[:, sk:QT], up[:, 0: QT - sk], gc,
                                      ALU.mult, eng=eng)
                        elif sk < QT:
                            op_stt(p, g[:, sk:QT], up[:, 0: QT - sk], gc,
                                   g[:, sk:QT], ALU.mult, ALU.add, eng=eng)
            for (wsl, act, dst_ofs, bc) in ((wv, AF.Silu, 0, 16 + c),
                                            (wg, AF.Sigmoid, 1024, 32 + c)):
                for s in range(2):
                    n0 = t0 + s * 512
                    ps = B.pb.nxt()
                    p.mm_group([(ps[:, :512], wsl[:, k * 128:(k + 1) * 128],
                                 xt_chunk(B.A2, k, n0, 512),
                                 k == 0, k == 7, None) for k in range(8)],
                               reads=[B.A3[:], B.A2[:]], writes=[ps[:]])
                    op_act(p, B.A3[:, 2048 + dst_ofs + s * 512:
                                   2048 + dst_ofs + s * 512 + 512],
                           ps[:, :512], act, bias=inb[:, bc:bc + 1])
            zf = B.xb.nxt_not(yg)
            for s in range(2):
                zs = zf[:, s * 512:(s + 1) * 512]
                op_tt(p, zs, yg[0][:, s * 512:(s + 1) * 512],
                      yg[1][:, s * 512:(s + 1) * 512], ALU.add)
                op_tt(p, zs, zs,
                      B.A3[:, 2048 + 1024 + s * 512: 2048 + 1024 + s * 512 + 512],
                      ALU.mult)
                op_tt(p, B.XT[:, ZOF + c * QT + s * 512:
                              ZOF + c * QT + s * 512 + 512], zs,
                      B.A3[:, 2048 + s * 512: 2048 + s * 512 + 512], ALU.add)
            p.barrier()
        for it0 in range(0, QT // P, 4):
            xres_t = {}
            for it in range(it0, it0 + 4):
                gi = t0 // P + it
                xres_t[it] = B.ft.nxt()
                p.dma(xres_t[it][:],
                      (xin if first else xout)[gi * P:(gi + 1) * P, :],
                      q="sp" if it % 2 else "pool")
            for it in range(it0, it0 + 4):
                gi = t0 // P + it
                xres = xres_t[it]
                for j in (0, 512):
                    ps = B.pb.nxt()
                    p.mm_group([(ps[:, :512],
                                 B.XT[:, ZOF + cc * QT + it * P:
                                      ZOF + cc * QT + (it + 1) * P],
                                 B.XT[:, cc * 1024 + j: cc * 1024 + j + 512],
                                 cc == 0, cc == 15, None)
                                for cc in range(16)],
                               reads=[B.XT[:]], writes=[ps[:]])
                    op_tt(p, xres[:, j:j + 512], xres[:, j:j + 512],
                          ps[:, :512], ALU.add)
                p.dma(xout[gi * P:(gi + 1) * P, :], xres[:], q="pool")
            p.barrier()
    ob = B.ft.nxt()
    p.dma(ob[0:1, :], B.din("w_l1_outb"))
    op_copy(p, grow[:], ob[0:1, :])
    p.barrier()
    for j in (0, 512):
        psb = B.pb.nxt()
        p.mm_group([(psb[:, :512], onesr[:], grow[:, j:j + 512],
                     True, True, None)], reads=[onesr[:], grow[:]],
                   writes=[psb[:]])
        op_copy(p, B.bbc[:, j:j + 512], psb[:, :512], eng="act")
    p.barrier()
    for i0 in range(0, NT, 2):
        xres_t = {}
        xin_t = {}
        for i in range(i0, i0 + 2):
            xres_t[i] = B.ft.nxt()
            p.dma(xres_t[i][:], xout[i * P:(i + 1) * P, :])
            xin_t[i] = B.ft.nxt()
            p.dma(xin_t[i][:], xin[i * P:(i + 1) * P, :], q="pool")
        for i in range(i0, i0 + 2):
            xres = xres_t[i]
            xt = xin_t[i]
            m = B.col.nxt(); ss = B.col.nxt(); rs = B.col.nxt()
            op_red(p, m[:], xt[:], ALU.add)
            op_ts(p, m[:], m[:], 1.0 / D, ALU.mult)
            op_ts(p, xt[:], xt[:], m[:], ALU.subtract)
            sq = B.fs.nxt()
            op_act(p, sq[:, :512], xt[:, :512], AF.Square, accum_out=ss[:])
            ss2 = B.col.nxt()
            op_act(p, sq[:, :512], xt[:, 512:], AF.Square, accum_out=ss2[:])
            op_tt(p, ss[:], ss[:], ss2[:], ALU.add)
            op_ts(p, ss[:], ss[:], 1.0 / D, ALU.mult)
            op_act(p, ss[:], ss[:], AF.Ln, bias=B.epscol[:])
            op_act(p, rs[:], ss[:], AF.Exp, scale=-0.5)
            op_ts(p, xt[:], xt[:], rs[:], ALU.mult)
            op_tt(p, xres[:], xres[:], xt[:], ALU.add)
            op_tt(p, xres[:], xres[:], B.bbc[:], ALU.add)
            p.dma(xout[i * P:(i + 1) * P, :], xres[:], q="pool")
        p.barrier()



def emit_quant(p, B, out_bf, src, W1, W2, W3, scl, rsc, F):
    G = F // 16
    s3 = src.rearrange("p (g k) -> p g k", k=16)
    p.op("dve", lambda e: e.tensor_reduce(
        scl.rearrange("p (g o) -> p g o", o=1), s3, AX.X, ALU.max,
        apply_absolute_value=True), reads=[src], writes=[scl])
    op_ts(p, scl, scl, 1.0 / 6.0, ALU.mult, 1e-12, ALU.max)
    p.op("dve", lambda e: e.reciprocal(rsc, scl), reads=[scl], writes=[rsc])
    op_act(p, W1, src, AF.Abs)
    rb = rsc.rearrange("p (g o) -> p g o", o=1)[:, :G].broadcast_to((P, G, 16))
    w13 = W1.rearrange("p (g k) -> p g k", k=16)
    p.op("dve", lambda e: e.tensor_tensor(w13, w13, rb, ALU.mult),
         reads=[W1, rsc], writes=[W1])
    op_ts(p, W2, W1, 0.25, ALU.is_gt)
    for m in (0.75, 1.25, 1.75):
        op_stt(p, W2, W1, m, W2, ALU.is_gt, ALU.add)
    op_ts(p, W3, W1, 2.5, ALU.is_gt)
    for m in (3.5, 5.0, 5.0):
        op_stt(p, W3, W1, m, W3, ALU.is_gt, ALU.add)
    op_stt(p, W2, W2, 0.5, W3, ALU.mult, ALU.add)
    op_act(p, W1, src, AF.Sign)
    op_tt(p, W2, W2, W1, ALU.mult)
    sb = scl.rearrange("p (g o) -> p g o", o=1)[:, :G].broadcast_to((P, G, 16))
    o3 = out_bf.rearrange("p (g k) -> p g k", k=16)
    w23 = W2.rearrange("p (g k) -> p g k", k=16)
    p.op("dve", lambda e: e.tensor_tensor(o3, w23, sb, ALU.mult),
         reads=[W2, scl], writes=[out_bf])


L3F_PARTS = [(0, 768), (768, 768), (1536, 768), (2304, 512)]


def emit_l3ffn(p, B, xin, xout, first):
    qsc = B.g1[:, 0:64]
    qrs = B.g1[:, 64:128]
    v2f = p._sbufs.get("l0_v2") or p.sbuf("l0_v2", [P, 2 * 32 * 65], BF16)
    qa = v2f[:, 0:1024]
    qb = v2f[:, 1024:2048]
    hqt = v2f[:, 2048:2816]
    XTv = B.XT[:].rearrange("p (c t) -> p c t", t=T)
    for i0 in range(0, NT, 2):
        tiles = range(i0, min(i0 + 2, NT))
        fts = {}
        for i in tiles:
            fts[i] = B.ft.nxt()
            p.dma(fts[i][:], xin[i * P:(i + 1) * P, :])
        for i in tiles:
            xt = fts[i]
            xc = B.ft.nxt_not(list(fts.values()))
            m = B.col.nxt(); ss = B.col.nxt(); rs = B.col.nxt()
            op_red(p, m[:], xt[:], ALU.add)
            op_ts(p, m[:], m[:], 1.0 / D, ALU.mult)
            op_ts(p, xc[:], xt[:], m[:], ALU.subtract)
            op_act(p, xt[:], xc[:], AF.Square, accum_out=ss[:])
            op_ts(p, ss[:], ss[:], 1.0 / D, ALU.mult)
            op_act(p, ss[:], ss[:], AF.Ln, bias=B.epscol[:])
            op_act(p, rs[:], ss[:], AF.Exp, scale=-0.5)
            op_ts(p, xc[:], xc[:], rs[:], ALU.mult)
            xnb = B.xb.nxt()
            w2 = B.ft.nxt_not([xt, xc] + list(fts.values()))
            emit_quant(p, B, xnb[:], xc[:], xt[:], w2[:], qa, qsc, qrs, 1024)
            ptb = B.pt.nxt()
            op_multi(p, "pe",
                     [(lambda e, c=c, xnb=xnb, ptb=ptb:
                       e.transpose(ptb[:, c * P:(c + 1) * P],
                                   xnb[:, c * P:(c + 1) * P], B.ident[:]))
                      for c in range(8)],
                     reads=[xnb[:], B.ident[:]], writes=[ptb[:]])
            op_copy(p, XTv[:, :, i * P:(i + 1) * P],
                    ptb[:].rearrange("p (c t) -> p c t", t=P))
        p.barrier()
    for (i0p, ipw) in L3F_PARTS:
        nrc = ipw // 128
        for (wname, aofs) in (("w_l3_gate", 0), ("w_l3_up", 8 * ipw)):
            stgs = {0: B.ft.nxt()}
            p.dma(stgs[0][:], B.din(wname)[i0p: i0p + P, :], q="pool")
            for r in range(nrc):
                stg = stgs[r]
                excl = [stg]
                if r + 1 < nrc:
                    stgs[r + 1] = B.ft.nxt_not([stg])
                    excl.append(stgs[r + 1])
                    p.dma(stgs[r + 1][:],
                          B.din(wname)[i0p + (r + 1) * P: i0p + (r + 2) * P,
                                       :],
                          q="sp" if (r + 1) % 2 else "pool")
                w2 = B.ft.nxt_not(excl)
                w3 = B.ft.nxt_not(excl + [w2])
                emit_quant(p, B, qa, stg[:], w2[:], w3[:], qb, qsc, qrs, 1024)
                ptb = B.pt.nxt()
                op_multi(p, "pe",
                         [(lambda e, c=c, ptb=ptb:
                           e.transpose(ptb[:, c * P:(c + 1) * P],
                                       qa[:, c * P:(c + 1) * P], B.ident[:]))
                          for c in range(8)],
                         reads=[qa, B.ident[:]], writes=[ptb[:]])
                for c in range(8):
                    op_copy(p, B.A2[:, aofs + c * ipw + r * P:
                                    aofs + c * ipw + (r + 1) * P],
                            ptb[:, c * P:(c + 1) * P],
                            eng="act" if c % 2 else "dve")
                p.barrier()
        WDOF = 16 * ipw
        stgs = {0: B.ft.nxt()}
        p.dma(stgs[0][:, :ipw], B.din("w_l3_down")[0:P, i0p:i0p + ipw],
              q="pool")
        for r in range(8):
            stg = stgs[r]
            excl = [stg]
            if r + 1 < 8:
                stgs[r + 1] = B.ft.nxt_not([stg])
                excl.append(stgs[r + 1])
                p.dma(stgs[r + 1][:, :ipw],
                      B.din("w_l3_down")[(r + 1) * P:(r + 2) * P,
                                         i0p:i0p + ipw],
                      q="sp" if (r + 1) % 2 else "pool")
            w2 = B.ft.nxt_not(excl)
            w3 = B.ft.nxt_not(excl + [w2])
            emit_quant(p, B, qa[:, :ipw], stg[:, :ipw], w2[:, :ipw],
                       w3[:, :ipw], qb[:, :ipw], qsc[:, :ipw // 16], qrs[:, :ipw // 16], ipw)
            ptb = B.pt.nxt()
            op_multi(p, "pe",
                     [(lambda e, c=c, ptb=ptb:
                       e.transpose(ptb[:, c * P:(c + 1) * P],
                                   qa[:, c * P:(c + 1) * P], B.ident[:]))
                      for c in range(nrc)],
                     reads=[qa, B.ident[:]], writes=[ptb[:]])
            for c in range(nrc):
                op_copy(p, B.A2[:, WDOF + c * 1024 + r * P:
                                WDOF + c * 1024 + (r + 1) * P],
                        ptb[:, c * P:(c + 1) * P],
                        eng="act" if c % 2 else "dve")
            p.barrier()
        for i0 in range(0, NT, 1):
            xres_t = {}
            for i in range(i0, i0 + 1):
                xres_t[i] = B.ft.nxt()
                p.dma(xres_t[i][:],
                      (xin if first and i0p == 0 else xout)
                      [i * P:(i + 1) * P, :])
            for i in range(i0, i0 + 1):
                evs = {}
                for (aofs, nm) in ((0, "g"), (8 * ipw, "u")):
                    pss = []
                    for n0 in range(0, ipw, 512):
                        nw = min(512, ipw - n0)
                        ps = B.pb.nxt()
                        p.mm_group(
                            [(ps[:, :nw], xt_chunk(B.XT, c, i * P, P),
                              B.A2[:, aofs + c * ipw + n0:
                                   aofs + c * ipw + n0 + nw],
                              c == 0, c == 7, None) for c in range(8)],
                            reads=[B.XT[:], B.A2[:]], writes=[ps[:]])
                        pss.append((ps, n0, nw))
                    evs[nm] = pss
                fg = B.ft.nxt_not(list(xres_t.values()))
                fu = B.ft.nxt_not(list(xres_t.values()) + [fg])
                for (ps, n0, nw) in evs["g"]:
                    op_act(p, fg[:, n0:n0 + nw], ps[:, :nw], AF.Silu)
                for (ps, n0, nw) in evs["u"]:
                    op_copy(p, fu[:, n0:n0 + nw], ps[:, :nw], eng="act")
                op_tt(p, fg[:, :ipw], fg[:, :ipw], fu[:, :ipw], ALU.mult)
                w3 = B.ft.nxt_not(list(xres_t.values()) + [fg, fu])
                emit_quant(p, B, qa[:, :ipw], fg[:, :ipw], fu[:, :ipw],
                           w3[:, :ipw], qb[:, :ipw], qsc[:, :ipw // 16],
                           qrs[:, :ipw // 16], ipw)
                ptb = B.pt.nxt()
                op_multi(p, "pe",
                         [(lambda e, c=c, ptb=ptb:
                           e.transpose(ptb[:, c * P:(c + 1) * P],
                                       qa[:, c * P:(c + 1) * P], B.ident[:]))
                          for c in range(nrc)],
                         reads=[qa, B.ident[:]], writes=[ptb[:]])
                op_copy(p, hqt[:, :nrc * P],
                        ptb[:, :nrc * P])
                xres = xres_t[i]
                for j in (0, 512):
                    ps = B.pb.nxt()
                    p.mm_group(
                        [(ps[:, :512], hqt[:, c * P:(c + 1) * P],
                          B.A2[:, WDOF + c * 1024 + j:
                               WDOF + c * 1024 + j + 512],
                          c == 0, c == nrc - 1, None) for c in range(nrc)],
                        reads=[hqt, B.A2[:]], writes=[ps[:]])
                    op_tt(p, xres[:, j:j + 512], xres[:, j:j + 512],
                          ps[:, :512], ALU.add)
                p.dma(xout[i * P:(i + 1) * P, :], xres[:], q="pool")
            p.barrier()



def emit_l0ffn(p, B, xin, xout, first):
    v2 = p._sbufs.get("l0_v2") or p.sbuf("l0_v2", [P, 2 * 32 * 65], BF16)
    e8c = p.sbuf("moe_e8", [8, 512], BF16)
    cf8b = B.sh1[0:8, :]
    p.dma(e8c[:], B.din("c_e8"))
    p.barrier()
    emit_load_wT(p, B, "w_l0_w1aT", B.A3, 0, D, 512)
    for erc in range(4):
        for tq in range(8):
            ps = B.pb.nxt()
            p.mm_group([(ps[:, :512],
                         B.A3[:, c * 512 + erc * P: c * 512 + (erc + 1) * P],
                         xt_chunk(B.XT, c, tq * 512, 512),
                         c == 0, c == 7, None) for c in range(8)],
                       reads=[B.A3[:], B.XT[:]], writes=[ps[:]])
            op_copy(p, B.A2[:, erc * T + tq * 512: erc * T + (tq + 1) * 512],
                    ps[:, :512], eng="act")
        p.barrier()
    w1b = B.din("w_l0_w1b")
    w2aT = B.din("w_l0_w2aT")
    w2b = B.din("w_l0_w2b")
    for pr in range(4):
        for piece in range(4):
            stg = B.ft.nxt()
            p.dma(stg[:], w1b[pr * P:(pr + 1) * P,
                              piece * 1024:(piece + 1) * 1024],
                  q="sp" if piece % 2 else "pool")
            op_copy(p, B.A2[:, 16384 + pr * 4096 + piece * 1024:
                            16384 + pr * 4096 + (piece + 1) * 1024],
                    stg[:], eng="dve" if piece % 2 else "act")
            if piece % 2 == 1:
                p.barrier()
    p.barrier()
    w2aT_r = w2aT.rearrange("(f p) e -> p f e", p=P)
    for pr in range(4):
        for fg in range(4):
            stg = B.ft.nxt()
            p.dma(stg[:].rearrange("p (f e) -> p f e", e=128),
                  w2aT_r[:, fg * 8:(fg + 1) * 8, pr * 128:(pr + 1) * 128],
                  q="sp" if fg % 2 else "pool")
            op_copy(p, B.XT[:, pr * 4096 + fg * 1024:
                            pr * 4096 + (fg + 1) * 1024],
                    stg[:], eng="dve" if fg % 2 else "act")
            if fg % 2 == 1:
                p.barrier()
    for ee in range(8):
        stg = B.ft.nxt()
        p.dma(stg[0:64, :], w2b[ee * 64:(ee + 1) * 64, :],
              q="sp" if ee % 2 else "pool")
        op_copy(p, B.XT[0:64, 16384 + ee * 1024: 16384 + (ee + 1) * 1024],
                stg[0:64, :], eng="act" if ee % 2 else "dve")
        if ee % 2 == 1:
            p.barrier()
    p.barrier()
    coeffT = B.din("l0_coeffT")
    for tq in range(8):
        xres_t = {}
        for ts in range(4):
            xres_t[ts] = B.ft.items[ts]
            p.dma(xres_t[ts][:],
                  (xin if first else xout)[(tq * 4 + ts) * P:
                                           (tq * 4 + ts + 1) * P, :])
        cf8f = B.fs.nxt()
        p.dma(cf8f[0:8, :], coeffT[:, tq * 512:(tq + 1) * 512], q="pool")
        op_copy(p, cf8b, cf8f[0:8, :])
        for pr in range(4):
            psH3A = B.pb.items[2]; psH3B = B.pb.items[3]
            psH2 = [(B.pb.items[0], B.pb.items[1]),
                    (B.pb.items[4], B.pb.items[5])]

            def emit_H2(f):
                pA, pB = psH2[f % 2]
                p.mm_group(
                    [(pA[:, :512],
                      B.A2[0:64, 16384 + pr * 4096 + f * 128:
                           16384 + pr * 4096 + f * 128 + 128],
                      B.A2[0:64, pr * T + tq * 512: pr * T + (tq + 1) * 512],
                      True, True, (0, 0)),
                     (pB[:, :512],
                      B.A2[64:128, 16384 + pr * 4096 + f * 128:
                           16384 + pr * 4096 + f * 128 + 128],
                      B.A2[64:128, pr * T + tq * 512: pr * T + (tq + 1) * 512],
                      True, True, (64, 0))],
                    reads=[B.A2[:]], writes=[pA[:], pB[:]])
            emit_H2(0)
            emit_H2(1)
            for f in range(32):
                pA, pB = psH2[f % 2]
                fsA = B.fs.items[(2 * f) % 4]
                fsB = B.fs.items[(2 * f + 1) % 4]
                op_copy(p, fsA[:, :512], pA[:, :512], eng="dve")
                op_copy(p, fsB[:, :512], pB[:, :512], eng="dve")
                gx = B.xb.items[f % 3]
                ga = gx[:, :512]
                gb = gx[:, 512:1024]
                op_act(p, ga, fsA[:, :512], AF.Gelu)
                op_act(p, gb, fsB[:, :512], AF.Gelu)
                if f + 2 < 32:
                    emit_H2(f + 2)
                p.mm_group([(psH3A[0:64, :512],
                             B.XT[:, pr * 4096 + f * 128: pr * 4096 + f * 128 + 64],
                             ga, f == 0, f == 31, None)],
                           reads=[B.XT[:], gx[:]], writes=[psH3A[:]])
                p.mm_group([(psH3B[0:64, :512],
                             B.XT[:, pr * 4096 + f * 128 + 64:
                                  pr * 4096 + f * 128 + 128],
                             gb, f == 0, f == 31, None)],
                           reads=[B.XT[:], gx[:]], writes=[psH3B[:]])
            psBC = B.pb.items[4]
            rotY = Rot([B.pb.items[5], B.pb.items[0], B.pb.items[1]])
            for (ee, psH3) in ((2 * pr, psH3A), (2 * pr + 1, psH3B)):
                vof = 1024 * (ee % 2)
                p.mm_group([(psBC[0:64, :512],
                             e8c[:, ee * 64:(ee + 1) * 64],
                             cf8b, True, True, None)],
                           reads=[e8c[:], B.sh1[:]], writes=[psBC[:]])
                op_copy(p, v2[0:64, vof:vof + 512], psBC[0:64, :512],
                        eng="act")
                p.op("dve",
                     lambda e, psH3=psH3, vof=vof:
                     e.tensor_tensor(v2[0:64, vof + 512:vof + 1024],
                                     psH3[0:64, :512], v2[0:64, vof:vof + 512],
                                     ALU.mult),
                     reads=[psH3[:], v2[:]], writes=[v2[:]])
                for ts in range(4):
                    for dc in (0, 512):
                        psY = rotY.nxt()
                        p.mm_group(
                            [(psY[:, :512],
                              v2[0:64, vof + 512 + ts * 128:
                                 vof + 512 + (ts + 1) * 128],
                              B.XT[0:64, 16384 + ee * 1024 + dc:
                                   16384 + ee * 1024 + dc + 512],
                              True, True, None)],
                            reads=[v2[:], B.XT[:]], writes=[psY[:]])
                        op_tt(p, xres_t[ts][:, dc:dc + 512],
                              xres_t[ts][:, dc:dc + 512], psY[:, :512],
                              ALU.add)
        for ts in range(4):
            p.dma(xout[(tq * 4 + ts) * P:(tq * 4 + ts + 1) * P, :],
                  xres_t[ts][:], q="pool")
        p.barrier()



def emit_l3attn(p, B, xin, xout, first):
    lam = B.sh1
    wdl = p.sbuf("l3_wdl", [P, 96], BF16)
    wdt = p.sbuf("l3_wdt", [P, 128], BF16)
    bdc = p.sbuf("l3_bdc", [P, 33])
    bd = bdc[:, 0:28]
    bcc = bdc[:, 28:32]
    l3sc = p.sbuf("l3_sc", [1, 24])
    carry = l3sc[0:1, 0:16]
    csav = l3sc[0:1, 16:18]
    nrf = l3sc[0:1, 18:22]
    f1r = p._sbufs.get("l0_lb16") or p.sbuf("l0_lb16", [1, 512], BF16)
    f2r = f1r
    g10s = p.sbuf("l3_g10s", [64, 512], BF16)
    g11s = p.sbuf("l3_g11s", [64, 512], BF16)
    onecol = bdc[:, 32:33]
    identf = p.sbuf("c_identf_t", [P, P])
    onesrf = p.sbuf("c_onesrf_t", [1, P])
    trilneg = p.sbuf("c_trilneg_t", [P, P], BF16)
    triu1f = p.sbuf("c_triu1f_t", [P, P])
    ehs = p.sbuf("c_ehs_t", [8, P], BF16)
    had = p.sbuf("c_had_t", [P, P], BF16)
    e12 = p.sbuf("c_e12_t", [12, 256], BF16)
    v2 = p._sbufs.get("l0_v2") or p.sbuf("l0_v2", [P, 2 * 32 * 65], BF16)
    onesr = p._sbufs.get("l0_onesr") or p.sbuf("l0_onesr", [1, P], BF16)
    lrow = p._sbufs.get("l0_lrow") or p.sbuf("l0_lrow", [1, 512])
    for (t, nm) in ((identf, "c_identf"), (onesrf, "c_onesrf"),
                    (trilneg, "c_trilneg"), (triu1f, "c_triu1f"),
                    (ehs, "c_ehs"), (had, "c_had"), (e12, "c_e12")):
        p.dma(t[:], B.din(nm))
    p.dma(onesr[:], B.din("c_onesr"), q="pool")
    op_memset(p, onecol, 1.0)
    p.barrier()
    emit_load_wT(p, B, "w_l3_dtT", wdt, 0, D, 16, nstride=16)
    arow = B.fs.nxt()
    p.dma(arow[0:1, 0:16], B.din("w_l3_alogR"))
    dbr = B.fs.nxt()
    p.dma(dbr[0:1, 0:16], B.din("w_l3_dtbR"))
    op_act(p, arow[0:1, 0:16], arow[0:1, 0:16], AF.Exp)
    op_ts(p, arow[0:1, 0:16], arow[0:1, 0:16], -1.0, ALU.mult)
    abc = B.fs.nxt(); dbc = B.fs.nxt()
    ps = B.pb.items[5]
    p.mm_group([(ps[:, 0:16], onesrf[:], arow[0:1, 0:16], True, True, None)],
               reads=[onesrf[:], arow[:]], writes=[ps[:]])
    op_copy(p, abc[:, 0:16], ps[:, 0:16], eng="act")
    p.mm_group([(ps[:, 0:16], onesrf[:], dbr[0:1, 0:16], True, True, None)],
               reads=[onesrf[:], dbr[:]], writes=[ps[:]])
    op_copy(p, dbc[:, 0:16], ps[:, 0:16], eng="act")
    op_memset(p, carry, 0.0)
    p.barrier()
    for i in range(NT):
        ps = B.pb.items[5]
        p.mm_group([(ps[:, 0:16], xt_chunk(B.XT, c, i * P, P),
                     wdt[:, c * 16:(c + 1) * 16], c == 0, c == 7, None)
                    for c in range(8)], reads=[wdt[:], B.XT[:]],
                   writes=[ps[:]])
        t0f = B.fs.nxt_not([abc, dbc])
        t1f = B.fs.nxt_not([abc, dbc, t0f])
        op_tt(p, t0f[:, 0:16], ps[:, 0:16], dbc[:, 0:16], ALU.add)
        op_act(p, t1f[:, 0:16], t0f[:, 0:16], AF.Abs)
        op_ts(p, t1f[:, 0:16], t1f[:, 0:16], -1.0, ALU.mult)
        op_act(p, t1f[:, 0:16], t1f[:, 0:16], AF.Exp)
        op_act(p, t1f[:, 0:16], t1f[:, 0:16], AF.Ln, bias=onecol)
        op_act(p, t0f[:, 0:16], t0f[:, 0:16], AF.Relu)
        op_tt(p, t0f[:, 0:16], t0f[:, 0:16], t1f[:, 0:16], ALU.add)
        op_tt(p, t0f[:, 0:16], t0f[:, 0:16], abc[:, 0:16], ALU.mult)
        pc = B.pb.items[4]
        p.mm_group([(pc[:, 0:16], triu1f[:], t0f[:, 0:16], True, False, None),
                    (pc[:, 0:16], onesrf[:], carry, False, True,
                     None)],
                   reads=[triu1f[:], t0f[:], onesrf[:], l3sc[:]],
                   writes=[pc[:]])
        op_copy(p, B.g1[:, i * 16:(i + 1) * 16], pc[:, 0:16])
        p.mm_group([(ps[0:1, 0:16], onecol[:], t0f[:, 0:16], True, True,
                     None)], reads=[onecol[:], t0f[:]], writes=[ps[:]])
        p.op("dve", lambda e, ps=ps: e.tensor_tensor(
            carry, carry, ps[0:1, 0:16], ALU.add),
            reads=[l3sc[:], ps[:]], writes=[l3sc[:]])
        if i % 4 == 3:
            p.barrier()
    p.barrier()
    G8, G9, G10, G11, Gd, WRK = (B.pb.items[0], B.pb.items[1], B.pb.items[2],
                                 B.pb.items[3], B.pb.items[4], B.pb.items[5])
    lpall = p.sbuf("l3_lpall", [12, 16])
    p.dma(lpall[:], B.din("w_l3_lpT"))
    p.barrier()
    lpc = B.fs.nxt()
    for h in range(16):
        wq_t = B.xb.items[0]; wk_t = B.xb.items[1]; wv_t = B.xb.items[2]
        for (wname, wt) in (("w_l3_qT", wq_t), ("w_l3_kT", wk_t),
                            ("w_l3_vT", wv_t)):
            for g4 in range(2):
                stgs = []
                for c in range(4 * g4, 4 * g4 + 4):
                    stg = B.ft.nxt()
                    stgs.append((c, stg))
                    p.dma(stg[:, :64],
                          B.din(wname)[c * P:(c + 1) * P,
                                       h * 64:(h + 1) * 64],
                          q="sp" if c % 2 else "pool")
                for c, stg in stgs:
                    op_copy(p, wt[:, c * 64:(c + 1) * 64], stg[:, :64],
                            eng="dve" if c % 2 else "act")
                p.barrier()
        for (wt, dst_ofs) in ((wq_t, T), (wk_t, 0)):
            for tq in range(8):
                wb = WRK if tq % 2 == 0 else Gd
                p.mm_group([(wb[0:64, :512], wt[:, c * 64:(c + 1) * 64],
                             xt_chunk(B.XT, c, tq * 512, 512),
                             c == 0, c == 7, None) for c in range(8)],
                           reads=[wt[:], B.XT[:]], writes=[wb[:]])
                op_copy(p, B.A3[0:64, dst_ofs + tq * 512:
                                dst_ofs + (tq + 1) * 512],
                        wb[0:64, :512], eng="act" if tq % 2 else "dve")
        for tq in range(8):
            wb = WRK if tq % 2 == 0 else Gd
            p.mm_group([(wb[0:64, :512], wv_t[:, c * 64:(c + 1) * 64],
                         xt_chunk(B.XT, c, tq * 512, 512),
                         c == 0, c == 7, None) for c in range(8)],
                       reads=[wv_t[:], B.XT[:]], writes=[wb[:]])
            vtb = B.xb.items[0] if tq % 2 == 0 else B.xb.items[1]
            op_copy(p, vtb[0:64, :512], wb[0:64, :512])
            for q4 in range(4):
                it = tq * 4 + q4
                ptb = B.pt.nxt()
                op_transpose(p, ptb[:, 0:64],
                             vtb[0:64, q4 * 128:(q4 + 1) * 128],
                             B.ident[0:64, 0:64])
                op_copy(p, v2[:, it * 64:(it + 1) * 64],
                        ptb[:, 0:64], eng="act")
        p.barrier()
        for g4 in range(2):
            stgs = []
            for c in range(4 * g4, 4 * g4 + 4):
                stg = B.ft.nxt()
                stgs.append((c, stg))
                p.dma(stg[:, :12],
                      B.din("w_l3_dlT")[c * P:(c + 1) * P,
                                        h * 12:(h + 1) * 12],
                      q="sp" if c % 2 else "pool")
            for c, stg in stgs:
                op_copy(p, wdl[:, c * 12:(c + 1) * 12], stg[:, :12],
                        eng="dve" if c % 2 else "act")
            p.barrier()
        op_memset(p, csav, 0.0)
        p.barrier()
        g1v3 = B.g1[:, 0:512].rearrange("p (i hh) -> p i hh", hh=16)
        for a in range(8):
            gfirst = {8: True, 9: True, 10: True, 11: True}
            glast = {}
            offlist = [(tk, min(10 + int(math.log2(a ^ (tk // 4))), 11))
                       for tk in range(4 * a)] if a > 0 else []
            for L in (10, 11):
                lst = [tk for tk, LL in offlist if LL == L]
                glast[L] = lst[-1] if lst else None
            inlist = {8: [(1, 0), (3, 2)], 9: [(2, 0), (3, 0), (2, 1), (3, 1)]}
            rotS = [Gd, G8, G9]

            def emit_S(tk, sb):
                p.mm_group([(sb[:, :512],
                             B.A3[0:64, tk * 128:(tk + 1) * 128],
                             B.A3[0:64, T + a * 512: T + (a + 1) * 512],
                             True, True, None)],
                           reads=[B.A3[:]], writes=[sb[:]])
            p.mm_group([(WRK[0:12, :512], wdl[:, c * 12:(c + 1) * 12],
                         xt_chunk(B.XT, c, a * 512, 512),
                         c == 0, c == 7, None) for c in range(8)],
                       reads=[wdl[:], B.XT[:]], writes=[WRK[:]])
            for i3 in range(min(3, len(offlist))):
                emit_S(offlist[i3][0], rotS[i3 % 3])
            tA = B.fs.nxt_not([lpc]); tB = B.fs.nxt_not([lpc, tA])
            op_ts(p, tA[0:12, :], WRK[0:12, :512], lpall[:, h:h + 1], ALU.mult)
            op_act(p, tB[0:12, :], tA[0:12, :], AF.Abs)
            op_ts(p, tB[0:12, :], tB[0:12, :], -1.0, ALU.mult)
            op_act(p, tB[0:12, :], tB[0:12, :], AF.Exp)
            op_act(p, tB[0:12, :], tB[0:12, :], AF.Ln, bias=onecol[0:12, 0:1])
            op_act(p, tA[0:12, :], tA[0:12, :], AF.Relu)
            op_tt(p, tA[0:12, :], tA[0:12, :], tB[0:12, :], ALU.add)
            op_copy(p, lam[:, 0:512], tA[0:12, :])
            for j4 in range(4):
                p.op("pe", lambda e, j4=j4: e.transpose(
                    WRK[0:1, j4 * 128:(j4 + 1) * 128],
                    B.g1[:, (4 * a + j4) * 16 + h:(4 * a + j4) * 16 + h + 1],
                    identf[:]), reads=[B.g1[:], identf[:]], writes=[WRK[:]])
            for j4 in range(4):
                op_copy(p, lrow[:, j4 * 128:(j4 + 1) * 128],
                        WRK[0:1, j4 * 128:(j4 + 1) * 128], eng="act")
            op_ts(p, nrf[0:1, 0:1], csav[0:1, 0:1], -1.0, ALU.mult)
            for i in (1, 2, 3):
                op_ts(p, nrf[0:1, i:i + 1], lrow[:, i * 128 - 1:i * 128],
                      -1.0, ALU.mult)
            op_copy(p, csav[0:1, 0:1], lrow[:, 511:512])
            for i in range(4):
                p.mm_group([(WRK[:, i:i + 1], onesrf[:], nrf[0:1, i:i + 1],
                             i == 0, i == 3, None)],
                           reads=[onesrf[:], l3sc[:]], writes=[WRK[:]])
            op_ts(p, bcc, WRK[:, 0:4], -1.0, ALU.mult)
            if a > 0:
                cgall = g1v3[:, 0:4 * a, h]
                op_act(p, bd[:, 0:4 * a], cgall, AF.Exp, scale=-1.0,
                       bias=bcc[:, 0:1])
                op_ts(p, bd[:, 0:4 * a], bd[:, 0:4 * a], 0.125, ALU.mult)
            for idx, (tk, L) in enumerate(offlist):
                G = G10 if L == 10 else G11
                sb = rotS[idx % 3]
                pt_ = B.xb.items[(tk % 2)]
                op_ts(p, pt_[:, :512], sb[:, :512], bd[:, tk:tk + 1],
                      ALU.mult)
                if idx + 3 < len(offlist):
                    emit_S(offlist[idx + 3][0], sb)
                p.mm_group([(G[0:64, :512], v2[:, tk * 64:(tk + 1) * 64],
                             pt_[:, :512], gfirst[L], glast[L] == tk, None)],
                           reads=[v2[:], pt_[:]], writes=[G[:]])
                gfirst[L] = False
            if glast[10] is not None:
                op_copy(p, g10s[0:64, :], G10[0:64, :512], eng="act")
            if glast[11] is not None:
                op_copy(p, g11s[0:64, :], G11[0:64, :512], eng="act")
            rW = Rot([WRK, G10, G11])
            for j in range(4):
                tk = 4 * a + j
                sb = rW.nxt()
                emit_S(tk, sb)
                sdg = B.xb.items[0]
                op_act(p, sdg[:, 0:128], sb[:, j * 128:(j + 1) * 128],
                       AF.Copy, scale=0.125)
                pcs = B.xb.items[1]
                for i in range(j + 1, 4):
                    col = B.col.nxt()
                    op_act(p, col[:], g1v3[:, 4 * a + j, h:h + 1], AF.Exp,
                           scale=-1.0, bias=bcc[:, i:i + 1])
                    op_ts(p, col[:], col[:], 0.125, ALU.mult)
                    op_ts(p, pcs[:, (i - j - 1) * 128:(i - j) * 128],
                          sb[:, i * 128:(i + 1) * 128], col[:], ALU.mult)
                db = rW.nxt()
                p.mm_group([(db[:, :128], onesrf[:],
                             lrow[:, j * 128:(j + 1) * 128], True, True,
                             None)], reads=[onesrf[:], lrow[:]],
                           writes=[db[:]])
                dxf = B.fs.nxt_not([lpc])
                op_ts(p, dxf[:, :128], db[:, :128],
                      g1v3[:, 4 * a + j, h:h + 1], ALU.subtract)
                op_tt(p, dxf[:, :128], dxf[:, :128], trilneg[:], ALU.add)
                dex = B.xb.items[2]
                op_act(p, dex[:, :128], dxf[:, :128], AF.Exp)
                p.mm_group([(db[:, 128:256], ehs[:],
                             lam[0:8, j * 128:(j + 1) * 128], True, True,
                             None)], reads=[ehs[:], lam[:]], writes=[db[:]])
                op_tt(p, dex[:, 128:256], db[:, 128:256], had[:], ALU.mult)
                p.mm_group([(db[:, 256:384], had[:], dex[:, 128:256],
                             True, True, None)],
                           reads=[had[:], dex[:]], writes=[db[:]])
                op_tt(p, sdg[:, 128:256], sdg[:, 0:128], dex[:, :128],
                      ALU.mult)
                p.op("dve", lambda e, sdg=sdg, db=db: e.tensor_tensor(
                    sdg[:, 256:384], sdg[:, 128:256], db[:, 256:384],
                    ALU.mult), reads=[sdg[:], db[:]], writes=[sdg[:]])
                p.mm_group([(Gd[0:64, j * 128:(j + 1) * 128],
                             v2[:, tk * 64:(tk + 1) * 64], sdg[:, 256:384],
                             j == 0, j == 3, None)],
                           reads=[v2[:], sdg[:]], writes=[Gd[:]])
                for i in range(j + 1, 4):
                    L = 8 + int(math.log2(i ^ j))
                    G = G8 if L == 8 else G9
                    last = inlist[L][-1] == (i, j)
                    p.mm_group([(G[0:64, i * 128:(i + 1) * 128],
                                 v2[:, tk * 64:(tk + 1) * 64],
                                 pcs[:, (i - j - 1) * 128:(i - j) * 128],
                                 gfirst[L], last, None)],
                               reads=[v2[:], pcs[:]], writes=[G[:]])
                    gfirst[L] = False
            acc = B.fs.nxt_not([lpc])
            op_copy(p, acc[0:64, :], Gd[0:64, :512], eng="act")
            tmp = B.fs.nxt_not([lpc, acc])
            fbc = B.fs.nxt_not([lpc, acc, tmp])
            if a > 0:
                op_act(p, f1r[:, :512], lrow[:, :512], AF.Exp,
                       bias=nrf[0:1, 0:1])
                cb = rW.nxt()
                p.mm_group([(cb[:, :512], onesr[:], f1r[:, :512],
                             True, True, None)],
                           reads=[onesr[:], f1r[:]], writes=[cb[:]])
                op_copy(p, fbc[:, :512], cb[:, :512], eng="act")
                for (L, Gs) in ((10, g10s), (11, g11s)):
                    if glast[L] is None:
                        continue
                    cb = rW.nxt()
                    p.mm_group([(cb[0:64, :512],
                                 e12[:, (L - 8) * 64:(L - 8 + 1) * 64],
                                 lam[:, 0:512], True, True,
                                 None)], reads=[e12[:], lam[:]],
                               writes=[cb[:]])
                    op_tt(p, tmp[0:64, :], cb[0:64, :512], fbc[0:64, :512],
                          ALU.mult)
                    p.op("dve", lambda e, Gs=Gs, tmp=tmp: e.tensor_tensor(
                        tmp[0:64, :], tmp[0:64, :], Gs[0:64, :], ALU.mult),
                        reads=[tmp[:], Gs[:]], writes=[tmp[:]])
                    op_tt(p, acc[0:64, :], acc[0:64, :], tmp[0:64, :],
                          ALU.add)
            for i4 in range(4):
                op_act(p, f2r[:, i4 * 128:(i4 + 1) * 128],
                       lrow[:, i4 * 128:(i4 + 1) * 128], AF.Exp,
                       bias=nrf[0:1, i4:i4 + 1] if i4 > 0 else nrf[0:1, 0:1])
            cb = rW.nxt()
            p.mm_group([(cb[:, :512], onesr[:], f2r[:, :512], True, True,
                         None)], reads=[onesr[:], f2r[:]], writes=[cb[:]])
            op_copy(p, fbc[:, :512], cb[:, :512], eng="act")
            for (L, G, ranges) in ((8, G8, [(128, 256), (384, 512)]),
                                   (9, G9, [(256, 512)])):
                for (lo, hi) in ranges:
                    cb = rW.nxt()
                    p.mm_group([(cb[0:64, lo:hi],
                                 e12[:, (L - 8) * 64:(L - 8 + 1) * 64],
                                 lam[:, lo:hi],
                                 True, True, None)],
                               reads=[e12[:], lam[:]], writes=[cb[:]])
                    op_tt(p, tmp[0:64, lo:hi], cb[0:64, lo:hi],
                          fbc[0:64, lo:hi], ALU.mult)
                    p.op("dve", lambda e, G=G, tmp=tmp, lo=lo, hi=hi:
                         e.tensor_tensor(tmp[0:64, lo:hi], tmp[0:64, lo:hi],
                                         G[0:64, lo:hi], ALU.mult),
                         reads=[tmp[:], G[:]], writes=[tmp[:]])
                    op_tt(p, acc[0:64, lo:hi], acc[0:64, lo:hi],
                          tmp[0:64, lo:hi], ALU.add)
            ro = (h % 2) * 64
            cch = h // 2
            op_copy(p, B.A2[ro:ro + 64, cch * T + a * 512:
                            cch * T + (a + 1) * 512], acc[0:64, :512],
                    eng="act")
            p.barrier()
    emit_load_wT(p, B, "w_l3_outT", B.A3, 0, D, D)
    gr = B.ft.nxt()
    p.dma(gr[0:1, :], B.din("w_l3_lng"))
    br_ = B.ft.nxt_not([gr])
    p.dma(br_[0:1, :], B.din("w_l3_lnb"))
    grow = p._sbufs.get("l1_grow") or p.sbuf("l1_grow", [1, D], BF16)
    op_copy(p, grow[:], gr[0:1, :])
    p.barrier()
    for j in (0, 512):
        psb = B.pb.nxt()
        p.mm_group([(psb[:, :512], onesr[:], grow[:, j:j + 512],
                     True, True, None)], reads=[onesr[:], grow[:]],
                   writes=[psb[:]])
        op_copy(p, B.gbc[:, j:j + 512], psb[:, :512], eng="act")
    op_copy(p, grow[:], br_[0:1, :])
    p.barrier()
    for j in (0, 512):
        psb = B.pb.nxt()
        p.mm_group([(psb[:, :512], onesr[:], grow[:, j:j + 512],
                     True, True, None)], reads=[onesr[:], grow[:]],
                   writes=[psb[:]])
        op_copy(p, B.bbc[:, j:j + 512], psb[:, :512], eng="act")
    p.barrier()
    for i in range(NT):
        xi = B.ft.nxt()
        p.dma(xi[:], xin[i * P:(i + 1) * P, :])
        if not first:
            xo = B.ft.nxt_not([xi])
            p.dma(xo[:], xout[i * P:(i + 1) * P, :], q="pool")
        else:
            xo = B.ft.nxt_not([xi])
            p.dma(xo[:], xin[i * P:(i + 1) * P, :], q="pool")
        o = B.ft.nxt_not([xi, xo])
        for j in (0, 512):
            ps = B.pb.nxt()
            p.mm_group([(ps[:, :512], xt_chunk(B.A2, c, i * P, P),
                         B.A3[:, c * 1024 + j: c * 1024 + j + 512],
                         c == 0, c == 7, None) for c in range(8)],
                       reads=[B.A2[:], B.A3[:]], writes=[ps[:]])
            op_copy(p, o[:, j:j + 512], ps[:, :512])
        m = B.col.nxt(); ss = B.col.nxt(); rs = B.col.nxt()
        op_red(p, m[:], xi[:], ALU.add)
        op_ts(p, m[:], m[:], 1.0 / D, ALU.mult)
        op_ts(p, xi[:], xi[:], m[:], ALU.subtract)
        sq = B.fs.nxt_not([lpc])
        op_act(p, sq[:, :512], xi[:, :512], AF.Square, accum_out=ss[:])
        ss2 = B.col.nxt()
        op_act(p, sq[:, :512], xi[:, 512:], AF.Square, accum_out=ss2[:])
        op_tt(p, ss[:], ss[:], ss2[:], ALU.add)
        op_ts(p, ss[:], ss[:], 1.0 / D, ALU.mult)
        op_act(p, ss[:], ss[:], AF.Ln, bias=B.epscol[:])
        op_act(p, rs[:], ss[:], AF.Exp, scale=-0.5)
        op_ts(p, xi[:], xi[:], rs[:], ALU.mult)
        op_tt(p, o[:], o[:], xi[:], ALU.add)
        m2 = B.col.nxt(); st = B.col.nxt(); r2 = B.col.nxt()
        op_red(p, m2[:], o[:], ALU.add)
        op_ts(p, m2[:], m2[:], 1.0 / D, ALU.mult)
        op_ts(p, o[:], o[:], m2[:], ALU.subtract)
        op_act(p, sq[:, :512], o[:, :512], AF.Square, accum_out=st[:])
        st2 = B.col.nxt()
        op_act(p, sq[:, :512], o[:, 512:], AF.Square, accum_out=st2[:])
        op_tt(p, st[:], st[:], st2[:], ALU.add)
        op_ts(p, st[:], st[:], 1.0 / D, ALU.mult)
        op_act(p, st[:], st[:], AF.Ln, bias=B.epscol[:])
        op_act(p, r2[:], st[:], AF.Exp, scale=-0.5)
        op_ts(p, o[:], o[:], r2[:], ALU.mult)
        op_tt(p, o[:], o[:], B.gbc[:], ALU.mult)
        op_tt(p, o[:], o[:], B.bbc[:], ALU.add)
        op_tt(p, o[:], o[:], xo[:], ALU.add)
        p.dma(xout[i * P:(i + 1) * P, :], o[:], q="pool")
        p.barrier()
    p.barrier()
